# version 05
# baseline (speedup 1.0000x reference)
import os
import numpy as np
import ml_dtypes
import concourse.bass as bass
import concourse.mybir as mybir
from concourse.bass_utils import run_bass_kernel_spmd

F32 = mybir.dt.float32
BF16 = mybir.dt.bfloat16
AF = mybir.ActivationFunctionType
ALU = mybir.AluOpType
AX = mybir.AxisListType

B_TOT, T, DM = 32, 2048, 1024
NCORE = 8
BPC = B_TOT // NCORE
CH = DM // 128
TC = T + 1
NT = T // 128
NL = 6
H, DH = 16, 64
INNER = H * DH
DI = 2730
NJC = 22
JCS = [128] * 21 + [42]
QK_SCALE = 8.0
NEG = -1e30

STAGE = int(os.environ.get("KSTAGE", "99"))
NPASS = int(os.environ.get("KNPASS", str(BPC)))
NLAY = int(os.environ.get("KNLAY", str(NL)))


def host_consts():
    c = {}
    half = np.arange(0, DH, 2, dtype=np.float64)
    inv_freq = 1.0 / 10000.0 ** (half / DH)
    t = np.arange(T, dtype=np.float64)
    freqs = t[:, None] * inv_freq[None, :]
    freqs = np.concatenate([freqs, freqs], -1)
    base_scale = (half + 0.4 * DH) / (1.4 * DH)
    power = (t - T // 2) / (256 // 2)
    scale = base_scale[None, :] ** power[:, None]
    scale = np.concatenate([scale, scale], -1)
    cos, sin = np.cos(freqs), np.sin(freqs)
    sgn = np.concatenate([-np.ones(DH // 2), np.ones(DH // 2)])[None, :]

    def tokmaj(a):
        return np.ascontiguousarray(
            a.reshape(NT, 128, DH).transpose(1, 0, 2).reshape(128, NT * DH)
        ).astype(np.float32)

    c["tb_aq"] = tokmaj(cos * scale * QK_SCALE)
    c["tb_bq"] = tokmaj(sin * scale * sgn * QK_SCALE)
    c["tb_ak"] = tokmaj(cos / scale)
    c["tb_bk"] = tokmaj(sin / scale * sgn)
    i = np.arange(128)[:, None]
    j = np.arange(128)[None, :]
    c["m_up"] = np.where(j >= i, 0.0, NEG).astype(ml_dtypes.bfloat16)
    c["m_lo"] = np.where(j <= i, 0.0, NEG).astype(ml_dtypes.bfloat16)
    c["maskT"] = (j >= i).astype(np.float32)
    c["ident"] = np.eye(128, dtype=ml_dtypes.bfloat16)
    return c


def host_layout(inp):
    d = {}
    d["x_t"] = np.ascontiguousarray(
        np.asarray(inp["L0_attn_x"], np.float32).transpose(0, 2, 1))
    d["w_attn_T"] = np.ascontiguousarray(
        np.asarray(inp["L0_attn_w_attn"], np.float32).T)
    d["w_proj_T"] = np.ascontiguousarray(
        np.asarray(inp["L0_attn_w_proj"], np.float32).T)
    d["wqkv"] = np.ascontiguousarray(np.asarray(inp["L0_ffn_Wqkv"], np.float32))
    d["wo"] = np.ascontiguousarray(np.asarray(inp["L0_ffn_Wo"], np.float32))
    d["w1"] = np.ascontiguousarray(np.asarray(inp["L0_ffn_W1"], np.float32))
    d["w2"] = np.ascontiguousarray(np.asarray(inp["L0_ffn_W2"], np.float32))
    d["qs"] = np.ascontiguousarray(np.broadcast_to(
        np.asarray(inp["L0_ffn_q_scale"], np.float32)[:, None, :], (NL, 128, DH)))
    d["ks"] = np.ascontiguousarray(np.broadcast_to(
        np.asarray(inp["L0_ffn_k_scale"], np.float32)[:, None, :], (NL, 128, DH)))
    g = np.asarray(inp["L0_ffn_ln_g"], np.float32).reshape(NL, CH, 128)
    b = np.asarray(inp["L0_ffn_ln_b"], np.float32).reshape(NL, CH, 128)
    d["gb"] = np.ascontiguousarray(np.stack([g, b], axis=2))
    b1 = np.asarray(inp["L0_ffn_b1"], np.float32)
    b1a = np.zeros((NL, NJC, 128), np.float32)
    b1g = np.zeros((NL, NJC, 128), np.float32)
    for l in range(NL):
        b1a[l].reshape(-1)[:DI] = b1[l, :DI]
        b1g[l].reshape(-1)[:DI] = b1[l, DI:]
    d["b1a"] = np.ascontiguousarray(b1a.transpose(0, 2, 1))
    d["b1g"] = np.ascontiguousarray(b1g.transpose(0, 2, 1))
    d["b2f"] = np.ascontiguousarray(
        np.asarray(inp["L0_ffn_b2"], np.float32).reshape(NL, CH, 128).transpose(0, 2, 1))
    return d


class VP:

    CSEMS = ("pe", "act", "dve", "gpc")
    QUEUES = ("spd", "gpd")
    NLANES = 8
    BODY = {"pe": "pe", "act": "act", "dve": "dve", "gpc": "gp",
            "spd": "sp", "gpd": "gp"}

    @classmethod
    def body_of_sem(cls, sname):
        for base in ("spd", "gpd", "gpc", "pe", "act", "dve"):
            if sname.startswith(base):
                return cls.BODY[base]
        raise KeyError(sname)

    SEM_LIMIT = 20000

    def __init__(self, nc):
        self.nc = nc
        self.sem = {}
        self.gen = {s: 0 for s in self.CSEMS}
        for s in self.CSEMS:
            self.sem[s] = nc.alloc_semaphore(f"vp_{s}")
        for q in self.QUEUES:
            for i in range(self.NLANES):
                self.sem[f"{q}{i}"] = nc.alloc_semaphore(f"vp_{q}{i}")
        self.cnt = {s: 0 for s in self.sem}
        self.rr = {q: 0 for q in self.QUEUES}
        self.ops = {b: [] for b in ("pe", "act", "dve", "gp", "sp")}
        self.last_w = {}
        self.readers = {}
        self.known = {b: {s: 0 for s in self.sem} for b in self.ops}
        self.fences = {}

    def op(self, sem, fn, reads=(), writes=(), tag=""):
        body = self.BODY[sem]
        is_dma = sem in self.QUEUES
        if is_dma:
            queue = sem
            lane = self.rr[queue]
            self.rr[queue] = (lane + 1) % self.NLANES
            sname = f"{queue}{lane}"
            amt = 16
        else:
            queue = None
            base = sem
            sname = base if self.gen[base] == 0 else f"{base}g{self.gen[base]}"
            if self.cnt.get(sname, 0) + 1 > self.SEM_LIMIT:
                self.gen[base] += 1
                sname = f"{base}g{self.gen[base]}"
                self.sem[sname] = self.nc.alloc_semaphore(f"vp_{sname}")
                self.cnt[sname] = 0
                for b in self.known:
                    self.known[b][sname] = 0
            amt = 1
        deps = {}

        def add(d):
            if d is None:
                return
            s, v = d
            if deps.get(s, 0) < v:
                deps[s] = v

        for k in reads:
            add(self.last_w.get(k))
            if k.startswith("ps"):
                for r in self.readers.get(k, ()):
                    if self.body_of_sem(r[0]) != body:
                        add(r)
        for k in writes:
            add(self.last_w.get(k))
            for r in self.readers.get(k, ()):
                add(r)
            for pfx, fd in self.fences.items():
                if k.startswith(pfx):
                    for s, v in fd.items():
                        add((s, v))
        if is_dma and self.cnt[sname] > 0:
            deps[sname] = max(deps.get(sname, 0), self.cnt[sname])
        waits = []
        for s, v in sorted(deps.items()):
            if sem == "pe" and s.startswith("pe"):
                continue
            if self.known[body][s] >= v:
                continue
            waits.append((s, v))
            self.known[body][s] = v
        self.cnt[sname] += amt
        me = (sname, self.cnt[sname])
        self.ops[body].append((waits, fn, sname, amt, tag))
        for k in reads:
            self.readers.setdefault(k, []).append(me)
        for k in writes:
            self.last_w[k] = me
            self.readers[k] = []
        return me

    def fence(self, from_prefixes, to_prefixes):
        deps = {}
        for k, lw in self.last_w.items():
            if any(k.startswith(p) for p in from_prefixes):
                s, v = lw
                deps[s] = max(deps.get(s, 0), v)
        for k, rd in self.readers.items():
            if any(k.startswith(p) for p in from_prefixes):
                for s, v in rd:
                    deps[s] = max(deps.get(s, 0), v)
        for p in to_prefixes:
            self.fences[p] = dict(deps)

    def emit(self):
        nc = self.nc
        finals = [(s, self.cnt[s]) for s in self.sem
                  if any(s.startswith(q) for q in self.QUEUES) and self.cnt[s]]
        with nc.Block() as block:
            def mk(body):
                def run(h):
                    for waits, fn, sname, amt, tag in self.ops[body]:
                        for s, v in waits:
                            h.wait_ge(self.sem[s], v)
                        inst = fn(h)
                        if inst is not None:
                            inst.then_inc(self.sem[sname], amt)
                    if body == "sp":
                        for s, v in finals:
                            h.wait_ge(self.sem[s], v)
                return run
            block.tensor(mk("pe"))
            block.scalar(mk("act"))
            block.vector(mk("dve"))
            block.gpsimd(mk("gp"))
            block.sync(mk("sp"))


def build(npass, nlay, stage):
    nc = bass.Bass("TRN2", target_bir_lowering=False)
    vp = VP(nc)

    Dx = nc.dram_tensor("x_t", [BPC, DM, T], F32, kind="ExternalInput").ap()
    Dout = nc.dram_tensor("out_t", [BPC, DM, T], F32, kind="ExternalOutput").ap()
    Dspill = nc.dram_tensor("x2_spill", [BPC, DM, T], F32, kind="Internal").ap()
    DwaT = nc.dram_tensor("w_attn_T", [DM, 3 * DM], F32, kind="ExternalInput").ap()
    DwpT = nc.dram_tensor("w_proj_T", [DM, DM], F32, kind="ExternalInput").ap()
    Dwqkv = nc.dram_tensor("wqkv", [NL, DM, 3 * INNER], F32, kind="ExternalInput").ap()
    Dwo = nc.dram_tensor("wo", [NL, INNER, DM], F32, kind="ExternalInput").ap()
    Dw1 = nc.dram_tensor("w1", [NL, DM, 2 * DI], F32, kind="ExternalInput").ap()
    Dw2 = nc.dram_tensor("w2", [NL, DI, DM], F32, kind="ExternalInput").ap()
    Dqs = nc.dram_tensor("qs", [NL, 128, DH], F32, kind="ExternalInput").ap()
    Dks = nc.dram_tensor("ks", [NL, 128, DH], F32, kind="ExternalInput").ap()
    Dgb = nc.dram_tensor("gb", [NL, CH, 2, 128], F32, kind="ExternalInput").ap()
    Db1a = nc.dram_tensor("b1a", [NL, 128, NJC], F32, kind="ExternalInput").ap()
    Db1g = nc.dram_tensor("b1g", [NL, 128, NJC], F32, kind="ExternalInput").ap()
    Db2f = nc.dram_tensor("b2f", [NL, 128, CH], F32, kind="ExternalInput").ap()
    Dtab = {k: nc.dram_tensor(k, [128, NT * DH], F32, kind="ExternalInput").ap()
            for k in ("tb_aq", "tb_bq", "tb_ak", "tb_bk")}
    Dmup = nc.dram_tensor("m_up", [128, 128], BF16, kind="ExternalInput").ap()
    Dmlo = nc.dram_tensor("m_lo", [128, 128], BF16, kind="ExternalInput").ap()
    DmT = nc.dram_tensor("maskT", [128, 128], F32, kind="ExternalInput").ap()
    Dident = nc.dram_tensor("ident", [128, 128], BF16, kind="ExternalInput").ap()

    S = {}

    def sb(name, shape, dt):
        S[name] = nc.alloc_sbuf_tensor(name, shape, dt)
        return S[name]

    u = sb("u", [128, CH * TC], F32)
    m_up = sb("c_m_up", [128, 128], BF16)
    m_lo = sb("c_m_lo", [128, 128], BF16)
    maskT = sb("c_maskT", [128, 128], F32)
    ident = sb("c_ident", [128, 128], BF16)
    ones_bf = sb("ones_bf", [128, 1], BF16)
    ones_f1 = sb("ones_f1", [1, 640], F32)
    gb_sb = sb("gb_sb", [2, CH * 128], F32)
    b1a_sb = sb("b1a_sb", [128, NJC], F32)
    b1g_sb = sb("b1g_sb", [128, NJC], F32)
    b2_sb = sb("b2_sb", [128, CH], F32)
    qs_sb = sb("qs_sb", [128, DH], F32)
    ks_sb = sb("ks_sb", [128, DH], F32)
    P0 = sb("P0", [128, 1024], F32)
    P1 = sb("P1", [128, 1024], F32)
    P2 = sb("P2", [128, 1024], F32)
    n2 = sb("n2", [128, 32], F32)
    rn = sb("rn", [128, 32], F32)
    negmax = sb("negmax", [128, 16], F32)
    den = sb("den", [128, 16], F32)
    rden = sb("rden", [128, 16], F32)
    kT_ring = sb("kT_ring", [128, 3 * 1024], BF16)
    v_ring = sb("v_ring", [128, 3 * 1024], BF16)
    out2_bf = sb("out2_bf", [128, 2 * 1024], BF16)
    lnbf = sb("lnbf", [128, 2 * 512], BF16)
    sqbf = sb("sqbf", [128, 2 * 512], BF16)
    rows = sb("rows", [1, 4 * 512], F32)
    r2neg = sb("r2neg", [2, 512], F32)
    cpool = sb("cpool", [128, 16], F32)
    cpool_vals = {}

    def cval(v):
        if v not in cpool_vals:
            i = len(cpool_vals)
            assert i < 16
            cpool_vals[v] = i
            vp.op("dve", (lambda i=i, v=v: lambda h: h.memset(
                cpool.ap()[:, i:i + 1], v))(), writes=["cpool"],
                tag=f"cset{i}")
        return cpool.ap()[:, cpool_vals[v]:cpool_vals[v] + 1]

    KB = 1024
    ARENA_B = 81 * KB
    arena = nc.alloc_sbuf_tensor("arena", [128, ARENA_B], mybir.dt.uint8)
    arena_addr = nc.lookup_mloc(arena).addr

    def ar(name, shape, dt, off):
        import math as _m
        per = 1
        for s in shape[1:]:
            per *= s
        per *= mybir.dt.size(dt)
        assert off + per <= ARENA_B, (name, off, per)
        hnd = nc.alloc_sbuf_tensor_at(name, shape, dt, offset=arena_addr + off)
        S[name] = hnd
        return hnd

    wqkv_sb = ar("wqkv_sb", [128, CH * 3072], BF16, 0)
    wo_sb = ar("wo_sb", [128, CH * 1024], BF16, 48 * KB)
    tabs = ar("tabs", [128, 2 * 4 * DH], F32, 64 * KB)
    tabs_raw = ar("tabs_raw", [128, 2 * 4 * DH], F32, 66 * KB)
    qT = ar("qT", [128, 1024], BF16, 68 * KB)
    rot_bf = ar("rot_bf", [128, 1024], BF16, 70 * KB)
    attn_bf = ar("attn_bf", [128, 2 * 384], BF16, 72 * KB)
    attnT_bf = ar("attnT_bf", [128, 2 * 384], BF16, 74 * KB)
    ubf = ar("ubf", [128, 2 * CH * 129], BF16, 76 * KB)
    xn_sb = ar("xn_sb", [128, CH * 1024], BF16, 0)
    hh_sb = ar("hh_sb", [128, NJC * 1024], BF16, 16 * KB)
    w1a_sb = ar("w1a_sb", [128, 2 * CH * 128], BF16, 60 * KB)
    w1g_sb = ar("w1g_sb", [128, 2 * CH * 128], BF16, 64 * KB)
    w2s_sb = ar("w2s_sb", [128, 2 * NJC * 128], BF16, 68 * KB)
    owT_sb = ar("owT_sb", [128, 2 * CH * 512], BF16, 0)
    olnx_sb = ar("olnx_sb", [128, CH * 512], BF16, 16 * KB)
    oqT_sb = ar("oqT_sb", [128, CH * 512], BF16, 24 * KB)
    okT_sb = ar("okT_sb", [128, CH * 512], BF16, 32 * KB)
    oktok_sb = ar("oktok_sb", [128, 4 * 1024], BF16, 40 * KB)
    ovaug_sb = ar("ovaug_sb", [128, 4 * H * 68], BF16, 48 * KB)
    oscm_sb = ar("oscm_sb", [128, 4 * 512], BF16, 57 * KB)
    oy_sb = ar("oy_sb", [128, 4 * 1024], BF16, 61 * KB)
    oyT_sb = ar("oyT_sb", [128, 1024], BF16, 69 * KB)
    oS = ar("oS", [128, 2 * 264], F32, 71 * KB + 256)
    oSbf = ar("oSbf", [128, 2 * 272], BF16, 74 * KB)
    owpT_sb = ar("owpT_sb", [128, 2 * 1024], BF16, 76 * KB)
    scr = ar("scr", [128, CH * 512], F32, 0)

    GROUPS = {
        "A": ("wqkv", "wo_sb", "tabs", "qT", "rot_bf", "attn_bf", "attnT_bf",
              "ubf"),
        "F": ("xn", "hh", "w1a", "w1g", "w2s"),
        "O": ("owT", "olnx", "oqT", "okT", "oktok", "ovaug", "oscm", "oy", "oS"),
        "FIN": ("scr",),
    }
    cur_group = ["FIN"]

    def phase(g):
        if cur_group[0] != g:
            vp.fence(GROUPS[cur_group[0]], GROUPS[g])
            cur_group[0] = g

    PS = {}

    def ps(name, shape, dt=F32):
        PS[name] = nc.alloc_psum_tensor(name, shape, dt)
        return PS[name]

    Q0 = ps("Q0", [128, 512]); Q1 = ps("Q1", [128, 512]); Q2 = ps("Q2", [128, 512])
    S0 = ps("S0", [128, 512]); S1 = ps("S1", [128, 512])
    O0 = ps("O0", [128, 512]); O1 = ps("O1", [128, 512])
    TB = ps("TB", [128, 1024], BF16)

    def vop(sem, fn, reads=(), writes=(), tag=""):
        return vp.op(sem, fn, reads=reads, writes=writes, tag=tag)

    def uap(c, col, n, f32=True):
        return u.ap()[:, c * TC + col: c * TC + col + n]

    def u_sh(c, tok, n):
        off = 0 if c >= CH // 2 else 1
        return u.ap()[:, c * TC + tok + off: c * TC + tok + off + n]

    NSL = 4

    def load_consts():
        vop("spd", lambda h: h.dma_start(out=m_up.ap(), in_=Dmup), writes=["c_m_up"])
        vop("spd", lambda h: h.dma_start(out=m_lo.ap(), in_=Dmlo), writes=["c_m_lo"])
        vop("spd", lambda h: h.dma_start(out=maskT.ap(), in_=DmT), writes=["c_maskT"])
        vop("spd", lambda h: h.dma_start(out=ident.ap(), in_=Dident), writes=["c_ident"])
        vop("dve", lambda h: h.memset(ones_bf.ap(), 1.0), writes=["ones_bf"])
        vop("dve", lambda h: h.memset(ones_f1.ap(), 1.0), writes=["ones_f1"])
        vop("dve", lambda h: h.memset(r2neg.ap(), -1.0), writes=["r2neg"])

    def load_x(b):
        src = Dx[b].rearrange("(c p) t -> p c t", p=128)
        dst = bass.AP(u, 1, [[CH * TC, 128], [TC, CH], [1, T]])
        vop("spd", lambda h: h.dma_start(out=dst, in_=src),
            writes=[f"u{c}_{t}" for c in range(CH) for t in range(NT)] + ["u"])
        zap = bass.AP(u, 0, [[CH * TC, 128], [TC, CH], [1, 1]])
        vop("dve", lambda h: h.memset(zap, 0.0), writes=["uzero"])

    ROW_R1, ROW_MEAN, ROW_TMP, ROW_VAR = 0, 1, 2, 3

    def row(k):
        return rows.ap()[0:1, k * 512:(k + 1) * 512]

    def emit_ln(dst_mode, sl, shifted, tag=""):
        t0 = sl * 512

        def ukey(c):
            lo = max(0, sl * 4 - (1 if shifted else 0))
            return [f"u{c}_{tt}" for tt in range(lo, sl * 4 + 4)]

        allkeys = [k for c in range(CH) for k in ukey(c)]

        for c in range(CH):
            off = 0 if (shifted and c >= CH // 2) else 1
            buf = c % 2
            src_ap = u.ap()[:, c * TC + t0 + off: c * TC + t0 + off + 512]
            vop("dve", (lambda src_ap=src_ap, buf=buf: lambda h: h.tensor_copy(
                lnbf.ap()[:, buf * 512:(buf + 1) * 512], src_ap))(),
                reads=ukey(c) + ["uzero"], writes=[f"lnbf{buf}"],
                tag=f"lncast{c}{tag}")
            vop("act", (lambda buf=buf: lambda h: h.activation(
                sqbf.ap()[:, buf * 512:(buf + 1) * 512],
                lnbf.ap()[:, buf * 512:(buf + 1) * 512], AF.Square))(),
                reads=[f"lnbf{buf}"], writes=[f"sqbf{buf}"], tag=f"lnsq{c}{tag}")
            vop("pe", (lambda c=c, buf=buf: lambda h: h.matmul(
                PS["S0"].ap()[0:1, :], ones_bf.ap(),
                lnbf.ap()[:, buf * 512:(buf + 1) * 512],
                start=(c == 0), stop=(c == CH - 1)))(),
                reads=[f"lnbf{buf}", "ones_bf"], writes=["psS0"],
                tag=f"lns1_{c}{tag}")
            vop("pe", (lambda c=c, buf=buf: lambda h: h.matmul(
                PS["S1"].ap()[0:1, :], ones_bf.ap(),
                sqbf.ap()[:, buf * 512:(buf + 1) * 512],
                start=(c == 0), stop=(c == CH - 1)))(),
                reads=[f"sqbf{buf}", "ones_bf"], writes=["psS1"],
                tag=f"lns2_{c}{tag}")
        vop("dve", lambda h: h.tensor_scalar_mul(
            row(ROW_MEAN), PS["S0"].ap()[0:1, :], 1.0 / DM),
            reads=["psS0"], writes=["rows_mean"], tag=f"lnmean{tag}")
        vop("act", lambda h: h.activation(row(ROW_TMP), row(ROW_MEAN), AF.Square),
            reads=["rows_mean"], writes=["rows_tmp"], tag=f"lnm2{tag}")
        vop("dve", lambda h: h.scalar_tensor_tensor(
            row(ROW_VAR), PS["S1"].ap()[0:1, :], 1.0 / DM,
            row(ROW_TMP), ALU.mult, ALU.subtract),
            reads=["psS1", "rows_tmp"], writes=["rows_var"], tag=f"lnvar{tag}")
        eps_ap = cval(1e-5)
        vop("act", lambda h: h.activation(row(ROW_TMP), row(ROW_VAR), AF.Ln,
                                          bias=eps_ap[0:1, :]),
            reads=["rows_var", "cpool"], writes=["rows_tmp"], tag=f"lnln{tag}")
        mhalf_ap = cval(-0.5)
        vop("act", lambda h: h.activation(row(ROW_R1), row(ROW_TMP), AF.Exp,
                                          scale=mhalf_ap[0:1, :]),
            reads=["rows_tmp", "cpool"], writes=["rows_r1"], tag=f"lnrstd{tag}")
        vop("dve", lambda h: h.tensor_mul(r2neg.ap()[0:1, :], row(ROW_MEAN),
                                          row(ROW_R1)),
            reads=["rows_r1", "rows_mean"], writes=["r2neg0"], tag=f"lnr2{tag}")
        vop("pe", lambda h: h.matmul(PS["O0"].ap(), ones_f1.ap()[0:1, 0:128],
                                     row(ROW_R1), start=True, stop=True),
            reads=["ones_f1", "rows_r1"], writes=["psO0"], tag=f"lnbcr1{tag}")
        vop("pe", lambda h: h.matmul(PS["O1"].ap(), ones_f1.ap()[0:1, 0:128],
                                     r2neg.ap()[0:1, :], start=True, stop=True),
            reads=["ones_f1", "r2neg0"], writes=["psO1"], tag=f"lnbcr2{tag}")
        for c in range(CH):
            off = 0 if (shifted and c >= CH // 2) else 1
            src_ap = u.ap()[:, c * TC + t0 + off: c * TC + t0 + off + 512]
            vop("dve", (lambda src_ap=src_ap: lambda h: h.tensor_mul(
                P0.ap()[:, 0:512], src_ap, PS["O0"].ap()))(),
                reads=ukey(c) + ["psO0", "uzero", "T_P0"], writes=["lnP0", "T_P0"],
                tag=f"lnc1_{c}{tag}")
            if dst_mode == "u":
                dst = u.ap()[:, c * TC + t0 + 1: c * TC + t0 + 1 + 512]
                dkey = [f"u{c}_{sl*4+q}" for q in range(4)]
            else:
                dst = olnx_sb.ap()[:, c * 512:(c + 1) * 512]
                dkey = ["olnx"]
            vop("dve", (lambda dst=dst: lambda h: h.tensor_sub(
                dst, P0.ap()[:, 0:512], PS["O1"].ap()))(),
                reads=["lnP0", "psO1", "T_P0"], writes=dkey, tag=f"lnc2_{c}{tag}")

    KDBG = os.environ.get("KDBG", "") == "1"
    NDBG = 16
    dbg_i = [0]
    if KDBG:
        Ddbg = nc.dram_tensor("dbg", [NDBG, 128, 2048], F32,
                              kind="ExternalOutput").ap()

    def dbg(ap, reads, note=""):
        if not KDBG or dbg_i[0] >= NDBG:
            return
        i = dbg_i[0]
        dbg_i[0] += 1
        p, f = ap.shape[0], ap.free_size()
        if ap.dtype == BF16:
            ap = ap.bitcast(F32)
            f = f // 2
        if f > 2048:
            f = 2048
            ap = ap[:, 0:2048]
        dst = Ddbg[i][0:p, 0:f]
        vop("spd", (lambda dst=dst, ap=ap: lambda h: h.dma_start(
            out=dst, in_=ap))(), reads=list(reads), writes=[f"dbg{i}"],
            tag=f"dbg{i}_{note}")
        print(f"DBG[{i}] {note} shape=({p},{f})")

    def emit_layer_consts(l):
        src_gb = Dgb[l].transpose([1, 0, 2])
        vop("spd", (lambda src_gb=src_gb: lambda h: h.dma_start(
            out=gb_sb.ap(), in_=src_gb))(), writes=["gb_sb"], tag=f"Lc_gb{l}")
        vop("spd", (lambda l=l: lambda h: h.dma_start(
            out=b1a_sb.ap(), in_=Db1a[l]))(), writes=["b1a_sb"], tag=f"Lc_b1a{l}")
        vop("spd", (lambda l=l: lambda h: h.dma_start(
            out=b1g_sb.ap(), in_=Db1g[l]))(), writes=["b1g_sb"], tag=f"Lc_b1g{l}")
        vop("spd", (lambda l=l: lambda h: h.dma_start(
            out=b2_sb.ap(), in_=Db2f[l]))(), writes=["b2_sb"], tag=f"Lc_b2{l}")
        vop("spd", (lambda l=l: lambda h: h.dma_start(
            out=qs_sb.ap(), in_=Dqs[l]))(), writes=["qs_sb"], tag=f"Lc_qs{l}")
        vop("spd", (lambda l=l: lambda h: h.dma_start(
            out=ks_sb.ap(), in_=Dks[l]))(), writes=["ks_sb"], tag=f"Lc_ks{l}")

    def emit_ln_xn(l, sl, tag=""):
        t0 = sl * 512
        base_in_pass = (sl % 2) * 512

        def ukey(c):
            lo = max(0, sl * 4 - 1)
            return [f"u{c}_{tt}" for tt in range(lo, sl * 4 + 4)]

        for c in range(CH):
            off = 0 if c >= CH // 2 else 1
            buf = c % 2
            src_ap = u.ap()[:, c * TC + t0 + off: c * TC + t0 + off + 512]
            vop("dve", (lambda src_ap=src_ap, buf=buf: lambda h: h.tensor_copy(
                lnbf.ap()[:, buf * 512:(buf + 1) * 512], src_ap))(),
                reads=ukey(c) + ["uzero"], writes=[f"lnbf{buf}"],
                tag=f"xcast{c}{tag}")
            vop("act", (lambda buf=buf: lambda h: h.activation(
                sqbf.ap()[:, buf * 512:(buf + 1) * 512],
                lnbf.ap()[:, buf * 512:(buf + 1) * 512], AF.Square))(),
                reads=[f"lnbf{buf}"], writes=[f"sqbf{buf}"], tag=f"xsq{c}{tag}")
            vop("pe", (lambda c=c, buf=buf: lambda h: h.matmul(
                PS["S0"].ap()[0:1, :], ones_bf.ap(),
                lnbf.ap()[:, buf * 512:(buf + 1) * 512],
                start=(c == 0), stop=(c == CH - 1)))(),
                reads=[f"lnbf{buf}", "ones_bf"], writes=["psS0"],
                tag=f"xs1_{c}{tag}")
            vop("pe", (lambda c=c, buf=buf: lambda h: h.matmul(
                PS["S1"].ap()[0:1, :], ones_bf.ap(),
                sqbf.ap()[:, buf * 512:(buf + 1) * 512],
                start=(c == 0), stop=(c == CH - 1)))(),
                reads=[f"sqbf{buf}", "ones_bf"], writes=["psS1"],
                tag=f"xs2_{c}{tag}")
        vop("dve", lambda h: h.tensor_scalar_mul(
            row(ROW_MEAN), PS["S0"].ap()[0:1, :], 1.0 / DM),
            reads=["psS0"], writes=["rows_mean"], tag=f"xmean{tag}")
        vop("act", lambda h: h.activation(row(ROW_TMP), row(ROW_MEAN), AF.Square),
            reads=["rows_mean"], writes=["rows_tmp"], tag=f"xm2{tag}")
        vop("dve", lambda h: h.scalar_tensor_tensor(
            row(ROW_VAR), PS["S1"].ap()[0:1, :], 1.0 / DM,
            row(ROW_TMP), ALU.mult, ALU.subtract),
            reads=["psS1", "rows_tmp"], writes=["rows_var"], tag=f"xvar{tag}")
        eps_ap = cval(1e-5)
        vop("act", lambda h: h.activation(row(ROW_TMP), row(ROW_VAR), AF.Ln,
                                          bias=eps_ap[0:1, :]),
            reads=["rows_var", "cpool"], writes=["rows_tmp"], tag=f"xln{tag}")
        mh_ap = cval(-0.5)
        vop("act", lambda h: h.activation(row(ROW_R1), row(ROW_TMP), AF.Exp,
                                          scale=mh_ap[0:1, :]),
            reads=["rows_tmp", "cpool"], writes=["rows_r1"], tag=f"xrstd{tag}")
        vop("dve", lambda h: h.tensor_mul(r2neg.ap()[0:1, :], row(ROW_MEAN),
                                          row(ROW_R1)),
            reads=["rows_r1", "rows_mean"], writes=["r2neg0"], tag=f"xr2{tag}")
        for c in range(CH):
            off = 0 if c >= CH // 2 else 1
            vop("pe", (lambda c=c: lambda h: h.matmul(
                PS["O0"].ap(), gb_sb.ap()[0:1, c * 128:(c + 1) * 128],
                row(ROW_R1), start=True, stop=True))(),
                reads=["gb_sb", "rows_r1"], writes=["psO0"], tag=f"xbc1_{c}{tag}")
            vop("pe", (lambda c=c: lambda h: h.matmul(
                PS["O1"].ap(), gb_sb.ap()[0:2, c * 128:(c + 1) * 128],
                r2neg.ap(), start=True, stop=True))(),
                reads=["gb_sb", "r2neg0", "r2neg"], writes=["psO1"],
                tag=f"xbc2_{c}{tag}")
            src_ap = u.ap()[:, c * TC + t0 + off: c * TC + t0 + off + 512]
            vop("dve", (lambda src_ap=src_ap: lambda h: h.tensor_mul(
                P0.ap()[:, 0:512], src_ap, PS["O0"].ap()))(),
                reads=ukey(c) + ["psO0", "uzero", "T_P0"], writes=["lnP0", "T_P0"],
                tag=f"xc1_{c}{tag}")
            dst = xn_sb.ap()[:, c * 1024 + base_in_pass: c * 1024 + base_in_pass + 512]
            vop("dve", (lambda dst=dst: lambda h: h.tensor_sub(
                dst, P0.ap()[:, 0:512], PS["O1"].ap()))(),
                reads=["lnP0", "psO1", "T_P0"], writes=[f"xn{c}"], tag=f"xc2_{c}{tag}")

    def emit_ffn(l, b):
        phase("F")
        W1l, W2l = Dw1[l], Dw2[l]

        def w1_load(jc, ring):
            jck = JCS[jc]
            for kind, dst, col0 in (("a", w1a_sb, 0), ("g", w1g_sb, DI)):
                src = bass.AP(W1l.tensor, W1l.offset + col0 + jc * 128,
                              [[2 * DI, 128], [128 * 2 * DI, CH], [1, jck]])
                dstap = dst.ap()[:, ring * CH * 128:(ring + 1) * CH * 128].rearrange(
                    "p (c j) -> p c j", c=CH)[:, :, 0:jck]
                vop("gpd", (lambda dstap=dstap, src=src: lambda h: h.dma_start(
                    out=dstap, in_=src))(),
                    writes=[f"w1{kind}r{ring}"], tag=f"w1{kind}_{jc}")

        def w2_load(oj, ring):
            src = bass.AP(W2l.tensor, W2l.offset + oj * 128,
                          [[DM, 128], [128 * DM, 21], [1, 128]])
            dstap = w2s_sb.ap()[:, ring * NJC * 128:(ring + 1) * NJC * 128].rearrange(
                "p (c j) -> p c j", c=NJC)[:, 0:21, :]
            vop("gpd", (lambda dstap=dstap, src=src: lambda h: h.dma_start(
                out=dstap, in_=src))(), writes=[f"w2r{ring}"], tag=f"w2m_{oj}")
            src2 = bass.AP(W2l.tensor, W2l.offset + 21 * 128 * DM + oj * 128,
                           [[DM, 42], [1, 128]])
            dst2 = w2s_sb.ap()[0:42, ring * NJC * 128 + 21 * 128:
                               ring * NJC * 128 + 22 * 128]
            vop("gpd", (lambda dst2=dst2, src2=src2: lambda h: h.dma_start(
                out=dst2, in_=src2))(), writes=[f"w2r{ring}t"], tag=f"w2t_{oj}")

        def w1_phase(p):
            for jc in range(NJC):
                jck = JCS[jc]
                ring = jc % 2
                w1_load(jc, ring)
                for ts in range(2):
                    gbank = "Q0" if ts == 0 else "Q2"
                    abank = "Q1" if ts == 0 else "O1"
                    def gmm(h, jc=jc, jck=jck, ring=ring, ts=ts, gbank=gbank):
                        last = None
                        for c in range(CH):
                            last = h.matmul(
                                PS[gbank].ap()[0:jck, :],
                                w1g_sb.ap()[:, ring * CH * 128 + c * 128:
                                            ring * CH * 128 + c * 128 + jck],
                                xn_sb.ap()[:, c * 1024 + ts * 512:
                                           c * 1024 + ts * 512 + 512],
                                start=(c == 0), stop=(c == CH - 1))
                        return last
                    vop("pe", gmm,
                        reads=[f"w1gr{ring}"] + [f"xn{c}" for c in range(CH)],
                        writes=[f"ps{gbank}"], tag=f"gmm{jc}_{ts}")
                    vop("act", (lambda jc=jc, jck=jck, ts=ts, gbank=gbank:
                                lambda h: h.activation(
                        P0.ap()[0:jck, ts * 512:ts * 512 + 512],
                        PS[gbank].ap()[0:jck, :], AF.Gelu,
                        bias=b1g_sb.ap()[0:jck, jc:jc + 1]))(),
                        reads=[f"ps{gbank}", "b1g_sb", "T_P0"], writes=[f"gel{ts}", "T_P0"],
                        tag=f"gelu{jc}_{ts}")
                    def amm(h, jc=jc, jck=jck, ring=ring, ts=ts, abank=abank):
                        last = None
                        for c in range(CH):
                            last = h.matmul(
                                PS[abank].ap()[0:jck, :],
                                w1a_sb.ap()[:, ring * CH * 128 + c * 128:
                                            ring * CH * 128 + c * 128 + jck],
                                xn_sb.ap()[:, c * 1024 + ts * 512:
                                           c * 1024 + ts * 512 + 512],
                                start=(c == 0), stop=(c == CH - 1))
                        return last
                    vop("pe", amm,
                        reads=[f"w1ar{ring}"] + [f"xn{c}" for c in range(CH)],
                        writes=[f"ps{abank}"], tag=f"amm{jc}_{ts}")
                    vop("dve", (lambda jc=jc, jck=jck, ts=ts, abank=abank:
                                lambda h: h.scalar_tensor_tensor(
                        hh_sb.ap()[0:jck, jc * 1024 + ts * 512:
                                   jc * 1024 + ts * 512 + 512],
                        PS[abank].ap()[0:jck, :],
                        b1a_sb.ap()[0:jck, jc:jc + 1],
                        P0.ap()[0:jck, ts * 512:ts * 512 + 512],
                        ALU.add, ALU.mult))(),
                        reads=[f"ps{abank}", f"gel{ts}", "b1a_sb", "T_P0"],
                        writes=[f"hh{jc}"], tag=f"hmul{jc}_{ts}")

        def w2_phase(p):
            for oj in range(CH):
                ring = oj % 2
                w2_load(oj, ring)
                for ts in range(2):
                    bank = "S0" if (oj * 2 + ts) % 2 == 0 else "S1"
                    def omm(h, oj=oj, ring=ring, ts=ts, bank=bank):
                        last = None
                        for hc in range(NJC):
                            hck = JCS[hc]
                            last = h.matmul(
                                PS[bank].ap(),
                                w2s_sb.ap()[0:hck, ring * NJC * 128 + hc * 128:
                                            ring * NJC * 128 + (hc + 1) * 128],
                                hh_sb.ap()[0:hck, hc * 1024 + ts * 512:
                                           hc * 1024 + ts * 512 + 512],
                                start=(hc == 0), stop=(hc == NJC - 1))
                        return last
                    vop("pe", omm,
                        reads=[f"w2r{ring}", f"w2r{ring}t"] +
                              [f"hh{hc}" for hc in range(NJC)],
                        writes=[f"ps{bank}"], tag=f"omm{oj}_{ts}")
                    tcol = p * 1024 + ts * 512
                    ukeys = [f"u{oj}_{tcol // 128 + q}" for q in range(4)]
                    vop("dve", (lambda oj=oj, ts=ts, bank=bank, tcol=tcol:
                                lambda h: h.scalar_tensor_tensor(
                        u.ap()[:, oj * TC + 1 + tcol: oj * TC + 1 + tcol + 512],
                        PS[bank].ap(), b2_sb.ap()[:, oj:oj + 1],
                        u.ap()[:, oj * TC + 1 + tcol: oj * TC + 1 + tcol + 512],
                        ALU.add, ALU.add))(),
                        reads=[f"ps{bank}", "b2_sb"] + ukeys,
                        writes=ukeys, tag=f"resid{oj}_{ts}")

        emit_ln_xn(l, 0, tag=f"_f{b}l{l}s0")
        emit_ln_xn(l, 1, tag=f"_f{b}l{l}s1")
        w1_phase(0)
        emit_ln_xn(l, 2, tag=f"_f{b}l{l}s2")
        dbg(rows.ap(), ["rows_r1", "rows_mean", "rows_var"], "rows_s2")
        emit_ln_xn(l, 3, tag=f"_f{b}l{l}s3")
        for c in range(CH):
            dbg(xn_sb.ap()[:, c * 1024:(c + 1) * 1024], [f"xn{c}"], f"xn_p1_c{c}")
        w2_phase(0)
        w1_phase(1)
        for hc in (0, 1):
            dbg(hh_sb.ap()[:, hc * 1024:(hc + 1) * 1024], [f"hh{hc}"],
                f"hh_p1_{hc}")
        w2_phase(1)

    def emit_attn(l, b):
        phase("A")
        tg = f"_a{b}l{l}"
        for c in range(CH):
            src_ap = Dwqkv[l][c * 128:(c + 1) * 128, :]
            vop("gpd", (lambda src_ap=src_ap, c=c: lambda h: h.dma_start(
                out=wqkv_sb.ap()[:, c * 3072:(c + 1) * 3072], in_=src_ap))(),
                writes=[f"wqkv{c}"], tag=f"wqkvL{c}{tg}")
            src2 = Dwo[l][c * 128:(c + 1) * 128, :]
            vop("gpd", (lambda src2=src2, c=c: lambda h: h.dma_start(
                out=wo_sb.ap()[:, c * 1024:(c + 1) * 1024], in_=src2))(),
                writes=[f"wo_sb{c}"], tag=f"woL{c}{tg}")

        TBKEYS = ["tb_aq", "tb_bq", "tb_ak", "tb_bk"]

        def emit_tables(t):
            tb = t % 2
            for k, key in enumerate(TBKEYS):
                src_ap = Dtab[key][:, t * DH:(t + 1) * DH]
                off = tb * 256 + k * 64
                vop("spd", (lambda src_ap=src_ap, off=off: lambda h: h.dma_start(
                    out=tabs_raw.ap()[:, off:off + DH], in_=src_ap))(),
                    writes=[f"tabs_raw{tb}_{k}"], tag=f"tabld{t}_{k}{tg}")
                ssb = qs_sb if k < 2 else ks_sb
                skey = "qs_sb" if k < 2 else "ks_sb"
                if k % 2 == 0:
                    sap = ssb.ap()
                else:
                    sap = bass.AP(ssb, 32, [[DH, 128], [-32, 2], [1, 32]])
                vop("gpc", (lambda off=off, sap=sap: lambda h: h.tensor_mul(
                    tabs.ap()[:, off:off + DH].rearrange(
                        "p (s d) -> p s d", s=2),
                    tabs_raw.ap()[:, off:off + DH].rearrange(
                        "p (s d) -> p s d", s=2),
                    sap if sap.ndim == 3 else
                    sap.rearrange("p (s d) -> p s d", s=2)))(),
                    reads=[f"tabs_raw{tb}_{k}", skey], writes=[f"tabs{tb}_{k}"],
                    tag=f"tabmul{t}_{k}{tg}")

        def emit_cast(t):
            buf = t % 2
            t0 = t * 128
            src_ap = bass.AP(u, t0, [[CH * TC, 128], [TC, CH], [1, 129]])
            dst = bass.AP(ubf, buf * CH * 129, [[2 * CH * 129, 128], [129, CH],
                                                [1, 129]])
            rkeys = ["uzero"] + [f"u{c}_{tt}" for c in range(CH)
                                 for tt in ([t - 1, t] if t > 0 else [t])]
            vop("gpc", (lambda dst=dst, src_ap=src_ap: lambda h: h.tensor_copy(
                dst, src_ap))(), reads=rkeys, writes=[f"ubf{buf}"],
                tag=f"ucast{t}{tg}")

        def ubf_slice(t, c):
            buf = t % 2
            off = 0 if c >= CH // 2 else 1
            base = buf * CH * 129 + c * 129 + off
            return ubf.ap()[:, base:base + 128]

        def emit_qkv_round(t, r):
            buf = t % 2
            for c in range(CH):
                def mm(h, t=t, r=r, c=c):
                    last = None
                    for j in range(2):
                        jt = 2 * r + j
                        last = h.matmul(
                            PS[f"Q{j}"].ap(), ubf_slice(t, c),
                            wqkv_sb.ap()[:, c * 3072 + jt * 512:
                                         c * 3072 + (jt + 1) * 512],
                            start=(c == 0), stop=(c == CH - 1))
                    return last
                vop("pe", mm, reads=[f"ubf{buf}", f"wqkv{c}"],
                    writes=["psQ0", "psQ1"], tag=f"qkv{t}r{r}c{c}{tg}")

        def process_qk(t, which):
            tb = t % 2
            ka, kb = ("tabs%d_0" % tb, "tabs%d_1" % tb) if which == "q" else                      ("tabs%d_2" % tb, "tabs%d_3" % tb)
            aoff = tb * 256 + (0 if which == "q" else 2) * 64
            boff = aoff + 64
            for j in range(2):
                vop("act", (lambda j=j: lambda h: h.copy(
                    P0.ap()[:, j * 512:(j + 1) * 512], PS[f"Q{j}"].ap()))(),
                    reads=[f"psQ{j}"], writes=["T_P0", f"evq{j}"],
                    tag=f"ev{which}{t}_{j}{tg}")
            vop("act", lambda h: h.activation(P1.ap(), P0.ap(), AF.Square),
                reads=["T_P0", "evq0", "evq1"], writes=["T_P1"],
                tag=f"sq{which}{t}{tg}")
            vop("dve", lambda h: h.reduce_sum(
                n2.ap()[:, 0:16],
                P1.ap().rearrange("p (hh d) -> p hh d", d=DH), axis=AX.X),
                reads=["T_P1"], writes=["n2"], tag=f"red{which}{t}{tg}")
            vop("act", lambda h: h.activation(rn.ap()[:, 16:32],
                                              n2.ap()[:, 0:16], AF.Ln),
                reads=["n2"], writes=["rnt"], tag=f"ln{which}{t}{tg}")
            mh = cval(-0.5)
            vop("act", lambda h: h.activation(rn.ap()[:, 0:16],
                                              rn.ap()[:, 16:32], AF.Exp,
                                              scale=mh),
                reads=["rnt", "cpool"], writes=["rn"], tag=f"ex{which}{t}{tg}")
            vop("gpc", lambda h: h.tensor_scalar_min(rn.ap()[:, 0:16],
                                                     rn.ap()[:, 0:16], 1e12),
                reads=["rn"], writes=["rn"], tag=f"cl{which}{t}{tg}")
            vop("gpc", lambda h: h.tensor_tensor(
                P1.ap().rearrange("p (hh d) -> p hh d", d=DH),
                P0.ap().rearrange("p (hh d) -> p hh d", d=DH),
                rn.ap()[:, 0:16].unsqueeze(2).broadcast_to([128, 16, DH]),
                ALU.mult),
                reads=["T_P0", "rn"], writes=["T_P1"], tag=f"qn{which}{t}{tg}")
            vop("gpc", (lambda aoff=aoff: lambda h: h.tensor_tensor(
                P0.ap().rearrange("p (hh d) -> p hh d", d=DH),
                P1.ap().rearrange("p (hh d) -> p hh d", d=DH),
                tabs.ap()[:, aoff:aoff + DH].unsqueeze(1)
                    .broadcast_to([128, 16, DH]),
                ALU.mult))(),
                reads=["T_P1", ka], writes=["T_P0"], tag=f"t1{which}{t}{tg}")
            swap_ap = bass.AP(P1, 32, [[1024, 128], [64, 16], [-32, 2], [1, 32]])
            vop("gpc", (lambda boff=boff, swap_ap=swap_ap: lambda h:
                        h.tensor_tensor(
                P2.ap().rearrange("p (hh s d) -> p hh s d", s=2, d=32),
                swap_ap,
                tabs.ap()[:, boff:boff + DH]
                    .rearrange("p (s d) -> p s d", s=2).unsqueeze(1)
                    .broadcast_to([128, 16, 2, 32]),
                ALU.mult))(),
                reads=["T_P1", kb], writes=["T_P2"], tag=f"t2{which}{t}{tg}")
            vop("gpc", lambda h: h.tensor_add(rot_bf.ap(), P0.ap(), P2.ap()),
                reads=["T_P0", "T_P2"], writes=["rot_bf"],
                tag=f"rot{which}{t}{tg}")
            def tp(h):
                last = None
                for c in range(CH):
                    last = h.transpose(PS["TB"].ap()[:, c * 128:(c + 1) * 128],
                                       rot_bf.ap()[:, c * 128:(c + 1) * 128],
                                       ident.ap())
                return last
            vop("pe", tp, reads=["rot_bf", "c_ident"], writes=["psTB"],
                tag=f"tp{which}{t}{tg}")
            if which == "q":
                dst, dkey = qT.ap(), "qTb"
            else:
                slot = t % 3
                dst, dkey = kT_ring.ap()[:, slot * 1024:(slot + 1) * 1024],                     f"kT{slot}"
            vop("act", (lambda dst=dst: lambda h: h.copy(
                dst, PS["TB"].ap()))(),
                reads=["psTB"], writes=[dkey], tag=f"tpe{which}{t}{tg}")

        def evac_v(t):
            slot = t % 3
            for j in range(2):
                vop("act", (lambda j=j, slot=slot: lambda h: h.copy(
                    v_ring.ap()[:, slot * 1024 + j * 512:
                                slot * 1024 + (j + 1) * 512],
                    PS[f"Q{j}"].ap()))(),
                    reads=[f"psQ{j}"], writes=[f"v{slot}"],
                    tag=f"evv{t}_{j}{tg}")

        def emit_head(t, hd):
            kts = [kt for kt in (t - 2, t - 1, t) if kt >= 0]
            w = len(kts) * 128
            sbank = "S0" if hd % 2 == 0 else "S1"
            ab = hd % 2
            hp = (hd % 2) * 64
            cc = hd // 2
            rk = [f"kT{kt % 3}" for kt in kts] + ["qTb"]

            def sim(h, t=t, kts=kts, sbank=sbank, hp=hp, cc=cc):
                last = None
                for idx, kt in enumerate(kts):
                    masked = (kt == t - 2) or (kt == t)
                    last = h.matmul(
                        PS[sbank].ap()[:, idx * 128:(idx + 1) * 128],
                        qT.ap()[hp:hp + 64, cc * 128:(cc + 1) * 128],
                        kT_ring.ap()[hp:hp + 64,
                                     (kt % 3) * 1024 + cc * 128:
                                     (kt % 3) * 1024 + (cc + 1) * 128],
                        start=True, stop=not masked)
                    if masked:
                        m = m_up if kt == t - 2 else m_lo
                        last = h.matmul(
                            PS[sbank].ap()[:, idx * 128:(idx + 1) * 128],
                            ident.ap(), m.ap(), start=False, stop=True)
                return last
            vop("pe", sim, reads=rk + ["c_ident", "c_m_up", "c_m_lo"],
                writes=[f"ps{sbank}"], tag=f"sim{t}h{hd}{tg}")
            vop("dve", (lambda w=w, sbank=sbank, hd=hd: lambda h: h.reduce_max(
                negmax.ap()[:, hd:hd + 1], PS[sbank].ap()[:, 0:w], axis=AX.X,
                negate=True))(),
                reads=[f"ps{sbank}"], writes=[f"nm{hd}"], tag=f"mx{t}h{hd}{tg}")
            vop("act", (lambda w=w, sbank=sbank, hd=hd, ab=ab: lambda h:
                        h.activation(
                attn_bf.ap()[:, ab * 384:ab * 384 + w], PS[sbank].ap()[:, 0:w],
                AF.Exp, bias=negmax.ap()[:, hd:hd + 1],
                accum_out=den.ap()[:, hd:hd + 1]))(),
                reads=[f"ps{sbank}", f"nm{hd}"],
                writes=[f"attn{ab}", f"den{hd}"], tag=f"exp{t}h{hd}{tg}")
            vop("dve", (lambda hd=hd: lambda h: h.reciprocal(
                rden.ap()[:, hd:hd + 1], den.ap()[:, hd:hd + 1]))(),
                reads=[f"den{hd}"], writes=[f"rden{hd}"], tag=f"rc{t}h{hd}{tg}")
            vop("gpc", (lambda w=w, hd=hd, ab=ab: lambda h: h.tensor_scalar_mul(
                attn_bf.ap()[:, ab * 384:ab * 384 + w],
                attn_bf.ap()[:, ab * 384:ab * 384 + w],
                rden.ap()[:, hd:hd + 1]))(),
                reads=[f"attn{ab}", f"rden{hd}"], writes=[f"attn{ab}"],
                tag=f"dv{t}h{hd}{tg}")
            def tpa(h, kts=kts, ab=ab):
                last = None
                for idx in range(len(kts)):
                    last = h.transpose(
                        PS["TB"].ap().bitcast(BF16)[:, idx * 128:(idx + 1) * 128]
                        if False else
                        PS["TB"].ap()[:, idx * 128:(idx + 1) * 128],
                        attn_bf.ap()[:, ab * 384 + idx * 128:
                                     ab * 384 + (idx + 1) * 128],
                        ident.ap())
                return last
            vop("pe", tpa, reads=[f"attn{ab}", "c_ident"], writes=["psTB"],
                tag=f"tpa{t}h{hd}{tg}")
            vop("act", (lambda w=w, ab=ab: lambda h: h.copy(
                attnT_bf.ap()[:, ab * 384:ab * 384 + w],
                PS["TB"].ap()[:, 0:w]))(),
                reads=["psTB"], writes=[f"attnT{ab}"], tag=f"tpe{t}h{hd}{tg}")
            obank = "O0" if cc < 4 else "O1"
            oreg = (cc % 4) * 128

            def av(h, kts=kts, hd=hd, hp=hp, ab=ab, obank=obank, oreg=oreg):
                last = None
                for idx, kt in enumerate(kts):
                    last = h.matmul(
                        PS[obank].ap()[hp:hp + 64, oreg:oreg + 128],
                        v_ring.ap()[:, (kt % 3) * 1024 + hd * 64:
                                    (kt % 3) * 1024 + hd * 64 + 64],
                        attnT_bf.ap()[:, ab * 384 + idx * 128:
                                      ab * 384 + (idx + 1) * 128],
                        start=(idx == 0), stop=(idx == len(kts) - 1),
                        tile_position=(0, 64) if hp else None)
                return last
            vop("pe", av,
                reads=[f"attnT{ab}"] + [f"v{kt % 3}" for kt in kts],
                writes=[f"ps{obank}"], tag=f"av{t}h{hd}{tg}")
            if hd % 2 == 1 and cc % 4 == 3:
                qq = t % 2
                c0 = cc - 3
                vop("dve", (lambda qq=qq, c0=c0, obank=obank: lambda h:
                            h.tensor_copy(
                    out2_bf.ap()[:, qq * 1024 + c0 * 128:
                                 qq * 1024 + (c0 + 4) * 128],
                    PS[obank].ap()))(),
                    reads=[f"ps{obank}"],
                    writes=[f"o2b{qq}_{c}" for c in range(c0, c0 + 4)],
                    tag=f"oev{t}h{hd}{tg}")

        def emit_wo(t):
            for oj in range(CH):
                bank = "Q2" if oj % 2 == 0 else "S0"
                rhs3 = out2_bf.ap().rearrange("p (qq cc q) -> p qq cc q",
                                              qq=2, cc=CH)

                def mm(h, oj=oj, bank=bank, rhs3=rhs3):
                    last = None
                    for cc in range(CH):
                        last = h.matmul(
                            PS[bank].ap()[:, 0:256],
                            wo_sb.ap()[:, cc * 1024 + oj * 128:
                                       cc * 1024 + (oj + 1) * 128],
                            rhs3[:, :, cc, :],
                            start=(cc == 0), stop=(cc == CH - 1))
                    return last
                vop("pe", mm,
                    reads=[f"o2b{q}_{c}" for q in range(2) for c in range(CH)] +
                          [f"wo_sb{c}" for c in range(CH)],
                    writes=[f"ps{bank}"], tag=f"wo{t}_{oj}{tg}")
                tcol = (t - 1) * 128
                ukeys = [f"u{oj}_{t - 1}", f"u{oj}_{t}"]
                vop("dve", (lambda oj=oj, bank=bank, tcol=tcol: lambda h:
                            h.tensor_add(
                    u.ap()[:, oj * TC + 1 + tcol: oj * TC + 1 + tcol + 256],
                    u.ap()[:, oj * TC + 1 + tcol: oj * TC + 1 + tcol + 256],
                    PS[bank].ap()[:, 0:256]))(),
                    reads=[f"ps{bank}"] + ukeys, writes=ukeys,
                    tag=f"wores{t}_{oj}{tg}")

        for t in range(NT):
            emit_tables(t)
            if t == 0:
                emit_cast(0)
            emit_qkv_round(t, 0)
            process_qk(t, "q")
            emit_qkv_round(t, 1)
            process_qk(t, "k")
            emit_qkv_round(t, 2)
            evac_v(t)
            for hd in range(H):
                emit_head(t, hd)
            if t + 1 < NT:
                emit_cast(t + 1)
            if t % 2 == 1:
                emit_wo(t)

    def emit_outer(b):
        tg = f"_o{b}"
        NSL_ = 4

        def wT_load(sl, jg):
            slot = jg % 2
            src_ap = bass.AP(DwaT.tensor, DwaT.offset + jg * 512,
                             [[3072, 128], [128 * 3072, CH], [1, 512]])
            vop("gpd", (lambda src_ap=src_ap, slot=slot: lambda h: h.dma_start(
                out=owT_sb.ap()[:, slot * CH * 512:(slot + 1) * CH * 512]
                .rearrange("p (c j) -> p c j", c=CH), in_=src_ap))(),
                writes=[f"owT{slot}"], tag=f"owT{sl}_{jg}{tg}")

        def qk_chunk(sl, jg, jj):
            slot = jg % 2
            bank = "Q0" if jj % 2 == 0 else "Q1"
            jc_glob = jg * 4 + jj

            def mm(h, slot=slot, bank=bank, jj=jj):
                last = None
                for c in range(CH):
                    last = h.matmul(
                        PS[bank].ap(),
                        owT_sb.ap()[:, slot * CH * 512 + c * 512 + jj * 128:
                                    slot * CH * 512 + c * 512 + (jj + 1) * 128],
                        olnx_sb.ap()[:, c * 512:(c + 1) * 512],
                        start=(c == 0), stop=(c == CH - 1))
                return last
            vop("pe", mm, reads=[f"owT{slot}", "olnx"], writes=[f"ps{bank}"],
                tag=f"oqk{sl}_{jc_glob}{tg}")
            vop("act", (lambda bank=bank: lambda h: h.activation(
                P0.ap()[:, 0:512], PS[bank].ap(), AF.Exp))(),
                reads=[f"ps{bank}"], writes=["T_P0"], tag=f"oexp{sl}_{jc_glob}{tg}")
            vop("dve", (lambda bank=bank: lambda h: h.tensor_scalar_add(
                P1.ap()[:, 0:512], PS[bank].ap(), 1.0))(),
                reads=[f"ps{bank}"], writes=["T_P1"], tag=f"ox1{sl}_{jc_glob}{tg}")
            if jc_glob < 8:
                dst, dkey = oqT_sb, f"oqT{jc_glob}"
                doff = jc_glob * 512
            else:
                dst, dkey = okT_sb, f"okT{jc_glob - 8}"
                doff = (jc_glob - 8) * 512
            vop("dve", (lambda dst=dst, doff=doff: lambda h: h.tensor_tensor(
                dst.ap()[:, doff:doff + 512], P0.ap()[:, 0:512],
                P1.ap()[:, 0:512], ALU.min))(),
                reads=["T_P0", "T_P1"], writes=[dkey], tag=f"omin{sl}_{jc_glob}{tg}")

        def v_tile(sl, tt):
            for g in range(2):
                jg = 4 + g
                slot = jg % 2
                bank = "Q0" if g == 0 else "Q1"

                def mm(h, slot=slot, bank=bank, tt=tt):
                    last = None
                    for c in range(CH):
                        last = h.matmul(
                            PS[bank].ap(),
                            olnx_sb.ap()[:, c * 512 + tt * 128:
                                         c * 512 + (tt + 1) * 128],
                            owT_sb.ap()[:, slot * CH * 512 + c * 512:
                                        slot * CH * 512 + (c + 1) * 512],
                            start=(c == 0), stop=(c == CH - 1))
                    return last
                vop("pe", mm, reads=[f"owT{slot}", "olnx"],
                    writes=[f"ps{bank}"], tag=f"ov{sl}_{tt}_{g}{tg}")
                dst = bass.AP(ovaug_sb, tt * H * 68 + g * 8 * 68,
                              [[4 * H * 68, 128], [68, 8], [1, 64]])
                vop("dve", (lambda dst=dst, bank=bank: lambda h: h.tensor_copy(
                    dst, PS[bank].ap().rearrange("p (hh d) -> p hh d", d=64)))(),
                    reads=[f"ps{bank}"], writes=[f"ovau{tt}_{g}"],
                    tag=f"ovev{sl}_{tt}_{g}{tg}")

        def ktok_tile(sl, tt):
            def tp(h, tt=tt):
                last = None
                for c in range(CH):
                    last = h.transpose(
                        PS["TB"].ap()[:, c * 128:(c + 1) * 128],
                        okT_sb.ap()[:, c * 512 + tt * 128:c * 512 + (tt + 1) * 128],
                        ident.ap())
                return last
            vop("pe", tp, reads=[f"okT{c}" for c in range(CH)] + ["c_ident"],
                writes=["psTB"], tag=f"oktp{sl}_{tt}{tg}")
            vop("dve", (lambda tt=tt: lambda h: h.tensor_copy(
                oktok_sb.ap()[:, tt * 1024:(tt + 1) * 1024], PS["TB"].ap()))(),
                reads=["psTB"], writes=[f"oktok{tt}"], tag=f"okte{sl}_{tt}{tg}")

        SCB = ["Q0", "Q1", "Q2", "S0"]
        YB = ["S1", "O0", "O1"]

        def scan_ctile(ct):
            sl, tt = ct // 4, ct % 4
            WAVES = int(os.environ.get('KOWAVE', '3'))
            for hd in range(H):
                bank = SCB[hd % 4]
                reg = (hd // 4) * 128
                hp = (hd % 2) * 64
                cc = hd // 2

                def mm(h, bank=bank, reg=reg, hp=hp, cc=cc, tt=tt):
                    return h.matmul(
                        PS[bank].ap()[:, reg:reg + 128],
                        okT_sb.ap()[hp:hp + 64, cc * 512 + tt * 128:
                                    cc * 512 + (tt + 1) * 128],
                        oqT_sb.ap()[hp:hp + 64, cc * 512 + tt * 128:
                                    cc * 512 + (tt + 1) * 128],
                        start=True, stop=True)
                vop("pe", mm,
                    reads=[f"okT{cc}", f"oqT{cc}"], writes=[f"ps{bank}"],
                    tag=f"osc{ct}h{hd}{tg}")
            if WAVES < 1:
                return
            for bi, bank in enumerate(SCB):
                def msk(h, bank=bank, bi=bi):
                    last = None
                    for r in range(4):
                        last = h.tensor_tensor(
                            oscm_sb.ap()[:, bi * 512 + r * 128:
                                         bi * 512 + (r + 1) * 128],
                            PS[bank].ap()[:, r * 128:(r + 1) * 128],
                            maskT.ap(), ALU.mult)
                    return last
                vop("dve", msk, reads=[f"ps{bank}", "c_maskT"],
                    writes=[f"oscm{bi}"], tag=f"omask{ct}_{bi}{tg}")
            if WAVES < 2:
                return
            for hd in range(H):
                bi = hd % 4
                yb = YB[min(hd // 7, 2)]
                yreg = (hd - 7 * min(hd // 7, 2)) * 66
                hp = (hd % 2) * 64
                cc = hd // 2

                def mm(h, hd=hd, bi=bi, yb=yb, yreg=yreg, hp=hp, cc=cc,
                       tt=tt, ct=ct):
                    last = h.matmul(
                        PS[yb].ap()[:, yreg:yreg + 65],
                        oscm_sb.ap()[:, bi * 512 + (hd // 4) * 128:
                                     bi * 512 + (hd // 4) * 128 + 128],
                        ovaug_sb.ap()[:, tt * H * 68 + hd * 68:
                                      tt * H * 68 + hd * 68 + 65],
                        start=True, stop=(ct == 0))
                    if ct > 0:
                        g = hd // 8
                        slot = (hd // 2) % 4
                        last = h.matmul(
                            PS[yb].ap()[:, yreg:yreg + 65],
                            oqT_sb.ap()[hp:hp + 64, cc * 512 + tt * 128:
                                        cc * 512 + (tt + 1) * 128],
                            oSbf.ap()[hp:hp + 64, g * 272 + slot * 68:
                                      g * 272 + slot * 68 + 65],
                            start=False, stop=True)
                    return last
                vop("pe", mm,
                    reads=[f"oscm{bi}", f"ovau{tt}_{hd // 8}", f"oqT{cc}",
                           "oSbf"],
                    writes=[f"ps{yb}"], tag=f"oy{ct}h{hd}{tg}")
            for yi, yb in enumerate(YB):
                nh = (7, 7, 2)[yi]
                h0 = (0, 7, 14)[yi]
                dap = bass.AP(PS[yb], 64, [[512, 128], [66, nh]])
                vop("dve", (lambda dap=dap, h0=h0, nh=nh: lambda h:
                            h.tensor_scalar_add(
                    den.ap()[:, h0:h0 + nh], dap, 1e-6))(),
                    reads=[f"ps{yb}"],
                    writes=[f"oden{yi}"], tag=f"oden{ct}_{yi}{tg}")
                vop("dve", (lambda h0=h0, nh=nh: lambda h: h.reciprocal(
                    rden.ap()[:, h0:h0 + nh], den.ap()[:, h0:h0 + nh]))(),
                    reads=[f"oden{yi}"], writes=[f"orden{yi}"],
                    tag=f"orc{ct}_{yi}{tg}")
                nap = bass.AP(PS[yb], 0, [[512, 128], [66, nh], [1, 64]])
                vop("dve", (lambda nap=nap, h0=h0, nh=nh, tt=tt, yi=yi:
                            lambda h: h.tensor_tensor(
                    oy_sb.ap()[:, tt * 1024 + h0 * 64:
                               tt * 1024 + (h0 + nh) * 64]
                    .rearrange("p (hh d) -> p hh d", d=64),
                    nap,
                    rden.ap()[:, h0:h0 + nh].unsqueeze(2)
                    .broadcast_to([128, nh, 64]), ALU.mult))(),
                    reads=[f"orden{yi}", f"ps{yb}"],
                    writes=[f"oyb{tt}_{yi}"], tag=f"oyev{ct}_{yi}{tg}")
            if WAVES < 3:
                return
            for hd in range(H):
                g = hd // 8
                bank = "Q0" if g == 0 else "Q1"
                slot = (hd % 8) // 2
                hp = (hd % 2) * 64

                def mm(h, hd=hd, bank=bank, slot=slot, hp=hp, tt=tt):
                    return h.matmul(
                        PS[bank].ap()[hp:hp + 64, slot * 66:slot * 66 + 65],
                        oktok_sb.ap()[:, tt * 1024 + hd * 64:
                                      tt * 1024 + hd * 64 + 64],
                        ovaug_sb.ap()[:, tt * H * 68 + hd * 68:
                                      tt * H * 68 + hd * 68 + 65],
                        start=True, stop=True,
                        tile_position=(0, 64) if hp else None)
                vop("pe", mm,
                    reads=[f"oktok{tt}", f"ovau{tt}_{hd // 8}"],
                    writes=[f"ps{bank}"], tag=f"oSd{ct}h{hd}{tg}")
            for g in range(2):
                bank = "Q0" if g == 0 else "Q1"
                rd = [f"ps{bank}"]
                if ct == 0:
                    vop("dve", (lambda bank=bank, g=g: lambda h: h.tensor_copy(
                        oS.ap()[:, g * 264:(g + 1) * 264],
                        PS[bank].ap()[:, 0:264]))(),
                        reads=rd, writes=[f"oS{g}"], tag=f"oSi{ct}_{g}{tg}")
                else:
                    vop("dve", (lambda bank=bank, g=g: lambda h: h.tensor_add(
                        oS.ap()[:, g * 264:(g + 1) * 264],
                        oS.ap()[:, g * 264:(g + 1) * 264],
                        PS[bank].ap()[:, 0:264]))(),
                        reads=rd + [f"oS{g}"], writes=[f"oS{g}"],
                        tag=f"oSa{ct}_{g}{tg}")
                vop("dve", (lambda g=g: lambda h: h.tensor_copy(
                    oSbf.ap()[:, g * 272:(g + 1) * 272]
                    .rearrange("p (s d) -> p s d", s=4)[:, :, 0:66],
                    oS.ap()[:, g * 264:(g + 1) * 264]
                    .rearrange("p (s d) -> p s d", s=4)))(),
                    reads=[f"oS{g}"], writes=["oSbf"], tag=f"oSc{ct}_{g}{tg}")

        def wpT_load(oj):
            slot = oj % 2
            src_ap = bass.AP(DwpT.tensor, DwpT.offset + oj * 128,
                             [[1024, 128], [128 * 1024, CH], [1, 128]])
            vop("gpd", (lambda src_ap=src_ap, slot=slot: lambda h: h.dma_start(
                out=owpT_sb.ap()[:, slot * 1024:(slot + 1) * 1024]
                .rearrange("p (c j) -> p c j", c=CH), in_=src_ap))(),
                writes=[f"owpT{slot}"], tag=f"owpT{oj}{tg}")

        def proj_ctile(ct):
            sl, tt = ct // 4, ct % 4
            def tp(h, tt=tt):
                last = None
                for c in range(CH):
                    last = h.transpose(
                        PS["TB"].ap()[:, c * 128:(c + 1) * 128],
                        oy_sb.ap()[:, tt * 1024 + c * 128:
                                   tt * 1024 + (c + 1) * 128],
                        ident.ap())
                return last
            vop("pe", tp, reads=[f"oyb{tt}_{yi}" for yi in range(3)] +
                ["c_ident"], writes=["psTB"], tag=f"oytp{ct}{tg}")
            vop("dve", lambda h: h.tensor_copy(oyT_sb.ap(), PS["TB"].ap()),
                reads=["psTB"], writes=["oyT"], tag=f"oyte{ct}{tg}")
            for oj in range(CH):
                slot = oj % 2
                wpT_load(oj)
                bank = "S1" if oj % 2 == 0 else "O0"

                def mm(h, oj=oj, slot=slot, bank=bank):
                    last = None
                    for c in range(CH):
                        last = h.matmul(
                            PS[bank].ap()[:, 0:128],
                            owpT_sb.ap()[:, slot * 1024 + c * 128:
                                         slot * 1024 + (c + 1) * 128],
                            oyT_sb.ap()[:, c * 128:(c + 1) * 128],
                            start=(c == 0), stop=(c == CH - 1))
                    return last
                vop("pe", mm, reads=[f"owpT{slot}", "oyT"],
                    writes=[f"ps{bank}"], tag=f"oprj{ct}_{oj}{tg}")
                tcol = ct * 128
                ukeys = [f"u{oj}_{ct}"]
                vop("dve", (lambda oj=oj, bank=bank, tcol=tcol: lambda h:
                            h.tensor_add(
                    u.ap()[:, oj * TC + 1 + tcol: oj * TC + 1 + tcol + 128],
                    u.ap()[:, oj * TC + 1 + tcol: oj * TC + 1 + tcol + 128],
                    PS[bank].ap()[:, 0:128]))(),
                    reads=[f"ps{bank}"] + ukeys, writes=ukeys,
                    tag=f"ores{ct}_{oj}{tg}")

        phase("O")
        oneap = bass.AP(ovaug_sb, 64, [[4 * H * 68, 128], [68, 4 * H]])
        vop("dve", lambda h: h.memset(oneap, 1.0), writes=["ovau_ones"],
            tag=f"ovones{tg}")
        OSUB = int(os.environ.get("KOSUB", "4"))
        for sl in range(NSL_):
            emit_ln("lnx", sl, shifted=False, tag=f"_ol{b}_{sl}")
            for jg in range(4):
                wT_load(sl, jg)
                if OSUB < 1:
                    continue
                for jj in range(4):
                    qk_chunk(sl, jg, jj)
            if OSUB < 2:
                continue
            for g in range(2):
                wT_load(sl, 4 + g)
            for tt in range(4):
                v_tile(sl, tt)
                ktok_tile(sl, tt)
            if OSUB < 3:
                continue
            for tt in range(4):
                ct = sl * 4 + tt
                scan_ctile(ct)
                if OSUB >= 4:
                    proj_ctile(ct)

    def emit_final(b, add_spill=True):
        phase("FIN")
        for sl in range(NSL):
            t0 = sl * 512
            src = Dspill[b].rearrange("(c p) t -> p c t", p=128)[:, :, t0:t0 + 512]
            if add_spill:
                vop("spd", (lambda src=src: lambda h: h.dma_start(out=scr.ap(), in_=src))(),
                    reads=["spill"], writes=["scr"], tag=f"fin_ld{sl}")
            ukeys = [f"u{c}_{sl*4+q}" for c in range(CH) for q in range(4)]
            def add(h, t0=t0):
                last = None
                for c in range(CH):
                    last = h.tensor_add(
                        u.ap()[:, c * TC + 1 + t0: c * TC + 1 + t0 + 512],
                        u.ap()[:, c * TC + 1 + t0: c * TC + 1 + t0 + 512],
                        scr.ap()[:, c * 512:(c + 1) * 512])
                return last
            if add_spill:
                vop("dve", add, reads=["scr"] + ukeys, writes=ukeys, tag=f"fin_add{sl}")
            dst = Dout[b].rearrange("(c p) t -> p c t", p=128)[:, :, t0:t0 + 512]
            srcu = bass.AP(u, 1 + t0, [[CH * TC, 128], [TC, CH], [1, 512]])
            vop("spd", (lambda dst=dst, srcu=srcu: lambda h: h.dma_start(
                out=dst, in_=srcu))(),
                reads=ukeys, writes=["outdma"], tag=f"fin_st{sl}")

    def emit_spill(b):
        ukeys = [f"u{c}_{t}" for c in range(CH) for t in range(NT)]
        dst = Dspill[b].rearrange("(c p) t -> p c t", p=128)
        srcu = bass.AP(u, 1, [[CH * TC, 128], [TC, CH], [1, T]])
        vop("spd", lambda h: h.dma_start(out=dst, in_=srcu),
            reads=ukeys, writes=["spill"], tag="spill")

    load_consts()
    for b in range(npass):
        load_x(b)
        if stage == 2:
            for sl in range(NSL):
                emit_ln("u", sl, shifted=False, tag=f"_o{b}_{sl}")
        if stage == 30:
            emit_layer_consts(0)
            emit_ffn(0, b)
        if stage == 40:
            emit_layer_consts(0)
            emit_attn(0, b)
        if stage == 50:
            for l in range(nlay):
                emit_layer_consts(l)
                emit_attn(l, b)
                emit_ffn(l, b)
        if stage == 60:
            emit_outer(b)
        if stage >= 99:
            emit_outer(b)
            emit_spill(b)
            for sl in range(NSL):
                emit_ln("u", sl, shifted=False, tag=f"_pi{b}_{sl}")
            for l in range(nlay):
                emit_layer_consts(l)
                emit_attn(l, b)
                emit_ffn(l, b)
        emit_final(b, add_spill=(stage >= 99))

    globals()['LAST_S'] = S
    globals()['LAST_PS'] = PS
    vp.emit()
    return nc


_BUILD_CACHE = {}


def kernel(**inputs):
    key = (STAGE, NPASS, NLAY)
    if key not in _BUILD_CACHE:
        _BUILD_CACHE[key] = build(NPASS, NLAY, STAGE)
    nc = _BUILD_CACHE[key]
    hl = host_layout(inputs)
    hc = host_consts()
    in_maps = []
    for c in range(NCORE):
        m = dict(
            x_t=hl["x_t"][c * BPC:(c + 1) * BPC],
            w_attn_T=hl["w_attn_T"], w_proj_T=hl["w_proj_T"],
            wqkv=hl["wqkv"], wo=hl["wo"], w1=hl["w1"], w2=hl["w2"],
            qs=hl["qs"], ks=hl["ks"], gb=hl["gb"],
            b1a=hl["b1a"], b1g=hl["b1g"], b2f=hl["b2f"],
            tb_aq=hc["tb_aq"], tb_bq=hc["tb_bq"],
            tb_ak=hc["tb_ak"], tb_bk=hc["tb_bk"],
            m_up=hc["m_up"], m_lo=hc["m_lo"], maskT=hc["maskT"],
            ident=hc["ident"],
        )
        in_maps.append(m)
    res = run_bass_kernel_spmd(nc, in_maps, core_ids=list(range(NCORE)))
    outs = [np.asarray(r["out_t"], np.float32) for r in res.results]
    full = np.concatenate(outs, axis=0)
    return np.ascontiguousarray(full.transpose(0, 2, 1))


# version 06
# speedup vs baseline: 1.4784x; 1.4784x over previous
import os
import numpy as np
import ml_dtypes
import concourse.bass as bass
import concourse.mybir as mybir
from concourse.bass_utils import run_bass_kernel_spmd

F32 = mybir.dt.float32
BF16 = mybir.dt.bfloat16
AF = mybir.ActivationFunctionType
ALU = mybir.AluOpType
AX = mybir.AxisListType

B_TOT, T, DM = 32, 2048, 1024
NCORE = 8
BPC = B_TOT // NCORE
CH = DM // 128
TC = T + 1
NT = T // 128
NL = 6
H, DH = 16, 64
INNER = H * DH
DI = 2730
NJC = 22
JCS = [128] * 21 + [42]
QK_SCALE = 8.0
NEG = -1e30

STAGE = int(os.environ.get("KSTAGE", "99"))
NPASS = int(os.environ.get("KNPASS", str(BPC)))
NLAY = int(os.environ.get("KNLAY", str(NL)))


def host_consts():
    c = {}
    half = np.arange(0, DH, 2, dtype=np.float64)
    inv_freq = 1.0 / 10000.0 ** (half / DH)
    t = np.arange(T, dtype=np.float64)
    freqs = t[:, None] * inv_freq[None, :]
    freqs = np.concatenate([freqs, freqs], -1)
    base_scale = (half + 0.4 * DH) / (1.4 * DH)
    power = (t - T // 2) / (256 // 2)
    scale = base_scale[None, :] ** power[:, None]
    scale = np.concatenate([scale, scale], -1)
    cos, sin = np.cos(freqs), np.sin(freqs)
    sgn = np.concatenate([-np.ones(DH // 2), np.ones(DH // 2)])[None, :]

    def tokmaj(a):
        return np.ascontiguousarray(
            a.reshape(NT, 128, DH).transpose(1, 0, 2).reshape(128, NT * DH)
        ).astype(np.float32)

    c["tb_aq"] = tokmaj(cos * scale * QK_SCALE)
    c["tb_bq"] = tokmaj(sin * scale * sgn * QK_SCALE)
    c["tb_ak"] = tokmaj(cos / scale)
    c["tb_bk"] = tokmaj(sin / scale * sgn)
    i = np.arange(128)[:, None]
    j = np.arange(128)[None, :]
    c["m_up"] = np.where(j >= i, 0.0, NEG).astype(ml_dtypes.bfloat16)
    c["m_lo"] = np.where(j <= i, 0.0, NEG).astype(ml_dtypes.bfloat16)
    c["maskT"] = (j >= i).astype(np.float32)
    c["ident"] = np.eye(128, dtype=ml_dtypes.bfloat16)
    return c


def host_layout(inp):
    d = {}
    d["x_t"] = np.ascontiguousarray(
        np.asarray(inp["L0_attn_x"], np.float32).transpose(0, 2, 1))
    d["w_attn_T"] = np.ascontiguousarray(
        np.asarray(inp["L0_attn_w_attn"], np.float32).T)
    d["w_proj_T"] = np.ascontiguousarray(
        np.asarray(inp["L0_attn_w_proj"], np.float32).T)
    d["wqkv"] = np.ascontiguousarray(np.asarray(inp["L0_ffn_Wqkv"], np.float32))
    d["wo"] = np.ascontiguousarray(np.asarray(inp["L0_ffn_Wo"], np.float32))
    d["w1"] = np.ascontiguousarray(np.asarray(inp["L0_ffn_W1"], np.float32))
    d["w2"] = np.ascontiguousarray(np.asarray(inp["L0_ffn_W2"], np.float32))
    d["qs"] = np.ascontiguousarray(np.broadcast_to(
        np.asarray(inp["L0_ffn_q_scale"], np.float32)[:, None, :], (NL, 128, DH)))
    d["ks"] = np.ascontiguousarray(np.broadcast_to(
        np.asarray(inp["L0_ffn_k_scale"], np.float32)[:, None, :], (NL, 128, DH)))
    g = np.asarray(inp["L0_ffn_ln_g"], np.float32).reshape(NL, CH, 128)
    b = np.asarray(inp["L0_ffn_ln_b"], np.float32).reshape(NL, CH, 128)
    d["gb"] = np.ascontiguousarray(np.stack([g, b], axis=2))
    b1 = np.asarray(inp["L0_ffn_b1"], np.float32)
    b1a = np.zeros((NL, NJC, 128), np.float32)
    b1g = np.zeros((NL, NJC, 128), np.float32)
    for l in range(NL):
        b1a[l].reshape(-1)[:DI] = b1[l, :DI]
        b1g[l].reshape(-1)[:DI] = b1[l, DI:]
    d["b1a"] = np.ascontiguousarray(b1a.transpose(0, 2, 1))
    d["b1g"] = np.ascontiguousarray(b1g.transpose(0, 2, 1))
    d["b2f"] = np.ascontiguousarray(
        np.asarray(inp["L0_ffn_b2"], np.float32).reshape(NL, CH, 128).transpose(0, 2, 1))
    return d


class VP:

    CSEMS = ("pe", "act", "dve", "gpc")
    QUEUES = ("spd", "gpd")
    NLANES = 8
    BODY = {"pe": "pe", "act": "act", "dve": "dve", "gpc": "gp",
            "spd": "sp", "gpd": "gp"}

    @classmethod
    def body_of_sem(cls, sname):
        for base in ("spd", "gpd", "gpc", "pe", "act", "dve"):
            if sname.startswith(base):
                return cls.BODY[base]
        raise KeyError(sname)

    SEM_LIMIT = 20000

    def __init__(self, nc):
        self.nc = nc
        self.sem = {}
        self.gen = {s: 0 for s in self.CSEMS}
        for s in self.CSEMS:
            self.sem[s] = nc.alloc_semaphore(f"vp_{s}")
        for q in self.QUEUES:
            for i in range(self.NLANES):
                self.sem[f"{q}{i}"] = nc.alloc_semaphore(f"vp_{q}{i}")
        self.cnt = {s: 0 for s in self.sem}
        self.rr = {q: 0 for q in self.QUEUES}
        self.ops = {b: [] for b in ("pe", "act", "dve", "gp", "sp")}
        self.last_w = {}
        self.readers = {}
        self.known = {b: {s: 0 for s in self.sem} for b in self.ops}
        self.fences = {}

    def op(self, sem, fn, reads=(), writes=(), tag=""):
        body = self.BODY[sem]
        is_dma = sem in self.QUEUES
        if is_dma:
            queue = sem
            lane = self.rr[queue]
            self.rr[queue] = (lane + 1) % self.NLANES
            sname = f"{queue}{lane}"
            amt = 16
        else:
            queue = None
            base = sem
            sname = base if self.gen[base] == 0 else f"{base}g{self.gen[base]}"
            if self.cnt.get(sname, 0) + 1 > self.SEM_LIMIT:
                self.gen[base] += 1
                sname = f"{base}g{self.gen[base]}"
                self.sem[sname] = self.nc.alloc_semaphore(f"vp_{sname}")
                self.cnt[sname] = 0
                for b in self.known:
                    self.known[b][sname] = 0
            amt = 1
        deps = {}

        def add(d):
            if d is None:
                return
            s, v = d
            if deps.get(s, 0) < v:
                deps[s] = v

        for k in reads:
            add(self.last_w.get(k))
            if k.startswith("ps"):
                for r in self.readers.get(k, ()):
                    if self.body_of_sem(r[0]) != body:
                        add(r)
        for k in writes:
            add(self.last_w.get(k))
            for r in self.readers.get(k, ()):
                add(r)
            for pfx, fd in self.fences.items():
                if k.startswith(pfx):
                    for s, v in fd.items():
                        add((s, v))
        if is_dma and self.cnt[sname] > 0:
            deps[sname] = max(deps.get(sname, 0), self.cnt[sname])
        waits = []
        for s, v in sorted(deps.items()):
            if sem == "pe" and s.startswith("pe"):
                continue
            if self.known[body][s] >= v:
                continue
            waits.append((s, v))
            self.known[body][s] = v
        self.cnt[sname] += amt
        me = (sname, self.cnt[sname])
        self.ops[body].append((waits, fn, sname, amt, tag))
        for k in reads:
            self.readers.setdefault(k, []).append(me)
        for k in writes:
            self.last_w[k] = me
            self.readers[k] = []
        return me

    def fence(self, from_prefixes, to_prefixes):
        deps = {}
        for k, lw in self.last_w.items():
            if any(k.startswith(p) for p in from_prefixes):
                s, v = lw
                deps[s] = max(deps.get(s, 0), v)
        for k, rd in self.readers.items():
            if any(k.startswith(p) for p in from_prefixes):
                for s, v in rd:
                    deps[s] = max(deps.get(s, 0), v)
        for p in to_prefixes:
            self.fences[p] = dict(deps)

    def emit(self):
        nc = self.nc
        finals = [(s, self.cnt[s]) for s in self.sem
                  if any(s.startswith(q) for q in self.QUEUES) and self.cnt[s]]
        with nc.Block() as block:
            def mk(body):
                def run(h):
                    for waits, fn, sname, amt, tag in self.ops[body]:
                        for s, v in waits:
                            h.wait_ge(self.sem[s], v)
                        inst = fn(h)
                        if inst is not None:
                            inst.then_inc(self.sem[sname], amt)
                    if body == "sp":
                        for s, v in finals:
                            h.wait_ge(self.sem[s], v)
                return run
            block.tensor(mk("pe"))
            block.scalar(mk("act"))
            block.vector(mk("dve"))
            block.gpsimd(mk("gp"))
            block.sync(mk("sp"))


def build(npass, nlay, stage):
    nc = bass.Bass("TRN2", target_bir_lowering=False)
    vp = VP(nc)

    Dx = nc.dram_tensor("x_t", [BPC, DM, T], F32, kind="ExternalInput").ap()
    Dout = nc.dram_tensor("out_t", [BPC, DM, T], F32, kind="ExternalOutput").ap()
    Dspill = nc.dram_tensor("x2_spill", [BPC, DM, T], F32, kind="Internal").ap()
    DwaT = nc.dram_tensor("w_attn_T", [DM, 3 * DM], F32, kind="ExternalInput").ap()
    DwpT = nc.dram_tensor("w_proj_T", [DM, DM], F32, kind="ExternalInput").ap()
    Dwqkv = nc.dram_tensor("wqkv", [NL, DM, 3 * INNER], F32, kind="ExternalInput").ap()
    Dwo = nc.dram_tensor("wo", [NL, INNER, DM], F32, kind="ExternalInput").ap()
    Dw1 = nc.dram_tensor("w1", [NL, DM, 2 * DI], F32, kind="ExternalInput").ap()
    Dw2 = nc.dram_tensor("w2", [NL, DI, DM], F32, kind="ExternalInput").ap()
    Dqs = nc.dram_tensor("qs", [NL, 128, DH], F32, kind="ExternalInput").ap()
    Dks = nc.dram_tensor("ks", [NL, 128, DH], F32, kind="ExternalInput").ap()
    Dgb = nc.dram_tensor("gb", [NL, CH, 2, 128], F32, kind="ExternalInput").ap()
    Db1a = nc.dram_tensor("b1a", [NL, 128, NJC], F32, kind="ExternalInput").ap()
    Db1g = nc.dram_tensor("b1g", [NL, 128, NJC], F32, kind="ExternalInput").ap()
    Db2f = nc.dram_tensor("b2f", [NL, 128, CH], F32, kind="ExternalInput").ap()
    Dtab = {k: nc.dram_tensor(k, [128, NT * DH], F32, kind="ExternalInput").ap()
            for k in ("tb_aq", "tb_bq", "tb_ak", "tb_bk")}
    Dmup = nc.dram_tensor("m_up", [128, 128], BF16, kind="ExternalInput").ap()
    Dmlo = nc.dram_tensor("m_lo", [128, 128], BF16, kind="ExternalInput").ap()
    DmT = nc.dram_tensor("maskT", [128, 128], F32, kind="ExternalInput").ap()
    Dident = nc.dram_tensor("ident", [128, 128], BF16, kind="ExternalInput").ap()

    S = {}

    def sb(name, shape, dt):
        S[name] = nc.alloc_sbuf_tensor(name, shape, dt)
        return S[name]

    u = sb("u", [128, CH * TC], F32)
    m_up = sb("c_m_up", [128, 128], BF16)
    m_lo = sb("c_m_lo", [128, 128], BF16)
    maskT = sb("c_maskT", [128, 128], F32)
    ident = sb("c_ident", [128, 128], BF16)
    ones_bf = sb("ones_bf", [128, 1], BF16)
    ones_f1 = sb("ones_f1", [1, 640], F32)
    gb_sb = sb("gb_sb", [2, CH * 128], F32)
    b1a_sb = sb("b1a_sb", [128, NJC], F32)
    b1g_sb = sb("b1g_sb", [128, NJC], F32)
    b2_sb = sb("b2_sb", [128, CH], F32)
    qs_sb = sb("qs_sb", [128, DH], F32)
    ks_sb = sb("ks_sb", [128, DH], F32)
    P0 = sb("P0", [128, 1024], F32)
    P1 = sb("P1", [128, 1024], F32)
    P2 = sb("P2", [128, 1024], F32)
    n2 = sb("n2", [128, 32], F32)
    rn = sb("rn", [128, 32], F32)
    negmax = sb("negmax", [128, 16], F32)
    den = sb("den", [128, 16], F32)
    rden = sb("rden", [128, 16], F32)
    kT_ring = sb("kT_ring", [128, 3 * 1024], BF16)
    v_ring = sb("v_ring", [128, 3 * 1024], BF16)
    out2_bf = sb("out2_bf", [128, 2 * 1024], BF16)
    lnbf = sb("lnbf", [128, 2 * 512], BF16)
    sqbf = sb("sqbf", [128, 2 * 512], BF16)
    rows = sb("rows", [1, 4 * 512], F32)
    r2neg = sb("r2neg", [2, 512], F32)
    cpool = sb("cpool", [128, 16], F32)
    cpool_vals = {}

    def cval(v):
        if v not in cpool_vals:
            i = len(cpool_vals)
            assert i < 16
            cpool_vals[v] = i
            vp.op("dve", (lambda i=i, v=v: lambda h: h.memset(
                cpool.ap()[:, i:i + 1], v))(), writes=["cpool"],
                tag=f"cset{i}")
        return cpool.ap()[:, cpool_vals[v]:cpool_vals[v] + 1]

    KB = 1024
    ARENA_B = 81 * KB
    arena = nc.alloc_sbuf_tensor("arena", [128, ARENA_B], mybir.dt.uint8)
    arena_addr = nc.lookup_mloc(arena).addr

    def ar(name, shape, dt, off):
        import math as _m
        per = 1
        for s in shape[1:]:
            per *= s
        per *= mybir.dt.size(dt)
        assert off + per <= ARENA_B, (name, off, per)
        hnd = nc.alloc_sbuf_tensor_at(name, shape, dt, offset=arena_addr + off)
        S[name] = hnd
        return hnd

    wqkv_sb = ar("wqkv_sb", [128, CH * 3072], BF16, 0)
    wo_sb = ar("wo_sb", [128, CH * 1024], BF16, 48 * KB)
    tabs = ar("tabs", [128, 2 * 4 * DH], F32, 64 * KB)
    tabs_raw = ar("tabs_raw", [128, 2 * 4 * DH], F32, 66 * KB)
    qT = ar("qT", [128, 1024], BF16, 68 * KB)
    rot_bf = ar("rot_bf", [128, 1024], BF16, 70 * KB)
    attn_bf = ar("attn_bf", [128, 2 * 384], BF16, 72 * KB)
    attnT_bf = ar("attnT_bf", [128, 2 * 384], BF16, 74 * KB)
    ubf = ar("ubf", [128, 2 * CH * 129], BF16, 76 * KB)
    xn_sb = ar("xn_sb", [128, CH * 1024], BF16, 0)
    hh_sb = ar("hh_sb", [128, NJC * 1024], BF16, 16 * KB)
    w1a_sb = ar("w1a_sb", [128, 2 * CH * 128], BF16, 60 * KB)
    w1g_sb = ar("w1g_sb", [128, 2 * CH * 128], BF16, 64 * KB)
    w2s_sb = ar("w2s_sb", [128, 2 * NJC * 128], BF16, 68 * KB)
    owT_sb = ar("owT_sb", [128, 2 * CH * 512], BF16, 0)
    olnx_sb = ar("olnx_sb", [128, CH * 512], BF16, 16 * KB)
    oqT_sb = ar("oqT_sb", [128, CH * 512], BF16, 24 * KB)
    okT_sb = ar("okT_sb", [128, CH * 512], BF16, 32 * KB)
    oktok_sb = ar("oktok_sb", [128, 4 * 1024], BF16, 40 * KB)
    ovaug_sb = ar("ovaug_sb", [128, 4 * H * 68], BF16, 48 * KB)
    oscm_sb = ar("oscm_sb", [128, 4 * 512], BF16, 57 * KB)
    oy_sb = ar("oy_sb", [128, 4 * 1024], BF16, 61 * KB)
    oyT_sb = ar("oyT_sb", [128, 1024], BF16, 69 * KB)
    oS = ar("oS", [128, 2 * 264], F32, 71 * KB + 256)
    oSbf = ar("oSbf", [128, 2 * 272], BF16, 74 * KB)
    owpT_sb = ar("owpT_sb", [128, 2 * 1024], BF16, 76 * KB)
    scr = ar("scr", [128, CH * 512], F32, 0)

    GROUPS = {
        "A": ("wqkv", "wo_sb", "tabs", "qT", "rot_bf", "attn_bf", "attnT_bf",
              "ubf"),
        "F": ("xn", "hh", "w1a", "w1g", "w2s"),
        "O": ("owT", "olnx", "oqT", "okT", "oktok", "ovaug", "oscm", "oy", "oS"),
        "FIN": ("scr",),
    }
    cur_group = ["FIN"]

    def phase(g):
        if cur_group[0] != g:
            vp.fence(GROUPS[cur_group[0]], GROUPS[g])
            cur_group[0] = g

    PS = {}

    def ps(name, shape, dt=F32):
        PS[name] = nc.alloc_psum_tensor(name, shape, dt)
        return PS[name]

    Q0 = ps("Q0", [128, 512]); Q1 = ps("Q1", [128, 512]); Q2 = ps("Q2", [128, 512])
    S0 = ps("S0", [128, 512]); S1 = ps("S1", [128, 512])
    O0 = ps("O0", [128, 512]); O1 = ps("O1", [128, 512])
    TB = ps("TB", [128, 1024], BF16)

    def vop(sem, fn, reads=(), writes=(), tag=""):
        return vp.op(sem, fn, reads=reads, writes=writes, tag=tag)

    def uap(c, col, n, f32=True):
        return u.ap()[:, c * TC + col: c * TC + col + n]

    def u_sh(c, tok, n):
        off = 0 if c >= CH // 2 else 1
        return u.ap()[:, c * TC + tok + off: c * TC + tok + off + n]

    NSL = 4

    def load_consts():
        vop("spd", lambda h: h.dma_start(out=m_up.ap(), in_=Dmup), writes=["c_m_up"])
        vop("spd", lambda h: h.dma_start(out=m_lo.ap(), in_=Dmlo), writes=["c_m_lo"])
        vop("spd", lambda h: h.dma_start(out=maskT.ap(), in_=DmT), writes=["c_maskT"])
        vop("spd", lambda h: h.dma_start(out=ident.ap(), in_=Dident), writes=["c_ident"])
        vop("dve", lambda h: h.memset(ones_bf.ap(), 1.0), writes=["ones_bf"])
        vop("dve", lambda h: h.memset(ones_f1.ap(), 1.0), writes=["ones_f1"])
        vop("dve", lambda h: h.memset(r2neg.ap(), -1.0), writes=["r2neg"])

    def load_x(b):
        src = Dx[b].rearrange("(c p) t -> p c t", p=128)
        dst = bass.AP(u, 1, [[CH * TC, 128], [TC, CH], [1, T]])
        vop("spd", lambda h: h.dma_start(out=dst, in_=src),
            writes=[f"u{c}_{t}" for c in range(CH) for t in range(NT)] + ["u"])
        zap = bass.AP(u, 0, [[CH * TC, 128], [TC, CH], [1, 1]])
        vop("dve", lambda h: h.memset(zap, 0.0), writes=["uzero"])

    ROW_R1, ROW_MEAN, ROW_TMP, ROW_VAR = 0, 1, 2, 3

    def row(k):
        return rows.ap()[0:1, k * 512:(k + 1) * 512]

    def emit_ln(dst_mode, sl, shifted, tag=""):
        t0 = sl * 512

        def ukey(c):
            lo = max(0, sl * 4 - (1 if shifted else 0))
            return [f"u{c}_{tt}" for tt in range(lo, sl * 4 + 4)]

        allkeys = [k for c in range(CH) for k in ukey(c)]

        for c in range(CH):
            off = 0 if (shifted and c >= CH // 2) else 1
            buf = c % 2
            src_ap = u.ap()[:, c * TC + t0 + off: c * TC + t0 + off + 512]
            vop("dve", (lambda src_ap=src_ap, buf=buf: lambda h: h.tensor_copy(
                lnbf.ap()[:, buf * 512:(buf + 1) * 512], src_ap))(),
                reads=ukey(c) + ["uzero"], writes=[f"lnbf{buf}"],
                tag=f"lncast{c}{tag}")
            vop("act", (lambda buf=buf: lambda h: h.activation(
                sqbf.ap()[:, buf * 512:(buf + 1) * 512],
                lnbf.ap()[:, buf * 512:(buf + 1) * 512], AF.Square))(),
                reads=[f"lnbf{buf}"], writes=[f"sqbf{buf}"], tag=f"lnsq{c}{tag}")
            vop("pe", (lambda c=c, buf=buf: lambda h: h.matmul(
                PS["S0"].ap()[0:1, :], ones_bf.ap(),
                lnbf.ap()[:, buf * 512:(buf + 1) * 512],
                start=(c == 0), stop=(c == CH - 1)))(),
                reads=[f"lnbf{buf}", "ones_bf"], writes=["psS0"],
                tag=f"lns1_{c}{tag}")
            vop("pe", (lambda c=c, buf=buf: lambda h: h.matmul(
                PS["S1"].ap()[0:1, :], ones_bf.ap(),
                sqbf.ap()[:, buf * 512:(buf + 1) * 512],
                start=(c == 0), stop=(c == CH - 1)))(),
                reads=[f"sqbf{buf}", "ones_bf"], writes=["psS1"],
                tag=f"lns2_{c}{tag}")
        vop("dve", lambda h: h.tensor_scalar_mul(
            row(ROW_MEAN), PS["S0"].ap()[0:1, :], 1.0 / DM),
            reads=["psS0"], writes=["rows_mean"], tag=f"lnmean{tag}")
        vop("act", lambda h: h.activation(row(ROW_TMP), row(ROW_MEAN), AF.Square),
            reads=["rows_mean"], writes=["rows_tmp"], tag=f"lnm2{tag}")
        vop("dve", lambda h: h.scalar_tensor_tensor(
            row(ROW_VAR), PS["S1"].ap()[0:1, :], 1.0 / DM,
            row(ROW_TMP), ALU.mult, ALU.subtract),
            reads=["psS1", "rows_tmp"], writes=["rows_var"], tag=f"lnvar{tag}")
        eps_ap = cval(1e-5)
        vop("act", lambda h: h.activation(row(ROW_TMP), row(ROW_VAR), AF.Ln,
                                          bias=eps_ap[0:1, :]),
            reads=["rows_var", "cpool"], writes=["rows_tmp"], tag=f"lnln{tag}")
        mhalf_ap = cval(-0.5)
        vop("act", lambda h: h.activation(row(ROW_R1), row(ROW_TMP), AF.Exp,
                                          scale=mhalf_ap[0:1, :]),
            reads=["rows_tmp", "cpool"], writes=["rows_r1"], tag=f"lnrstd{tag}")
        vop("dve", lambda h: h.tensor_mul(r2neg.ap()[0:1, :], row(ROW_MEAN),
                                          row(ROW_R1)),
            reads=["rows_r1", "rows_mean"], writes=["r2neg0"], tag=f"lnr2{tag}")
        vop("pe", lambda h: h.matmul(PS["O0"].ap(), ones_f1.ap()[0:1, 0:128],
                                     row(ROW_R1), start=True, stop=True),
            reads=["ones_f1", "rows_r1"], writes=["psO0"], tag=f"lnbcr1{tag}")
        vop("pe", lambda h: h.matmul(PS["O1"].ap(), ones_f1.ap()[0:1, 0:128],
                                     r2neg.ap()[0:1, :], start=True, stop=True),
            reads=["ones_f1", "r2neg0"], writes=["psO1"], tag=f"lnbcr2{tag}")
        for c in range(CH):
            off = 0 if (shifted and c >= CH // 2) else 1
            src_ap = u.ap()[:, c * TC + t0 + off: c * TC + t0 + off + 512]
            vop("dve", (lambda src_ap=src_ap: lambda h: h.tensor_mul(
                P0.ap()[:, 0:512], src_ap, PS["O0"].ap()))(),
                reads=ukey(c) + ["psO0", "uzero", "T_P0"], writes=["lnP0", "T_P0"],
                tag=f"lnc1_{c}{tag}")
            if dst_mode == "u":
                dst = u.ap()[:, c * TC + t0 + 1: c * TC + t0 + 1 + 512]
                dkey = [f"u{c}_{sl*4+q}" for q in range(4)]
            else:
                dst = olnx_sb.ap()[:, c * 512:(c + 1) * 512]
                dkey = ["olnx"]
            vop("dve", (lambda dst=dst: lambda h: h.tensor_sub(
                dst, P0.ap()[:, 0:512], PS["O1"].ap()))(),
                reads=["lnP0", "psO1", "T_P0"], writes=dkey, tag=f"lnc2_{c}{tag}")

    KDBG = os.environ.get("KDBG", "") == "1"
    NDBG = 16
    dbg_i = [0]
    if KDBG:
        Ddbg = nc.dram_tensor("dbg", [NDBG, 128, 2048], F32,
                              kind="ExternalOutput").ap()

    def dbg(ap, reads, note=""):
        if not KDBG or dbg_i[0] >= NDBG:
            return
        i = dbg_i[0]
        dbg_i[0] += 1
        p, f = ap.shape[0], ap.free_size()
        if ap.dtype == BF16:
            ap = ap.bitcast(F32)
            f = f // 2
        if f > 2048:
            f = 2048
            ap = ap[:, 0:2048]
        dst = Ddbg[i][0:p, 0:f]
        vop("spd", (lambda dst=dst, ap=ap: lambda h: h.dma_start(
            out=dst, in_=ap))(), reads=list(reads), writes=[f"dbg{i}"],
            tag=f"dbg{i}_{note}")
        print(f"DBG[{i}] {note} shape=({p},{f})")

    def emit_layer_consts(l):
        src_gb = Dgb[l].transpose([1, 0, 2])
        vop("spd", (lambda src_gb=src_gb: lambda h: h.dma_start(
            out=gb_sb.ap(), in_=src_gb))(), writes=["gb_sb"], tag=f"Lc_gb{l}")
        vop("spd", (lambda l=l: lambda h: h.dma_start(
            out=b1a_sb.ap(), in_=Db1a[l]))(), writes=["b1a_sb"], tag=f"Lc_b1a{l}")
        vop("spd", (lambda l=l: lambda h: h.dma_start(
            out=b1g_sb.ap(), in_=Db1g[l]))(), writes=["b1g_sb"], tag=f"Lc_b1g{l}")
        vop("spd", (lambda l=l: lambda h: h.dma_start(
            out=b2_sb.ap(), in_=Db2f[l]))(), writes=["b2_sb"], tag=f"Lc_b2{l}")
        vop("spd", (lambda l=l: lambda h: h.dma_start(
            out=qs_sb.ap(), in_=Dqs[l]))(), writes=["qs_sb"], tag=f"Lc_qs{l}")
        vop("spd", (lambda l=l: lambda h: h.dma_start(
            out=ks_sb.ap(), in_=Dks[l]))(), writes=["ks_sb"], tag=f"Lc_ks{l}")

    def emit_ln_xn(l, sl, tag=""):
        t0 = sl * 512
        base_in_pass = (sl % 2) * 512

        def ukey(c):
            lo = max(0, sl * 4 - 1)
            return [f"u{c}_{tt}" for tt in range(lo, sl * 4 + 4)]

        for c in range(CH):
            off = 0 if c >= CH // 2 else 1
            buf = c % 2
            src_ap = u.ap()[:, c * TC + t0 + off: c * TC + t0 + off + 512]
            vop("dve", (lambda src_ap=src_ap, buf=buf: lambda h: h.tensor_copy(
                lnbf.ap()[:, buf * 512:(buf + 1) * 512], src_ap))(),
                reads=ukey(c) + ["uzero"], writes=[f"lnbf{buf}"],
                tag=f"xcast{c}{tag}")
            vop("act", (lambda buf=buf: lambda h: h.activation(
                sqbf.ap()[:, buf * 512:(buf + 1) * 512],
                lnbf.ap()[:, buf * 512:(buf + 1) * 512], AF.Square))(),
                reads=[f"lnbf{buf}"], writes=[f"sqbf{buf}"], tag=f"xsq{c}{tag}")
            vop("pe", (lambda c=c, buf=buf: lambda h: h.matmul(
                PS["S0"].ap()[0:1, :], ones_bf.ap(),
                lnbf.ap()[:, buf * 512:(buf + 1) * 512],
                start=(c == 0), stop=(c == CH - 1)))(),
                reads=[f"lnbf{buf}", "ones_bf"], writes=["psS0"],
                tag=f"xs1_{c}{tag}")
            vop("pe", (lambda c=c, buf=buf: lambda h: h.matmul(
                PS["S1"].ap()[0:1, :], ones_bf.ap(),
                sqbf.ap()[:, buf * 512:(buf + 1) * 512],
                start=(c == 0), stop=(c == CH - 1)))(),
                reads=[f"sqbf{buf}", "ones_bf"], writes=["psS1"],
                tag=f"xs2_{c}{tag}")
        vop("dve", lambda h: h.tensor_scalar_mul(
            row(ROW_MEAN), PS["S0"].ap()[0:1, :], 1.0 / DM),
            reads=["psS0"], writes=["rows_mean"], tag=f"xmean{tag}")
        vop("act", lambda h: h.activation(row(ROW_TMP), row(ROW_MEAN), AF.Square),
            reads=["rows_mean"], writes=["rows_tmp"], tag=f"xm2{tag}")
        vop("dve", lambda h: h.scalar_tensor_tensor(
            row(ROW_VAR), PS["S1"].ap()[0:1, :], 1.0 / DM,
            row(ROW_TMP), ALU.mult, ALU.subtract),
            reads=["psS1", "rows_tmp"], writes=["rows_var"], tag=f"xvar{tag}")
        eps_ap = cval(1e-5)
        vop("act", lambda h: h.activation(row(ROW_TMP), row(ROW_VAR), AF.Ln,
                                          bias=eps_ap[0:1, :]),
            reads=["rows_var", "cpool"], writes=["rows_tmp"], tag=f"xln{tag}")
        mh_ap = cval(-0.5)
        vop("act", lambda h: h.activation(row(ROW_R1), row(ROW_TMP), AF.Exp,
                                          scale=mh_ap[0:1, :]),
            reads=["rows_tmp", "cpool"], writes=["rows_r1"], tag=f"xrstd{tag}")
        vop("dve", lambda h: h.tensor_mul(r2neg.ap()[0:1, :], row(ROW_MEAN),
                                          row(ROW_R1)),
            reads=["rows_r1", "rows_mean"], writes=["r2neg0"], tag=f"xr2{tag}")
        for c in range(CH):
            off = 0 if c >= CH // 2 else 1
            vop("pe", (lambda c=c: lambda h: h.matmul(
                PS["O0"].ap(), gb_sb.ap()[0:1, c * 128:(c + 1) * 128],
                row(ROW_R1), start=True, stop=True))(),
                reads=["gb_sb", "rows_r1"], writes=["psO0"], tag=f"xbc1_{c}{tag}")
            vop("pe", (lambda c=c: lambda h: h.matmul(
                PS["O1"].ap(), gb_sb.ap()[0:2, c * 128:(c + 1) * 128],
                r2neg.ap(), start=True, stop=True))(),
                reads=["gb_sb", "r2neg0", "r2neg"], writes=["psO1"],
                tag=f"xbc2_{c}{tag}")
            src_ap = u.ap()[:, c * TC + t0 + off: c * TC + t0 + off + 512]
            vop("dve", (lambda src_ap=src_ap: lambda h: h.tensor_mul(
                P0.ap()[:, 0:512], src_ap, PS["O0"].ap()))(),
                reads=ukey(c) + ["psO0", "uzero", "T_P0"], writes=["lnP0", "T_P0"],
                tag=f"xc1_{c}{tag}")
            dst = xn_sb.ap()[:, c * 1024 + base_in_pass: c * 1024 + base_in_pass + 512]
            vop("dve", (lambda dst=dst: lambda h: h.tensor_sub(
                dst, P0.ap()[:, 0:512], PS["O1"].ap()))(),
                reads=["lnP0", "psO1", "T_P0"], writes=[f"xn{c}"], tag=f"xc2_{c}{tag}")

    def emit_ffn(l, b):
        phase("F")
        W1l, W2l = Dw1[l], Dw2[l]

        def w1_load(jc, ring):
            jck = JCS[jc]
            for kind, dst, col0 in (("a", w1a_sb, 0), ("g", w1g_sb, DI)):
                src = bass.AP(W1l.tensor, W1l.offset + col0 + jc * 128,
                              [[2 * DI, 128], [128 * 2 * DI, CH], [1, jck]])
                dstap = dst.ap()[:, ring * CH * 128:(ring + 1) * CH * 128].rearrange(
                    "p (c j) -> p c j", c=CH)[:, :, 0:jck]
                vop("gpd", (lambda dstap=dstap, src=src: lambda h: h.dma_start(
                    out=dstap, in_=src))(),
                    writes=[f"w1{kind}r{ring}"], tag=f"w1{kind}_{jc}")

        def w2_load(oj, ring):
            src = bass.AP(W2l.tensor, W2l.offset + oj * 128,
                          [[DM, 128], [128 * DM, 21], [1, 128]])
            dstap = w2s_sb.ap()[:, ring * NJC * 128:(ring + 1) * NJC * 128].rearrange(
                "p (c j) -> p c j", c=NJC)[:, 0:21, :]
            vop("gpd", (lambda dstap=dstap, src=src: lambda h: h.dma_start(
                out=dstap, in_=src))(), writes=[f"w2r{ring}"], tag=f"w2m_{oj}")
            src2 = bass.AP(W2l.tensor, W2l.offset + 21 * 128 * DM + oj * 128,
                           [[DM, 42], [1, 128]])
            dst2 = w2s_sb.ap()[0:42, ring * NJC * 128 + 21 * 128:
                               ring * NJC * 128 + 22 * 128]
            vop("gpd", (lambda dst2=dst2, src2=src2: lambda h: h.dma_start(
                out=dst2, in_=src2))(), writes=[f"w2r{ring}t"], tag=f"w2t_{oj}")

        def w1_phase(p):
            for jc in range(NJC):
                jck = JCS[jc]
                ring = jc % 2
                w1_load(jc, ring)
                for ts in range(2):
                    gbank = "Q0" if ts == 0 else "Q2"
                    abank = "Q1" if ts == 0 else "O1"
                    def gmm(h, jc=jc, jck=jck, ring=ring, ts=ts, gbank=gbank):
                        last = None
                        for c in range(CH):
                            last = h.matmul(
                                PS[gbank].ap()[0:jck, :],
                                w1g_sb.ap()[:, ring * CH * 128 + c * 128:
                                            ring * CH * 128 + c * 128 + jck],
                                xn_sb.ap()[:, c * 1024 + ts * 512:
                                           c * 1024 + ts * 512 + 512],
                                start=(c == 0), stop=(c == CH - 1))
                        return last
                    vop("pe", gmm,
                        reads=[f"w1gr{ring}"] + [f"xn{c}" for c in range(CH)],
                        writes=[f"ps{gbank}"], tag=f"gmm{jc}_{ts}")
                    vop("act", (lambda jc=jc, jck=jck, ts=ts, gbank=gbank:
                                lambda h: h.activation(
                        P0.ap()[0:jck, ts * 512:ts * 512 + 512],
                        PS[gbank].ap()[0:jck, :], AF.Gelu,
                        bias=b1g_sb.ap()[0:jck, jc:jc + 1]))(),
                        reads=[f"ps{gbank}", "b1g_sb", "T_P0"], writes=[f"gel{ts}", "T_P0"],
                        tag=f"gelu{jc}_{ts}")
                    def amm(h, jc=jc, jck=jck, ring=ring, ts=ts, abank=abank):
                        last = None
                        for c in range(CH):
                            last = h.matmul(
                                PS[abank].ap()[0:jck, :],
                                w1a_sb.ap()[:, ring * CH * 128 + c * 128:
                                            ring * CH * 128 + c * 128 + jck],
                                xn_sb.ap()[:, c * 1024 + ts * 512:
                                           c * 1024 + ts * 512 + 512],
                                start=(c == 0), stop=(c == CH - 1))
                        return last
                    vop("pe", amm,
                        reads=[f"w1ar{ring}"] + [f"xn{c}" for c in range(CH)],
                        writes=[f"ps{abank}"], tag=f"amm{jc}_{ts}")
                    vop("dve", (lambda jc=jc, jck=jck, ts=ts, abank=abank:
                                lambda h: h.scalar_tensor_tensor(
                        hh_sb.ap()[0:jck, jc * 1024 + ts * 512:
                                   jc * 1024 + ts * 512 + 512],
                        PS[abank].ap()[0:jck, :],
                        b1a_sb.ap()[0:jck, jc:jc + 1],
                        P0.ap()[0:jck, ts * 512:ts * 512 + 512],
                        ALU.add, ALU.mult))(),
                        reads=[f"ps{abank}", f"gel{ts}", "b1a_sb", "T_P0"],
                        writes=[f"hh{jc}"], tag=f"hmul{jc}_{ts}")

        def w2_phase(p):
            for oj in range(CH):
                ring = oj % 2
                w2_load(oj, ring)
                for ts in range(2):
                    bank = "S0" if (oj * 2 + ts) % 2 == 0 else "S1"
                    def omm(h, oj=oj, ring=ring, ts=ts, bank=bank):
                        last = None
                        for hc in range(NJC):
                            hck = JCS[hc]
                            last = h.matmul(
                                PS[bank].ap(),
                                w2s_sb.ap()[0:hck, ring * NJC * 128 + hc * 128:
                                            ring * NJC * 128 + (hc + 1) * 128],
                                hh_sb.ap()[0:hck, hc * 1024 + ts * 512:
                                           hc * 1024 + ts * 512 + 512],
                                start=(hc == 0), stop=(hc == NJC - 1))
                        return last
                    vop("pe", omm,
                        reads=[f"w2r{ring}", f"w2r{ring}t"] +
                              [f"hh{hc}" for hc in range(NJC)],
                        writes=[f"ps{bank}"], tag=f"omm{oj}_{ts}")
                    tcol = p * 1024 + ts * 512
                    ukeys = [f"u{oj}_{tcol // 128 + q}" for q in range(4)]
                    vop("dve", (lambda oj=oj, ts=ts, bank=bank, tcol=tcol:
                                lambda h: h.scalar_tensor_tensor(
                        u.ap()[:, oj * TC + 1 + tcol: oj * TC + 1 + tcol + 512],
                        PS[bank].ap(), b2_sb.ap()[:, oj:oj + 1],
                        u.ap()[:, oj * TC + 1 + tcol: oj * TC + 1 + tcol + 512],
                        ALU.add, ALU.add))(),
                        reads=[f"ps{bank}", "b2_sb"] + ukeys,
                        writes=ukeys, tag=f"resid{oj}_{ts}")

        emit_ln_xn(l, 0, tag=f"_f{b}l{l}s0")
        emit_ln_xn(l, 1, tag=f"_f{b}l{l}s1")
        w1_phase(0)
        emit_ln_xn(l, 2, tag=f"_f{b}l{l}s2")
        dbg(rows.ap(), ["rows_r1", "rows_mean", "rows_var"], "rows_s2")
        emit_ln_xn(l, 3, tag=f"_f{b}l{l}s3")
        for c in range(CH):
            dbg(xn_sb.ap()[:, c * 1024:(c + 1) * 1024], [f"xn{c}"], f"xn_p1_c{c}")
        w2_phase(0)
        w1_phase(1)
        for hc in (0, 1):
            dbg(hh_sb.ap()[:, hc * 1024:(hc + 1) * 1024], [f"hh{hc}"],
                f"hh_p1_{hc}")
        w2_phase(1)

    def emit_attn(l, b):
        phase("A")
        tg = f"_a{b}l{l}"
        for c in range(CH):
            src_ap = Dwqkv[l][c * 128:(c + 1) * 128, :]
            vop("gpd", (lambda src_ap=src_ap, c=c: lambda h: h.dma_start(
                out=wqkv_sb.ap()[:, c * 3072:(c + 1) * 3072], in_=src_ap))(),
                writes=[f"wqkv{c}"], tag=f"wqkvL{c}{tg}")
            src2 = Dwo[l][c * 128:(c + 1) * 128, :]
            vop("gpd", (lambda src2=src2, c=c: lambda h: h.dma_start(
                out=wo_sb.ap()[:, c * 1024:(c + 1) * 1024], in_=src2))(),
                writes=[f"wo_sb{c}"], tag=f"woL{c}{tg}")

        TBKEYS = ["tb_aq", "tb_bq", "tb_ak", "tb_bk"]

        def emit_tables(t):
            tb = t % 2
            for k, key in enumerate(TBKEYS):
                src_ap = Dtab[key][:, t * DH:(t + 1) * DH]
                off = tb * 256 + k * 64
                vop("spd", (lambda src_ap=src_ap, off=off: lambda h: h.dma_start(
                    out=tabs_raw.ap()[:, off:off + DH], in_=src_ap))(),
                    writes=[f"tabs_raw{tb}_{k}"], tag=f"tabld{t}_{k}{tg}")
                ssb = qs_sb if k < 2 else ks_sb
                skey = "qs_sb" if k < 2 else "ks_sb"
                if k % 2 == 0:
                    sap = ssb.ap()
                else:
                    sap = bass.AP(ssb, 32, [[DH, 128], [-32, 2], [1, 32]])
                vop("dve", (lambda off=off, sap=sap: lambda h: h.tensor_mul(
                    tabs.ap()[:, off:off + DH].rearrange(
                        "p (s d) -> p s d", s=2),
                    tabs_raw.ap()[:, off:off + DH].rearrange(
                        "p (s d) -> p s d", s=2),
                    sap if sap.ndim == 3 else
                    sap.rearrange("p (s d) -> p s d", s=2)))(),
                    reads=[f"tabs_raw{tb}_{k}", skey], writes=[f"tabs{tb}_{k}"],
                    tag=f"tabmul{t}_{k}{tg}")

        def emit_cast(t):
            buf = t % 2
            t0 = t * 128
            src_ap = bass.AP(u, t0, [[CH * TC, 128], [TC, CH], [1, 129]])
            dst = bass.AP(ubf, buf * CH * 129, [[2 * CH * 129, 128], [129, CH],
                                                [1, 129]])
            rkeys = ["uzero"] + [f"u{c}_{tt}" for c in range(CH)
                                 for tt in ([t - 1, t] if t > 0 else [t])]
            vop("dve", (lambda dst=dst, src_ap=src_ap: lambda h: h.tensor_copy(
                dst, src_ap))(), reads=rkeys, writes=[f"ubf{buf}"],
                tag=f"ucast{t}{tg}")

        def ubf_slice(t, c):
            buf = t % 2
            off = 0 if c >= CH // 2 else 1
            base = buf * CH * 129 + c * 129 + off
            return ubf.ap()[:, base:base + 128]

        def emit_qkv_round(t, r):
            buf = t % 2
            for c in range(CH):
                def mm(h, t=t, r=r, c=c):
                    last = None
                    for j in range(2):
                        jt = 2 * r + j
                        last = h.matmul(
                            PS[f"Q{j}"].ap(), ubf_slice(t, c),
                            wqkv_sb.ap()[:, c * 3072 + jt * 512:
                                         c * 3072 + (jt + 1) * 512],
                            start=(c == 0), stop=(c == CH - 1))
                    return last
                vop("pe", mm, reads=[f"ubf{buf}", f"wqkv{c}"],
                    writes=["psQ0", "psQ1"], tag=f"qkv{t}r{r}c{c}{tg}")

        def process_qk(t, which):
            tb = t % 2
            ka, kb = ("tabs%d_0" % tb, "tabs%d_1" % tb) if which == "q" else                      ("tabs%d_2" % tb, "tabs%d_3" % tb)
            aoff = tb * 256 + (0 if which == "q" else 2) * 64
            boff = aoff + 64
            for j in range(2):
                vop("act", (lambda j=j: lambda h: h.copy(
                    P0.ap()[:, j * 512:(j + 1) * 512], PS[f"Q{j}"].ap()))(),
                    reads=[f"psQ{j}"], writes=["T_P0", f"evq{j}"],
                    tag=f"ev{which}{t}_{j}{tg}")
            vop("act", lambda h: h.activation(P1.ap(), P0.ap(), AF.Square),
                reads=["T_P0", "evq0", "evq1"], writes=["T_P1"],
                tag=f"sq{which}{t}{tg}")
            vop("dve", lambda h: h.reduce_sum(
                n2.ap()[:, 0:16],
                P1.ap().rearrange("p (hh d) -> p hh d", d=DH), axis=AX.X),
                reads=["T_P1"], writes=["n2"], tag=f"red{which}{t}{tg}")
            vop("act", lambda h: h.activation(rn.ap()[:, 16:32],
                                              n2.ap()[:, 0:16], AF.Ln),
                reads=["n2"], writes=["rnt"], tag=f"ln{which}{t}{tg}")
            mh = cval(-0.5)
            vop("act", lambda h: h.activation(rn.ap()[:, 0:16],
                                              rn.ap()[:, 16:32], AF.Exp,
                                              scale=mh),
                reads=["rnt", "cpool"], writes=["rn"], tag=f"ex{which}{t}{tg}")
            vop("dve", lambda h: h.tensor_scalar_min(rn.ap()[:, 0:16],
                                                     rn.ap()[:, 0:16], 1e12),
                reads=["rn"], writes=["rn"], tag=f"cl{which}{t}{tg}")
            vop("dve", lambda h: h.tensor_tensor(
                P1.ap().rearrange("p (hh d) -> p hh d", d=DH),
                P0.ap().rearrange("p (hh d) -> p hh d", d=DH),
                rn.ap()[:, 0:16].unsqueeze(2).broadcast_to([128, 16, DH]),
                ALU.mult),
                reads=["T_P0", "rn"], writes=["T_P1"], tag=f"qn{which}{t}{tg}")
            vop("dve", (lambda aoff=aoff: lambda h: h.tensor_tensor(
                P0.ap().rearrange("p (hh d) -> p hh d", d=DH),
                P1.ap().rearrange("p (hh d) -> p hh d", d=DH),
                tabs.ap()[:, aoff:aoff + DH].unsqueeze(1)
                    .broadcast_to([128, 16, DH]),
                ALU.mult))(),
                reads=["T_P1", ka], writes=["T_P0"], tag=f"t1{which}{t}{tg}")
            swap_ap = bass.AP(P1, 32, [[1024, 128], [64, 16], [-32, 2], [1, 32]])
            vop("dve", (lambda boff=boff, swap_ap=swap_ap: lambda h:
                        h.tensor_tensor(
                P2.ap().rearrange("p (hh s d) -> p hh s d", s=2, d=32),
                swap_ap,
                tabs.ap()[:, boff:boff + DH]
                    .rearrange("p (s d) -> p s d", s=2).unsqueeze(1)
                    .broadcast_to([128, 16, 2, 32]),
                ALU.mult))(),
                reads=["T_P1", kb], writes=["T_P2"], tag=f"t2{which}{t}{tg}")
            vop("dve", lambda h: h.tensor_add(rot_bf.ap(), P0.ap(), P2.ap()),
                reads=["T_P0", "T_P2"], writes=["rot_bf"],
                tag=f"rot{which}{t}{tg}")
            def tp(h):
                last = None
                for c in range(CH):
                    last = h.transpose(PS["TB"].ap()[:, c * 128:(c + 1) * 128],
                                       rot_bf.ap()[:, c * 128:(c + 1) * 128],
                                       ident.ap())
                return last
            vop("pe", tp, reads=["rot_bf", "c_ident"], writes=["psTB"],
                tag=f"tp{which}{t}{tg}")
            if which == "q":
                dst, dkey = qT.ap(), "qTb"
            else:
                slot = t % 3
                dst, dkey = kT_ring.ap()[:, slot * 1024:(slot + 1) * 1024],                     f"kT{slot}"
            vop("act", (lambda dst=dst: lambda h: h.copy(
                dst, PS["TB"].ap()))(),
                reads=["psTB"], writes=[dkey], tag=f"tpe{which}{t}{tg}")

        def evac_v(t):
            slot = t % 3
            for j in range(2):
                vop("act", (lambda j=j, slot=slot: lambda h: h.copy(
                    v_ring.ap()[:, slot * 1024 + j * 512:
                                slot * 1024 + (j + 1) * 512],
                    PS[f"Q{j}"].ap()))(),
                    reads=[f"psQ{j}"], writes=[f"v{slot}"],
                    tag=f"evv{t}_{j}{tg}")

        def emit_head(t, hd):
            kts = [kt for kt in (t - 2, t - 1, t) if kt >= 0]
            w = len(kts) * 128
            sbank = "S0" if hd % 2 == 0 else "S1"
            ab = hd % 2
            hp = (hd % 2) * 64
            cc = hd // 2
            rk = [f"kT{kt % 3}" for kt in kts] + ["qTb"]

            def sim(h, t=t, kts=kts, sbank=sbank, hp=hp, cc=cc):
                last = None
                for idx, kt in enumerate(kts):
                    masked = (kt == t - 2) or (kt == t)
                    last = h.matmul(
                        PS[sbank].ap()[:, idx * 128:(idx + 1) * 128],
                        qT.ap()[hp:hp + 64, cc * 128:(cc + 1) * 128],
                        kT_ring.ap()[hp:hp + 64,
                                     (kt % 3) * 1024 + cc * 128:
                                     (kt % 3) * 1024 + (cc + 1) * 128],
                        start=True, stop=not masked)
                    if masked:
                        m = m_up if kt == t - 2 else m_lo
                        last = h.matmul(
                            PS[sbank].ap()[:, idx * 128:(idx + 1) * 128],
                            ident.ap(), m.ap(), start=False, stop=True)
                return last
            vop("pe", sim, reads=rk + ["c_ident", "c_m_up", "c_m_lo"],
                writes=[f"ps{sbank}"], tag=f"sim{t}h{hd}{tg}")
            vop("dve", (lambda w=w, sbank=sbank, hd=hd: lambda h: h.reduce_max(
                negmax.ap()[:, hd:hd + 1], PS[sbank].ap()[:, 0:w], axis=AX.X,
                negate=True))(),
                reads=[f"ps{sbank}"], writes=[f"nm{hd}"], tag=f"mx{t}h{hd}{tg}")
            vop("act", (lambda w=w, sbank=sbank, hd=hd, ab=ab: lambda h:
                        h.activation(
                attn_bf.ap()[:, ab * 384:ab * 384 + w], PS[sbank].ap()[:, 0:w],
                AF.Exp, bias=negmax.ap()[:, hd:hd + 1],
                accum_out=den.ap()[:, hd:hd + 1]))(),
                reads=[f"ps{sbank}", f"nm{hd}"],
                writes=[f"attn{ab}", f"den{hd}"], tag=f"exp{t}h{hd}{tg}")
            vop("dve", (lambda hd=hd: lambda h: h.reciprocal(
                rden.ap()[:, hd:hd + 1], den.ap()[:, hd:hd + 1]))(),
                reads=[f"den{hd}"], writes=[f"rden{hd}"], tag=f"rc{t}h{hd}{tg}")
            vop("dve", (lambda w=w, hd=hd, ab=ab: lambda h: h.tensor_scalar_mul(
                attn_bf.ap()[:, ab * 384:ab * 384 + w],
                attn_bf.ap()[:, ab * 384:ab * 384 + w],
                rden.ap()[:, hd:hd + 1]))(),
                reads=[f"attn{ab}", f"rden{hd}"], writes=[f"attn{ab}"],
                tag=f"dv{t}h{hd}{tg}")
            def tpa(h, kts=kts, ab=ab):
                last = None
                for idx in range(len(kts)):
                    last = h.transpose(
                        PS["TB"].ap().bitcast(BF16)[:, idx * 128:(idx + 1) * 128]
                        if False else
                        PS["TB"].ap()[:, idx * 128:(idx + 1) * 128],
                        attn_bf.ap()[:, ab * 384 + idx * 128:
                                     ab * 384 + (idx + 1) * 128],
                        ident.ap())
                return last
            vop("pe", tpa, reads=[f"attn{ab}", "c_ident"], writes=["psTB"],
                tag=f"tpa{t}h{hd}{tg}")
            vop("act", (lambda w=w, ab=ab: lambda h: h.copy(
                attnT_bf.ap()[:, ab * 384:ab * 384 + w],
                PS["TB"].ap()[:, 0:w]))(),
                reads=["psTB"], writes=[f"attnT{ab}"], tag=f"tpe{t}h{hd}{tg}")
            obank = "O0" if cc < 4 else "O1"
            oreg = (cc % 4) * 128

            def av(h, kts=kts, hd=hd, hp=hp, ab=ab, obank=obank, oreg=oreg):
                last = None
                for idx, kt in enumerate(kts):
                    last = h.matmul(
                        PS[obank].ap()[hp:hp + 64, oreg:oreg + 128],
                        v_ring.ap()[:, (kt % 3) * 1024 + hd * 64:
                                    (kt % 3) * 1024 + hd * 64 + 64],
                        attnT_bf.ap()[:, ab * 384 + idx * 128:
                                      ab * 384 + (idx + 1) * 128],
                        start=(idx == 0), stop=(idx == len(kts) - 1),
                        tile_position=(0, 64) if hp else None)
                return last
            vop("pe", av,
                reads=[f"attnT{ab}"] + [f"v{kt % 3}" for kt in kts],
                writes=[f"ps{obank}"], tag=f"av{t}h{hd}{tg}")
            if hd % 2 == 1 and cc % 4 == 3:
                qq = t % 2
                c0 = cc - 3
                vop("dve", (lambda qq=qq, c0=c0, obank=obank: lambda h:
                            h.tensor_copy(
                    out2_bf.ap()[:, qq * 1024 + c0 * 128:
                                 qq * 1024 + (c0 + 4) * 128],
                    PS[obank].ap()))(),
                    reads=[f"ps{obank}"],
                    writes=[f"o2b{qq}_{c}" for c in range(c0, c0 + 4)],
                    tag=f"oev{t}h{hd}{tg}")

        def emit_wo(t):
            for oj in range(CH):
                bank = "Q2" if oj % 2 == 0 else "S0"
                rhs3 = out2_bf.ap().rearrange("p (qq cc q) -> p qq cc q",
                                              qq=2, cc=CH)

                def mm(h, oj=oj, bank=bank, rhs3=rhs3):
                    last = None
                    for cc in range(CH):
                        last = h.matmul(
                            PS[bank].ap()[:, 0:256],
                            wo_sb.ap()[:, cc * 1024 + oj * 128:
                                       cc * 1024 + (oj + 1) * 128],
                            rhs3[:, :, cc, :],
                            start=(cc == 0), stop=(cc == CH - 1))
                    return last
                vop("pe", mm,
                    reads=[f"o2b{q}_{c}" for q in range(2) for c in range(CH)] +
                          [f"wo_sb{c}" for c in range(CH)],
                    writes=[f"ps{bank}"], tag=f"wo{t}_{oj}{tg}")
                tcol = (t - 1) * 128
                ukeys = [f"u{oj}_{t - 1}", f"u{oj}_{t}"]
                vop("dve", (lambda oj=oj, bank=bank, tcol=tcol: lambda h:
                            h.tensor_add(
                    u.ap()[:, oj * TC + 1 + tcol: oj * TC + 1 + tcol + 256],
                    u.ap()[:, oj * TC + 1 + tcol: oj * TC + 1 + tcol + 256],
                    PS[bank].ap()[:, 0:256]))(),
                    reads=[f"ps{bank}"] + ukeys, writes=ukeys,
                    tag=f"wores{t}_{oj}{tg}")

        for t in range(NT):
            emit_tables(t)
            if t == 0:
                emit_cast(0)
            emit_qkv_round(t, 0)
            process_qk(t, "q")
            emit_qkv_round(t, 1)
            process_qk(t, "k")
            emit_qkv_round(t, 2)
            evac_v(t)
            for hd in range(H):
                emit_head(t, hd)
            if t + 1 < NT:
                emit_cast(t + 1)
            if t % 2 == 1:
                emit_wo(t)

    def emit_outer(b):
        tg = f"_o{b}"
        NSL_ = 4

        def wT_load(sl, jg):
            slot = jg % 2
            src_ap = bass.AP(DwaT.tensor, DwaT.offset + jg * 512,
                             [[3072, 128], [128 * 3072, CH], [1, 512]])
            vop("gpd", (lambda src_ap=src_ap, slot=slot: lambda h: h.dma_start(
                out=owT_sb.ap()[:, slot * CH * 512:(slot + 1) * CH * 512]
                .rearrange("p (c j) -> p c j", c=CH), in_=src_ap))(),
                writes=[f"owT{slot}"], tag=f"owT{sl}_{jg}{tg}")

        def qk_chunk(sl, jg, jj):
            slot = jg % 2
            bank = "Q0" if jj % 2 == 0 else "Q1"
            jc_glob = jg * 4 + jj

            def mm(h, slot=slot, bank=bank, jj=jj):
                last = None
                for c in range(CH):
                    last = h.matmul(
                        PS[bank].ap(),
                        owT_sb.ap()[:, slot * CH * 512 + c * 512 + jj * 128:
                                    slot * CH * 512 + c * 512 + (jj + 1) * 128],
                        olnx_sb.ap()[:, c * 512:(c + 1) * 512],
                        start=(c == 0), stop=(c == CH - 1))
                return last
            vop("pe", mm, reads=[f"owT{slot}", "olnx"], writes=[f"ps{bank}"],
                tag=f"oqk{sl}_{jc_glob}{tg}")
            vop("act", (lambda bank=bank: lambda h: h.activation(
                P0.ap()[:, 0:512], PS[bank].ap(), AF.Exp))(),
                reads=[f"ps{bank}"], writes=["T_P0"], tag=f"oexp{sl}_{jc_glob}{tg}")
            vop("dve", (lambda bank=bank: lambda h: h.tensor_scalar_add(
                P1.ap()[:, 0:512], PS[bank].ap(), 1.0))(),
                reads=[f"ps{bank}"], writes=["T_P1"], tag=f"ox1{sl}_{jc_glob}{tg}")
            if jc_glob < 8:
                dst, dkey = oqT_sb, f"oqT{jc_glob}"
                doff = jc_glob * 512
            else:
                dst, dkey = okT_sb, f"okT{jc_glob - 8}"
                doff = (jc_glob - 8) * 512
            vop("dve", (lambda dst=dst, doff=doff: lambda h: h.tensor_tensor(
                dst.ap()[:, doff:doff + 512], P0.ap()[:, 0:512],
                P1.ap()[:, 0:512], ALU.min))(),
                reads=["T_P0", "T_P1"], writes=[dkey], tag=f"omin{sl}_{jc_glob}{tg}")

        def v_tile(sl, tt):
            for g in range(2):
                jg = 4 + g
                slot = jg % 2
                bank = "Q0" if g == 0 else "Q1"

                def mm(h, slot=slot, bank=bank, tt=tt):
                    last = None
                    for c in range(CH):
                        last = h.matmul(
                            PS[bank].ap(),
                            olnx_sb.ap()[:, c * 512 + tt * 128:
                                         c * 512 + (tt + 1) * 128],
                            owT_sb.ap()[:, slot * CH * 512 + c * 512:
                                        slot * CH * 512 + (c + 1) * 512],
                            start=(c == 0), stop=(c == CH - 1))
                    return last
                vop("pe", mm, reads=[f"owT{slot}", "olnx"],
                    writes=[f"ps{bank}"], tag=f"ov{sl}_{tt}_{g}{tg}")
                dst = bass.AP(ovaug_sb, tt * H * 68 + g * 8 * 68,
                              [[4 * H * 68, 128], [68, 8], [1, 64]])
                vop("dve", (lambda dst=dst, bank=bank: lambda h: h.tensor_copy(
                    dst, PS[bank].ap().rearrange("p (hh d) -> p hh d", d=64)))(),
                    reads=[f"ps{bank}"], writes=[f"ovau{tt}_{g}"],
                    tag=f"ovev{sl}_{tt}_{g}{tg}")

        def ktok_tile(sl, tt):
            def tp(h, tt=tt):
                last = None
                for c in range(CH):
                    last = h.transpose(
                        PS["TB"].ap()[:, c * 128:(c + 1) * 128],
                        okT_sb.ap()[:, c * 512 + tt * 128:c * 512 + (tt + 1) * 128],
                        ident.ap())
                return last
            vop("pe", tp, reads=[f"okT{c}" for c in range(CH)] + ["c_ident"],
                writes=["psTB"], tag=f"oktp{sl}_{tt}{tg}")
            vop("dve", (lambda tt=tt: lambda h: h.tensor_copy(
                oktok_sb.ap()[:, tt * 1024:(tt + 1) * 1024], PS["TB"].ap()))(),
                reads=["psTB"], writes=[f"oktok{tt}"], tag=f"okte{sl}_{tt}{tg}")

        SCB = ["Q0", "Q1", "Q2", "S0"]
        YB = ["S1", "O0", "O1"]

        def scan_ctile(ct):
            sl, tt = ct // 4, ct % 4
            WAVES = int(os.environ.get('KOWAVE', '3'))
            for hd in range(H):
                bank = SCB[hd % 4]
                reg = (hd // 4) * 128
                hp = (hd % 2) * 64
                cc = hd // 2

                def mm(h, bank=bank, reg=reg, hp=hp, cc=cc, tt=tt):
                    return h.matmul(
                        PS[bank].ap()[:, reg:reg + 128],
                        okT_sb.ap()[hp:hp + 64, cc * 512 + tt * 128:
                                    cc * 512 + (tt + 1) * 128],
                        oqT_sb.ap()[hp:hp + 64, cc * 512 + tt * 128:
                                    cc * 512 + (tt + 1) * 128],
                        start=True, stop=True)
                vop("pe", mm,
                    reads=[f"okT{cc}", f"oqT{cc}"], writes=[f"ps{bank}"],
                    tag=f"osc{ct}h{hd}{tg}")
            if WAVES < 1:
                return
            for bi, bank in enumerate(SCB):
                def msk(h, bank=bank, bi=bi):
                    last = None
                    for r in range(4):
                        last = h.tensor_tensor(
                            oscm_sb.ap()[:, bi * 512 + r * 128:
                                         bi * 512 + (r + 1) * 128],
                            PS[bank].ap()[:, r * 128:(r + 1) * 128],
                            maskT.ap(), ALU.mult)
                    return last
                vop("dve", msk, reads=[f"ps{bank}", "c_maskT"],
                    writes=[f"oscm{bi}"], tag=f"omask{ct}_{bi}{tg}")
            if WAVES < 2:
                return
            for hd in range(H):
                bi = hd % 4
                yb = YB[min(hd // 7, 2)]
                yreg = (hd - 7 * min(hd // 7, 2)) * 66
                hp = (hd % 2) * 64
                cc = hd // 2

                def mm(h, hd=hd, bi=bi, yb=yb, yreg=yreg, hp=hp, cc=cc,
                       tt=tt, ct=ct):
                    last = h.matmul(
                        PS[yb].ap()[:, yreg:yreg + 65],
                        oscm_sb.ap()[:, bi * 512 + (hd // 4) * 128:
                                     bi * 512 + (hd // 4) * 128 + 128],
                        ovaug_sb.ap()[:, tt * H * 68 + hd * 68:
                                      tt * H * 68 + hd * 68 + 65],
                        start=True, stop=(ct == 0))
                    if ct > 0:
                        g = hd // 8
                        slot = (hd // 2) % 4
                        last = h.matmul(
                            PS[yb].ap()[:, yreg:yreg + 65],
                            oqT_sb.ap()[hp:hp + 64, cc * 512 + tt * 128:
                                        cc * 512 + (tt + 1) * 128],
                            oSbf.ap()[hp:hp + 64, g * 272 + slot * 68:
                                      g * 272 + slot * 68 + 65],
                            start=False, stop=True)
                    return last
                vop("pe", mm,
                    reads=[f"oscm{bi}", f"ovau{tt}_{hd // 8}", f"oqT{cc}",
                           "oSbf"],
                    writes=[f"ps{yb}"], tag=f"oy{ct}h{hd}{tg}")
            for yi, yb in enumerate(YB):
                nh = (7, 7, 2)[yi]
                h0 = (0, 7, 14)[yi]
                dap = bass.AP(PS[yb], 64, [[512, 128], [66, nh]])
                vop("dve", (lambda dap=dap, h0=h0, nh=nh: lambda h:
                            h.tensor_scalar_add(
                    den.ap()[:, h0:h0 + nh], dap, 1e-6))(),
                    reads=[f"ps{yb}"],
                    writes=[f"oden{yi}"], tag=f"oden{ct}_{yi}{tg}")
                vop("dve", (lambda h0=h0, nh=nh: lambda h: h.reciprocal(
                    rden.ap()[:, h0:h0 + nh], den.ap()[:, h0:h0 + nh]))(),
                    reads=[f"oden{yi}"], writes=[f"orden{yi}"],
                    tag=f"orc{ct}_{yi}{tg}")
                nap = bass.AP(PS[yb], 0, [[512, 128], [66, nh], [1, 64]])
                vop("dve", (lambda nap=nap, h0=h0, nh=nh, tt=tt, yi=yi:
                            lambda h: h.tensor_tensor(
                    oy_sb.ap()[:, tt * 1024 + h0 * 64:
                               tt * 1024 + (h0 + nh) * 64]
                    .rearrange("p (hh d) -> p hh d", d=64),
                    nap,
                    rden.ap()[:, h0:h0 + nh].unsqueeze(2)
                    .broadcast_to([128, nh, 64]), ALU.mult))(),
                    reads=[f"orden{yi}", f"ps{yb}"],
                    writes=[f"oyb{tt}_{yi}"], tag=f"oyev{ct}_{yi}{tg}")
            if WAVES < 3:
                return
            for hd in range(H):
                g = hd // 8
                bank = "Q0" if g == 0 else "Q1"
                slot = (hd % 8) // 2
                hp = (hd % 2) * 64

                def mm(h, hd=hd, bank=bank, slot=slot, hp=hp, tt=tt):
                    return h.matmul(
                        PS[bank].ap()[hp:hp + 64, slot * 66:slot * 66 + 65],
                        oktok_sb.ap()[:, tt * 1024 + hd * 64:
                                      tt * 1024 + hd * 64 + 64],
                        ovaug_sb.ap()[:, tt * H * 68 + hd * 68:
                                      tt * H * 68 + hd * 68 + 65],
                        start=True, stop=True,
                        tile_position=(0, 64) if hp else None)
                vop("pe", mm,
                    reads=[f"oktok{tt}", f"ovau{tt}_{hd // 8}"],
                    writes=[f"ps{bank}"], tag=f"oSd{ct}h{hd}{tg}")
            for g in range(2):
                bank = "Q0" if g == 0 else "Q1"
                rd = [f"ps{bank}"]
                if ct == 0:
                    vop("dve", (lambda bank=bank, g=g: lambda h: h.tensor_copy(
                        oS.ap()[:, g * 264:(g + 1) * 264],
                        PS[bank].ap()[:, 0:264]))(),
                        reads=rd, writes=[f"oS{g}"], tag=f"oSi{ct}_{g}{tg}")
                else:
                    vop("dve", (lambda bank=bank, g=g: lambda h: h.tensor_add(
                        oS.ap()[:, g * 264:(g + 1) * 264],
                        oS.ap()[:, g * 264:(g + 1) * 264],
                        PS[bank].ap()[:, 0:264]))(),
                        reads=rd + [f"oS{g}"], writes=[f"oS{g}"],
                        tag=f"oSa{ct}_{g}{tg}")
                vop("dve", (lambda g=g: lambda h: h.tensor_copy(
                    oSbf.ap()[:, g * 272:(g + 1) * 272]
                    .rearrange("p (s d) -> p s d", s=4)[:, :, 0:66],
                    oS.ap()[:, g * 264:(g + 1) * 264]
                    .rearrange("p (s d) -> p s d", s=4)))(),
                    reads=[f"oS{g}"], writes=["oSbf"], tag=f"oSc{ct}_{g}{tg}")

        def wpT_load(oj):
            slot = oj % 2
            src_ap = bass.AP(DwpT.tensor, DwpT.offset + oj * 128,
                             [[1024, 128], [128 * 1024, CH], [1, 128]])
            vop("gpd", (lambda src_ap=src_ap, slot=slot: lambda h: h.dma_start(
                out=owpT_sb.ap()[:, slot * 1024:(slot + 1) * 1024]
                .rearrange("p (c j) -> p c j", c=CH), in_=src_ap))(),
                writes=[f"owpT{slot}"], tag=f"owpT{oj}{tg}")

        def proj_ctile(ct):
            sl, tt = ct // 4, ct % 4
            def tp(h, tt=tt):
                last = None
                for c in range(CH):
                    last = h.transpose(
                        PS["TB"].ap()[:, c * 128:(c + 1) * 128],
                        oy_sb.ap()[:, tt * 1024 + c * 128:
                                   tt * 1024 + (c + 1) * 128],
                        ident.ap())
                return last
            vop("pe", tp, reads=[f"oyb{tt}_{yi}" for yi in range(3)] +
                ["c_ident"], writes=["psTB"], tag=f"oytp{ct}{tg}")
            vop("dve", lambda h: h.tensor_copy(oyT_sb.ap(), PS["TB"].ap()),
                reads=["psTB"], writes=["oyT"], tag=f"oyte{ct}{tg}")
            for oj in range(CH):
                slot = oj % 2
                wpT_load(oj)
                bank = "S1" if oj % 2 == 0 else "O0"

                def mm(h, oj=oj, slot=slot, bank=bank):
                    last = None
                    for c in range(CH):
                        last = h.matmul(
                            PS[bank].ap()[:, 0:128],
                            owpT_sb.ap()[:, slot * 1024 + c * 128:
                                         slot * 1024 + (c + 1) * 128],
                            oyT_sb.ap()[:, c * 128:(c + 1) * 128],
                            start=(c == 0), stop=(c == CH - 1))
                    return last
                vop("pe", mm, reads=[f"owpT{slot}", "oyT"],
                    writes=[f"ps{bank}"], tag=f"oprj{ct}_{oj}{tg}")
                tcol = ct * 128
                ukeys = [f"u{oj}_{ct}"]
                vop("dve", (lambda oj=oj, bank=bank, tcol=tcol: lambda h:
                            h.tensor_add(
                    u.ap()[:, oj * TC + 1 + tcol: oj * TC + 1 + tcol + 128],
                    u.ap()[:, oj * TC + 1 + tcol: oj * TC + 1 + tcol + 128],
                    PS[bank].ap()[:, 0:128]))(),
                    reads=[f"ps{bank}"] + ukeys, writes=ukeys,
                    tag=f"ores{ct}_{oj}{tg}")

        phase("O")
        oneap = bass.AP(ovaug_sb, 64, [[4 * H * 68, 128], [68, 4 * H]])
        vop("dve", lambda h: h.memset(oneap, 1.0), writes=["ovau_ones"],
            tag=f"ovones{tg}")
        OSUB = int(os.environ.get("KOSUB", "4"))
        for sl in range(NSL_):
            emit_ln("lnx", sl, shifted=False, tag=f"_ol{b}_{sl}")
            for jg in range(4):
                wT_load(sl, jg)
                if OSUB < 1:
                    continue
                for jj in range(4):
                    qk_chunk(sl, jg, jj)
            if OSUB < 2:
                continue
            for g in range(2):
                wT_load(sl, 4 + g)
            for tt in range(4):
                v_tile(sl, tt)
                ktok_tile(sl, tt)
            if OSUB < 3:
                continue
            for tt in range(4):
                ct = sl * 4 + tt
                scan_ctile(ct)
                if OSUB >= 4:
                    proj_ctile(ct)

    def emit_final(b, add_spill=True):
        phase("FIN")
        for sl in range(NSL):
            t0 = sl * 512
            src = Dspill[b].rearrange("(c p) t -> p c t", p=128)[:, :, t0:t0 + 512]
            if add_spill:
                vop("spd", (lambda src=src: lambda h: h.dma_start(out=scr.ap(), in_=src))(),
                    reads=["spill"], writes=["scr"], tag=f"fin_ld{sl}")
            ukeys = [f"u{c}_{sl*4+q}" for c in range(CH) for q in range(4)]
            def add(h, t0=t0):
                last = None
                for c in range(CH):
                    last = h.tensor_add(
                        u.ap()[:, c * TC + 1 + t0: c * TC + 1 + t0 + 512],
                        u.ap()[:, c * TC + 1 + t0: c * TC + 1 + t0 + 512],
                        scr.ap()[:, c * 512:(c + 1) * 512])
                return last
            if add_spill:
                vop("dve", add, reads=["scr"] + ukeys, writes=ukeys, tag=f"fin_add{sl}")
            dst = Dout[b].rearrange("(c p) t -> p c t", p=128)[:, :, t0:t0 + 512]
            srcu = bass.AP(u, 1 + t0, [[CH * TC, 128], [TC, CH], [1, 512]])
            vop("spd", (lambda dst=dst, srcu=srcu: lambda h: h.dma_start(
                out=dst, in_=srcu))(),
                reads=ukeys, writes=["outdma"], tag=f"fin_st{sl}")

    def emit_spill(b):
        ukeys = [f"u{c}_{t}" for c in range(CH) for t in range(NT)]
        dst = Dspill[b].rearrange("(c p) t -> p c t", p=128)
        srcu = bass.AP(u, 1, [[CH * TC, 128], [TC, CH], [1, T]])
        vop("spd", lambda h: h.dma_start(out=dst, in_=srcu),
            reads=ukeys, writes=["spill"], tag="spill")

    load_consts()
    for b in range(npass):
        load_x(b)
        if stage == 2:
            for sl in range(NSL):
                emit_ln("u", sl, shifted=False, tag=f"_o{b}_{sl}")
        if stage == 30:
            emit_layer_consts(0)
            emit_ffn(0, b)
        if stage == 40:
            emit_layer_consts(0)
            emit_attn(0, b)
        if stage == 50:
            for l in range(nlay):
                emit_layer_consts(l)
                emit_attn(l, b)
                emit_ffn(l, b)
        if stage == 60:
            emit_outer(b)
        if stage >= 99:
            emit_outer(b)
            emit_spill(b)
            for sl in range(NSL):
                emit_ln("u", sl, shifted=False, tag=f"_pi{b}_{sl}")
            for l in range(nlay):
                emit_layer_consts(l)
                emit_attn(l, b)
                emit_ffn(l, b)
        emit_final(b, add_spill=(stage >= 99))

    globals()['LAST_S'] = S
    globals()['LAST_PS'] = PS
    vp.emit()
    return nc


_BUILD_CACHE = {}


def kernel(**inputs):
    key = (STAGE, NPASS, NLAY)
    if key not in _BUILD_CACHE:
        _BUILD_CACHE[key] = build(NPASS, NLAY, STAGE)
    nc = _BUILD_CACHE[key]
    hl = host_layout(inputs)
    hc = host_consts()
    in_maps = []
    for c in range(NCORE):
        m = dict(
            x_t=hl["x_t"][c * BPC:(c + 1) * BPC],
            w_attn_T=hl["w_attn_T"], w_proj_T=hl["w_proj_T"],
            wqkv=hl["wqkv"], wo=hl["wo"], w1=hl["w1"], w2=hl["w2"],
            qs=hl["qs"], ks=hl["ks"], gb=hl["gb"],
            b1a=hl["b1a"], b1g=hl["b1g"], b2f=hl["b2f"],
            tb_aq=hc["tb_aq"], tb_bq=hc["tb_bq"],
            tb_ak=hc["tb_ak"], tb_bk=hc["tb_bk"],
            m_up=hc["m_up"], m_lo=hc["m_lo"], maskT=hc["maskT"],
            ident=hc["ident"],
        )
        in_maps.append(m)
    res = run_bass_kernel_spmd(nc, in_maps, core_ids=list(range(NCORE)))
    outs = [np.asarray(r["out_t"], np.float32) for r in res.results]
    full = np.concatenate(outs, axis=0)
    return np.ascontiguousarray(full.transpose(0, 2, 1))


# version 07 (final)
# speedup vs baseline: 1.5142x; 1.0242x over previous
import os
import numpy as np
import ml_dtypes
import concourse.bass as bass
import concourse.mybir as mybir
from concourse.bass_utils import run_bass_kernel_spmd

F32 = mybir.dt.float32
BF16 = mybir.dt.bfloat16
AF = mybir.ActivationFunctionType
ALU = mybir.AluOpType
AX = mybir.AxisListType

B_TOT, T, DM = 32, 2048, 1024
NCORE = 8
BPC = B_TOT // NCORE
CH = DM // 128
TC = T + 1
NT = T // 128
NL = 6
H, DH = 16, 64
INNER = H * DH
DI = 2730
NJC = 22
JCS = [128] * 21 + [42]
QK_SCALE = 8.0
NEG = -1e30

STAGE = int(os.environ.get("KSTAGE", "99"))
NPASS = int(os.environ.get("KNPASS", str(BPC)))
NLAY = int(os.environ.get("KNLAY", str(NL)))


def host_consts():
    c = {}
    half = np.arange(0, DH, 2, dtype=np.float64)
    inv_freq = 1.0 / 10000.0 ** (half / DH)
    t = np.arange(T, dtype=np.float64)
    freqs = t[:, None] * inv_freq[None, :]
    freqs = np.concatenate([freqs, freqs], -1)
    base_scale = (half + 0.4 * DH) / (1.4 * DH)
    power = (t - T // 2) / (256 // 2)
    scale = base_scale[None, :] ** power[:, None]
    scale = np.concatenate([scale, scale], -1)
    cos, sin = np.cos(freqs), np.sin(freqs)
    sgn = np.concatenate([-np.ones(DH // 2), np.ones(DH // 2)])[None, :]

    def tokmaj(a):
        return np.ascontiguousarray(
            a.reshape(NT, 128, DH).transpose(1, 0, 2).reshape(128, NT * DH)
        ).astype(np.float32)

    c["tb_aq"] = tokmaj(cos * scale * QK_SCALE)
    c["tb_bq"] = tokmaj(sin * scale * sgn * QK_SCALE)
    c["tb_ak"] = tokmaj(cos / scale)
    c["tb_bk"] = tokmaj(sin / scale * sgn)
    i = np.arange(128)[:, None]
    j = np.arange(128)[None, :]
    c["m_up"] = np.where(j >= i, 0.0, NEG).astype(ml_dtypes.bfloat16)
    c["m_lo"] = np.where(j <= i, 0.0, NEG).astype(ml_dtypes.bfloat16)
    c["maskT"] = (j >= i).astype(np.float32)
    c["ident"] = np.eye(128, dtype=ml_dtypes.bfloat16)
    return c


def host_layout(inp):
    d = {}
    d["x_t"] = np.ascontiguousarray(
        np.asarray(inp["L0_attn_x"], np.float32).transpose(0, 2, 1))
    d["w_attn_T"] = np.ascontiguousarray(
        np.asarray(inp["L0_attn_w_attn"], np.float32).T)
    d["w_proj_T"] = np.ascontiguousarray(
        np.asarray(inp["L0_attn_w_proj"], np.float32).T)
    d["wqkv"] = np.ascontiguousarray(np.asarray(inp["L0_ffn_Wqkv"], np.float32))
    d["wo"] = np.ascontiguousarray(np.asarray(inp["L0_ffn_Wo"], np.float32))
    d["w1"] = np.ascontiguousarray(np.asarray(inp["L0_ffn_W1"], np.float32))
    d["w2"] = np.ascontiguousarray(np.asarray(inp["L0_ffn_W2"], np.float32))
    d["qs"] = np.ascontiguousarray(np.broadcast_to(
        np.asarray(inp["L0_ffn_q_scale"], np.float32)[:, None, :], (NL, 128, DH)))
    d["ks"] = np.ascontiguousarray(np.broadcast_to(
        np.asarray(inp["L0_ffn_k_scale"], np.float32)[:, None, :], (NL, 128, DH)))
    g = np.asarray(inp["L0_ffn_ln_g"], np.float32).reshape(NL, CH, 128)
    b = np.asarray(inp["L0_ffn_ln_b"], np.float32).reshape(NL, CH, 128)
    d["gb"] = np.ascontiguousarray(np.stack([g, b], axis=2))
    b1 = np.asarray(inp["L0_ffn_b1"], np.float32)
    b1a = np.zeros((NL, NJC, 128), np.float32)
    b1g = np.zeros((NL, NJC, 128), np.float32)
    for l in range(NL):
        b1a[l].reshape(-1)[:DI] = b1[l, :DI]
        b1g[l].reshape(-1)[:DI] = b1[l, DI:]
    d["b1a"] = np.ascontiguousarray(b1a.transpose(0, 2, 1))
    d["b1g"] = np.ascontiguousarray(b1g.transpose(0, 2, 1))
    d["b2f"] = np.ascontiguousarray(
        np.asarray(inp["L0_ffn_b2"], np.float32).reshape(NL, CH, 128).transpose(0, 2, 1))
    return d


class VP:

    CSEMS = ("pe", "act", "dve", "gpc")
    QUEUES = ("spd", "gpd")
    NLANES = 8
    BODY = {"pe": "pe", "act": "act", "dve": "dve", "gpc": "gp",
            "spd": "sp", "gpd": "gp"}

    @classmethod
    def body_of_sem(cls, sname):
        for base in ("spd", "gpd", "gpc", "pe", "act", "dve"):
            if sname.startswith(base):
                return cls.BODY[base]
        raise KeyError(sname)

    SEM_LIMIT = 20000

    def __init__(self, nc):
        self.nc = nc
        self.sem = {}
        self.gen = {s: 0 for s in self.CSEMS}
        for s in self.CSEMS:
            self.sem[s] = nc.alloc_semaphore(f"vp_{s}")
        for q in self.QUEUES:
            for i in range(self.NLANES):
                self.sem[f"{q}{i}"] = nc.alloc_semaphore(f"vp_{q}{i}")
        self.cnt = {s: 0 for s in self.sem}
        self.rr = {q: 0 for q in self.QUEUES}
        self.ops = {b: [] for b in ("pe", "act", "dve", "gp", "sp")}
        self.last_w = {}
        self.readers = {}
        self.known = {b: {s: 0 for s in self.sem} for b in self.ops}
        self.fences = {}

    def op(self, sem, fn, reads=(), writes=(), tag=""):
        body = self.BODY[sem]
        is_dma = sem in self.QUEUES
        if is_dma:
            queue = sem
            lane = self.rr[queue]
            self.rr[queue] = (lane + 1) % self.NLANES
            sname = f"{queue}{lane}"
            amt = 16
        else:
            queue = None
            base = sem
            sname = base if self.gen[base] == 0 else f"{base}g{self.gen[base]}"
            if self.cnt.get(sname, 0) + 1 > self.SEM_LIMIT:
                self.gen[base] += 1
                sname = f"{base}g{self.gen[base]}"
                self.sem[sname] = self.nc.alloc_semaphore(f"vp_{sname}")
                self.cnt[sname] = 0
                for b in self.known:
                    self.known[b][sname] = 0
            amt = 1
        deps = {}

        def add(d):
            if d is None:
                return
            s, v = d
            if deps.get(s, 0) < v:
                deps[s] = v

        for k in reads:
            add(self.last_w.get(k))
            if k.startswith("ps"):
                for r in self.readers.get(k, ()):
                    if self.body_of_sem(r[0]) != body:
                        add(r)
        for k in writes:
            add(self.last_w.get(k))
            for r in self.readers.get(k, ()):
                add(r)
            for pfx, fd in self.fences.items():
                if k.startswith(pfx):
                    for s, v in fd.items():
                        add((s, v))
        if is_dma and self.cnt[sname] > 0:
            deps[sname] = max(deps.get(sname, 0), self.cnt[sname])
        waits = []
        for s, v in sorted(deps.items()):
            if sem == "pe" and s.startswith("pe"):
                continue
            if self.known[body][s] >= v:
                continue
            waits.append((s, v))
            self.known[body][s] = v
        self.cnt[sname] += amt
        me = (sname, self.cnt[sname])
        self.ops[body].append((waits, fn, sname, amt, tag))
        for k in reads:
            self.readers.setdefault(k, []).append(me)
        for k in writes:
            self.last_w[k] = me
            self.readers[k] = []
        return me

    def fence(self, from_prefixes, to_prefixes):
        deps = {}
        for k, lw in self.last_w.items():
            if any(k.startswith(p) for p in from_prefixes):
                s, v = lw
                deps[s] = max(deps.get(s, 0), v)
        for k, rd in self.readers.items():
            if any(k.startswith(p) for p in from_prefixes):
                for s, v in rd:
                    deps[s] = max(deps.get(s, 0), v)
        for p in to_prefixes:
            self.fences[p] = dict(deps)

    def emit(self):
        nc = self.nc
        finals = [(s, self.cnt[s]) for s in self.sem
                  if any(s.startswith(q) for q in self.QUEUES) and self.cnt[s]]
        with nc.Block() as block:
            def mk(body):
                def run(h):
                    for waits, fn, sname, amt, tag in self.ops[body]:
                        for s, v in waits:
                            h.wait_ge(self.sem[s], v)
                        inst = fn(h)
                        if inst is not None:
                            inst.then_inc(self.sem[sname], amt)
                    if body == "sp":
                        for s, v in finals:
                            h.wait_ge(self.sem[s], v)
                return run
            block.tensor(mk("pe"))
            block.scalar(mk("act"))
            block.vector(mk("dve"))
            block.gpsimd(mk("gp"))
            block.sync(mk("sp"))


def build(npass, nlay, stage):
    nc = bass.Bass("TRN2", target_bir_lowering=False)
    vp = VP(nc)

    Dx = nc.dram_tensor("x_t", [BPC, DM, T], F32, kind="ExternalInput").ap()
    Dout = nc.dram_tensor("out_t", [BPC, DM, T], F32, kind="ExternalOutput").ap()
    Dspill = nc.dram_tensor("x2_spill", [BPC, DM, T], F32, kind="Internal").ap()
    DwaT = nc.dram_tensor("w_attn_T", [DM, 3 * DM], F32, kind="ExternalInput").ap()
    DwpT = nc.dram_tensor("w_proj_T", [DM, DM], F32, kind="ExternalInput").ap()
    Dwqkv = nc.dram_tensor("wqkv", [NL, DM, 3 * INNER], F32, kind="ExternalInput").ap()
    Dwo = nc.dram_tensor("wo", [NL, INNER, DM], F32, kind="ExternalInput").ap()
    Dw1 = nc.dram_tensor("w1", [NL, DM, 2 * DI], F32, kind="ExternalInput").ap()
    Dw2 = nc.dram_tensor("w2", [NL, DI, DM], F32, kind="ExternalInput").ap()
    Dqs = nc.dram_tensor("qs", [NL, 128, DH], F32, kind="ExternalInput").ap()
    Dks = nc.dram_tensor("ks", [NL, 128, DH], F32, kind="ExternalInput").ap()
    Dgb = nc.dram_tensor("gb", [NL, CH, 2, 128], F32, kind="ExternalInput").ap()
    Db1a = nc.dram_tensor("b1a", [NL, 128, NJC], F32, kind="ExternalInput").ap()
    Db1g = nc.dram_tensor("b1g", [NL, 128, NJC], F32, kind="ExternalInput").ap()
    Db2f = nc.dram_tensor("b2f", [NL, 128, CH], F32, kind="ExternalInput").ap()
    Dtab = {k: nc.dram_tensor(k, [128, NT * DH], F32, kind="ExternalInput").ap()
            for k in ("tb_aq", "tb_bq", "tb_ak", "tb_bk")}
    Dmup = nc.dram_tensor("m_up", [128, 128], BF16, kind="ExternalInput").ap()
    Dmlo = nc.dram_tensor("m_lo", [128, 128], BF16, kind="ExternalInput").ap()
    DmT = nc.dram_tensor("maskT", [128, 128], F32, kind="ExternalInput").ap()
    Dident = nc.dram_tensor("ident", [128, 128], BF16, kind="ExternalInput").ap()

    S = {}

    def sb(name, shape, dt):
        S[name] = nc.alloc_sbuf_tensor(name, shape, dt)
        return S[name]

    u = sb("u", [128, CH * TC], F32)
    m_up = sb("c_m_up", [128, 128], BF16)
    m_lo = sb("c_m_lo", [128, 128], BF16)
    maskT = sb("c_maskT", [128, 128], F32)
    ident = sb("c_ident", [128, 128], BF16)
    ones_bf = sb("ones_bf", [128, 1], BF16)
    ones_f1 = sb("ones_f1", [1, 640], F32)
    gb_sb = sb("gb_sb", [2, CH * 128], F32)
    b1a_sb = sb("b1a_sb", [128, NJC], F32)
    b1g_sb = sb("b1g_sb", [128, NJC], F32)
    b2_sb = sb("b2_sb", [128, CH], F32)
    qs_sb = sb("qs_sb", [128, DH], F32)
    ks_sb = sb("ks_sb", [128, DH], F32)
    P0 = sb("P0", [128, 1024], F32)
    P1 = sb("P1", [128, 1024], F32)
    P2 = sb("P2", [128, 1024], F32)
    n2 = sb("n2", [128, 32], F32)
    rn = sb("rn", [128, 32], F32)
    negmax = sb("negmax", [128, 16], F32)
    den = sb("den", [128, 16], F32)
    rden = sb("rden", [128, 16], F32)
    kT_ring = sb("kT_ring", [128, 3 * 1024], BF16)
    v_ring = sb("v_ring", [128, 3 * 1024], BF16)
    out2_bf = sb("out2_bf", [128, 2 * 1024], BF16)
    lnbf = sb("lnbf", [128, 2 * 512], BF16)
    sqbf = sb("sqbf", [128, 2 * 512], BF16)
    rows = sb("rows", [1, 4 * 512], F32)
    r2neg = sb("r2neg", [2, 512], F32)
    cpool = sb("cpool", [128, 16], F32)
    cpool_vals = {}

    def cval(v):
        if v not in cpool_vals:
            i = len(cpool_vals)
            assert i < 16
            cpool_vals[v] = i
            vp.op("dve", (lambda i=i, v=v: lambda h: h.memset(
                cpool.ap()[:, i:i + 1], v))(), writes=["cpool"],
                tag=f"cset{i}")
        return cpool.ap()[:, cpool_vals[v]:cpool_vals[v] + 1]

    KB = 1024
    ARENA_B = 81 * KB
    arena = nc.alloc_sbuf_tensor("arena", [128, ARENA_B], mybir.dt.uint8)
    arena_addr = nc.lookup_mloc(arena).addr

    def ar(name, shape, dt, off):
        import math as _m
        per = 1
        for s in shape[1:]:
            per *= s
        per *= mybir.dt.size(dt)
        assert off + per <= ARENA_B, (name, off, per)
        hnd = nc.alloc_sbuf_tensor_at(name, shape, dt, offset=arena_addr + off)
        S[name] = hnd
        return hnd

    wqkv_sb = ar("wqkv_sb", [128, CH * 3072], BF16, 0)
    wo_sb = ar("wo_sb", [128, CH * 1024], BF16, 48 * KB)
    tabs = ar("tabs", [128, 2 * 4 * DH], F32, 64 * KB)
    tabs_raw = ar("tabs_raw", [128, 2 * 4 * DH], F32, 66 * KB)
    qT = ar("qT", [128, 1024], BF16, 68 * KB)
    rot_bf = ar("rot_bf", [128, 1024], BF16, 70 * KB)
    ubf = ar("ubf", [128, 2 * CH * 129], BF16, 76 * KB)
    attn_bf = sb("attn_bf", [128, 4 * 384], BF16)
    attnT_bf = sb("attnT_bf", [128, 4 * 384], BF16)
    xn_sb = ar("xn_sb", [128, CH * 1024], BF16, 0)
    hh_sb = ar("hh_sb", [128, NJC * 1024], BF16, 16 * KB)
    w1a_sb = ar("w1a_sb", [128, 2 * CH * 128], BF16, 60 * KB)
    w1g_sb = ar("w1g_sb", [128, 2 * CH * 128], BF16, 64 * KB)
    w2s_sb = ar("w2s_sb", [128, 2 * NJC * 128], BF16, 68 * KB)
    owT_sb = ar("owT_sb", [128, 2 * CH * 512], BF16, 0)
    olnx_sb = ar("olnx_sb", [128, CH * 512], BF16, 16 * KB)
    oqT_sb = ar("oqT_sb", [128, CH * 512], BF16, 24 * KB)
    okT_sb = ar("okT_sb", [128, CH * 512], BF16, 32 * KB)
    oktok_sb = ar("oktok_sb", [128, 4 * 1024], BF16, 40 * KB)
    ovaug_sb = ar("ovaug_sb", [128, 4 * H * 68], BF16, 48 * KB)
    oscm_sb = ar("oscm_sb", [128, 4 * 512], BF16, 57 * KB)
    oy_sb = ar("oy_sb", [128, 4 * 1024], BF16, 61 * KB)
    oyT_sb = ar("oyT_sb", [128, 1024], BF16, 69 * KB)
    oS = ar("oS", [128, 2 * 264], F32, 71 * KB + 256)
    oSbf = ar("oSbf", [128, 2 * 272], BF16, 74 * KB)
    owpT_sb = ar("owpT_sb", [128, 2 * 1024], BF16, 76 * KB)
    scr = ar("scr", [128, CH * 512], F32, 0)

    GROUPS = {
        "A": ("wqkv", "wo_sb", "tabs", "qT", "rot_bf", "attn_bf", "attnT_bf",
              "ubf"),
        "F": ("xn", "hh", "w1a", "w1g", "w2s"),
        "O": ("owT", "olnx", "oqT", "okT", "oktok", "ovaug", "oscm", "oy", "oS"),
        "FIN": ("scr",),
    }
    cur_group = ["FIN"]

    def phase(g):
        if cur_group[0] != g:
            vp.fence(GROUPS[cur_group[0]], GROUPS[g])
            cur_group[0] = g

    PS = {}

    def ps(name, shape, dt=F32):
        PS[name] = nc.alloc_psum_tensor(name, shape, dt)
        return PS[name]

    Q0 = ps("Q0", [128, 512]); Q1 = ps("Q1", [128, 512])
    _pb = nc.psum_base
    Q2 = ps("Q2", [128, 512])
    nc.psum_base = _pb
    TB2 = ps("TB2", [128, 1024], BF16)
    S0 = ps("S0", [128, 512]); S1 = ps("S1", [128, 512])
    O0 = ps("O0", [128, 512]); O1 = ps("O1", [128, 512])
    TB = ps("TB", [128, 1024], BF16)

    def vop(sem, fn, reads=(), writes=(), tag=""):
        return vp.op(sem, fn, reads=reads, writes=writes, tag=tag)

    def uap(c, col, n, f32=True):
        return u.ap()[:, c * TC + col: c * TC + col + n]

    def u_sh(c, tok, n):
        off = 0 if c >= CH // 2 else 1
        return u.ap()[:, c * TC + tok + off: c * TC + tok + off + n]

    NSL = 4

    def load_consts():
        vop("spd", lambda h: h.dma_start(out=m_up.ap(), in_=Dmup), writes=["c_m_up"])
        vop("spd", lambda h: h.dma_start(out=m_lo.ap(), in_=Dmlo), writes=["c_m_lo"])
        vop("spd", lambda h: h.dma_start(out=maskT.ap(), in_=DmT), writes=["c_maskT"])
        vop("spd", lambda h: h.dma_start(out=ident.ap(), in_=Dident), writes=["c_ident"])
        vop("dve", lambda h: h.memset(ones_bf.ap(), 1.0), writes=["ones_bf"])
        vop("dve", lambda h: h.memset(ones_f1.ap(), 1.0), writes=["ones_f1"])
        vop("dve", lambda h: h.memset(r2neg.ap(), -1.0), writes=["r2neg"])

    def load_x(b):
        src = Dx[b].rearrange("(c p) t -> p c t", p=128)
        dst = bass.AP(u, 1, [[CH * TC, 128], [TC, CH], [1, T]])
        vop("spd", lambda h: h.dma_start(out=dst, in_=src),
            writes=[f"u{c}_{t}" for c in range(CH) for t in range(NT)] + ["u"])
        zap = bass.AP(u, 0, [[CH * TC, 128], [TC, CH], [1, 1]])
        vop("dve", lambda h: h.memset(zap, 0.0), writes=["uzero"])

    ROW_R1, ROW_MEAN, ROW_TMP, ROW_VAR = 0, 1, 2, 3

    def row(k):
        return rows.ap()[0:1, k * 512:(k + 1) * 512]

    def emit_ln(dst_mode, sl, shifted, tag=""):
        t0 = sl * 512

        def ukey(c):
            lo = max(0, sl * 4 - (1 if shifted else 0))
            return [f"u{c}_{tt}" for tt in range(lo, sl * 4 + 4)]

        allkeys = [k for c in range(CH) for k in ukey(c)]

        for c in range(CH):
            off = 0 if (shifted and c >= CH // 2) else 1
            buf = c % 2
            src_ap = u.ap()[:, c * TC + t0 + off: c * TC + t0 + off + 512]
            vop("dve", (lambda src_ap=src_ap, buf=buf: lambda h: h.tensor_copy(
                lnbf.ap()[:, buf * 512:(buf + 1) * 512], src_ap))(),
                reads=ukey(c) + ["uzero"], writes=[f"lnbf{buf}"],
                tag=f"lncast{c}{tag}")
            vop("act", (lambda buf=buf: lambda h: h.activation(
                sqbf.ap()[:, buf * 512:(buf + 1) * 512],
                lnbf.ap()[:, buf * 512:(buf + 1) * 512], AF.Square))(),
                reads=[f"lnbf{buf}"], writes=[f"sqbf{buf}"], tag=f"lnsq{c}{tag}")
            vop("pe", (lambda c=c, buf=buf: lambda h: h.matmul(
                PS["S0"].ap()[0:1, :], ones_bf.ap(),
                lnbf.ap()[:, buf * 512:(buf + 1) * 512],
                start=(c == 0), stop=(c == CH - 1)))(),
                reads=[f"lnbf{buf}", "ones_bf"], writes=["psS0"],
                tag=f"lns1_{c}{tag}")
            vop("pe", (lambda c=c, buf=buf: lambda h: h.matmul(
                PS["S1"].ap()[0:1, :], ones_bf.ap(),
                sqbf.ap()[:, buf * 512:(buf + 1) * 512],
                start=(c == 0), stop=(c == CH - 1)))(),
                reads=[f"sqbf{buf}", "ones_bf"], writes=["psS1"],
                tag=f"lns2_{c}{tag}")
        vop("dve", lambda h: h.tensor_scalar_mul(
            row(ROW_MEAN), PS["S0"].ap()[0:1, :], 1.0 / DM),
            reads=["psS0"], writes=["rows_mean"], tag=f"lnmean{tag}")
        vop("act", lambda h: h.activation(row(ROW_TMP), row(ROW_MEAN), AF.Square),
            reads=["rows_mean"], writes=["rows_tmp"], tag=f"lnm2{tag}")
        vop("dve", lambda h: h.scalar_tensor_tensor(
            row(ROW_VAR), PS["S1"].ap()[0:1, :], 1.0 / DM,
            row(ROW_TMP), ALU.mult, ALU.subtract),
            reads=["psS1", "rows_tmp"], writes=["rows_var"], tag=f"lnvar{tag}")
        eps_ap = cval(1e-5)
        vop("act", lambda h: h.activation(row(ROW_TMP), row(ROW_VAR), AF.Ln,
                                          bias=eps_ap[0:1, :]),
            reads=["rows_var", "cpool"], writes=["rows_tmp"], tag=f"lnln{tag}")
        mhalf_ap = cval(-0.5)
        vop("act", lambda h: h.activation(row(ROW_R1), row(ROW_TMP), AF.Exp,
                                          scale=mhalf_ap[0:1, :]),
            reads=["rows_tmp", "cpool"], writes=["rows_r1"], tag=f"lnrstd{tag}")
        vop("dve", lambda h: h.tensor_mul(r2neg.ap()[0:1, :], row(ROW_MEAN),
                                          row(ROW_R1)),
            reads=["rows_r1", "rows_mean"], writes=["r2neg0"], tag=f"lnr2{tag}")
        vop("pe", lambda h: h.matmul(PS["O0"].ap(), ones_f1.ap()[0:1, 0:128],
                                     row(ROW_R1), start=True, stop=True),
            reads=["ones_f1", "rows_r1"], writes=["psO0"], tag=f"lnbcr1{tag}")
        vop("pe", lambda h: h.matmul(PS["O1"].ap(), ones_f1.ap()[0:1, 0:128],
                                     r2neg.ap()[0:1, :], start=True, stop=True),
            reads=["ones_f1", "r2neg0"], writes=["psO1"], tag=f"lnbcr2{tag}")
        for c in range(CH):
            off = 0 if (shifted and c >= CH // 2) else 1
            src_ap = u.ap()[:, c * TC + t0 + off: c * TC + t0 + off + 512]
            vop("dve", (lambda src_ap=src_ap: lambda h: h.tensor_mul(
                P0.ap()[:, 0:512], src_ap, PS["O0"].ap()))(),
                reads=ukey(c) + ["psO0", "uzero", "T_P0"], writes=["lnP0", "T_P0"],
                tag=f"lnc1_{c}{tag}")
            if dst_mode == "u":
                dst = u.ap()[:, c * TC + t0 + 1: c * TC + t0 + 1 + 512]
                dkey = [f"u{c}_{sl*4+q}" for q in range(4)]
            else:
                dst = olnx_sb.ap()[:, c * 512:(c + 1) * 512]
                dkey = ["olnx"]
            vop("dve", (lambda dst=dst: lambda h: h.tensor_sub(
                dst, P0.ap()[:, 0:512], PS["O1"].ap()))(),
                reads=["lnP0", "psO1", "T_P0"], writes=dkey, tag=f"lnc2_{c}{tag}")

    KDBG = os.environ.get("KDBG", "") == "1"
    NDBG = 16
    dbg_i = [0]
    if KDBG:
        Ddbg = nc.dram_tensor("dbg", [NDBG, 128, 2048], F32,
                              kind="ExternalOutput").ap()

    def dbg(ap, reads, note=""):
        if not KDBG or dbg_i[0] >= NDBG:
            return
        i = dbg_i[0]
        dbg_i[0] += 1
        p, f = ap.shape[0], ap.free_size()
        if ap.dtype == BF16:
            ap = ap.bitcast(F32)
            f = f // 2
        if f > 2048:
            f = 2048
            ap = ap[:, 0:2048]
        dst = Ddbg[i][0:p, 0:f]
        vop("spd", (lambda dst=dst, ap=ap: lambda h: h.dma_start(
            out=dst, in_=ap))(), reads=list(reads), writes=[f"dbg{i}"],
            tag=f"dbg{i}_{note}")
        print(f"DBG[{i}] {note} shape=({p},{f})")

    def emit_layer_consts(l):
        src_gb = Dgb[l].transpose([1, 0, 2])
        vop("spd", (lambda src_gb=src_gb: lambda h: h.dma_start(
            out=gb_sb.ap(), in_=src_gb))(), writes=["gb_sb"], tag=f"Lc_gb{l}")
        vop("spd", (lambda l=l: lambda h: h.dma_start(
            out=b1a_sb.ap(), in_=Db1a[l]))(), writes=["b1a_sb"], tag=f"Lc_b1a{l}")
        vop("spd", (lambda l=l: lambda h: h.dma_start(
            out=b1g_sb.ap(), in_=Db1g[l]))(), writes=["b1g_sb"], tag=f"Lc_b1g{l}")
        vop("spd", (lambda l=l: lambda h: h.dma_start(
            out=b2_sb.ap(), in_=Db2f[l]))(), writes=["b2_sb"], tag=f"Lc_b2{l}")
        vop("spd", (lambda l=l: lambda h: h.dma_start(
            out=qs_sb.ap(), in_=Dqs[l]))(), writes=["qs_sb"], tag=f"Lc_qs{l}")
        vop("spd", (lambda l=l: lambda h: h.dma_start(
            out=ks_sb.ap(), in_=Dks[l]))(), writes=["ks_sb"], tag=f"Lc_ks{l}")

    def emit_ln_xn(l, sl, tag=""):
        t0 = sl * 512
        base_in_pass = (sl % 2) * 512

        def ukey(c):
            lo = max(0, sl * 4 - 1)
            return [f"u{c}_{tt}" for tt in range(lo, sl * 4 + 4)]

        for c in range(CH):
            off = 0 if c >= CH // 2 else 1
            buf = c % 2
            src_ap = u.ap()[:, c * TC + t0 + off: c * TC + t0 + off + 512]
            vop("dve", (lambda src_ap=src_ap, buf=buf: lambda h: h.tensor_copy(
                lnbf.ap()[:, buf * 512:(buf + 1) * 512], src_ap))(),
                reads=ukey(c) + ["uzero"], writes=[f"lnbf{buf}"],
                tag=f"xcast{c}{tag}")
            vop("act", (lambda buf=buf: lambda h: h.activation(
                sqbf.ap()[:, buf * 512:(buf + 1) * 512],
                lnbf.ap()[:, buf * 512:(buf + 1) * 512], AF.Square))(),
                reads=[f"lnbf{buf}"], writes=[f"sqbf{buf}"], tag=f"xsq{c}{tag}")
            vop("pe", (lambda c=c, buf=buf: lambda h: h.matmul(
                PS["S0"].ap()[0:1, :], ones_bf.ap(),
                lnbf.ap()[:, buf * 512:(buf + 1) * 512],
                start=(c == 0), stop=(c == CH - 1)))(),
                reads=[f"lnbf{buf}", "ones_bf"], writes=["psS0"],
                tag=f"xs1_{c}{tag}")
            vop("pe", (lambda c=c, buf=buf: lambda h: h.matmul(
                PS["S1"].ap()[0:1, :], ones_bf.ap(),
                sqbf.ap()[:, buf * 512:(buf + 1) * 512],
                start=(c == 0), stop=(c == CH - 1)))(),
                reads=[f"sqbf{buf}", "ones_bf"], writes=["psS1"],
                tag=f"xs2_{c}{tag}")
        vop("dve", lambda h: h.tensor_scalar_mul(
            row(ROW_MEAN), PS["S0"].ap()[0:1, :], 1.0 / DM),
            reads=["psS0"], writes=["rows_mean"], tag=f"xmean{tag}")
        vop("act", lambda h: h.activation(row(ROW_TMP), row(ROW_MEAN), AF.Square),
            reads=["rows_mean"], writes=["rows_tmp"], tag=f"xm2{tag}")
        vop("dve", lambda h: h.scalar_tensor_tensor(
            row(ROW_VAR), PS["S1"].ap()[0:1, :], 1.0 / DM,
            row(ROW_TMP), ALU.mult, ALU.subtract),
            reads=["psS1", "rows_tmp"], writes=["rows_var"], tag=f"xvar{tag}")
        eps_ap = cval(1e-5)
        vop("act", lambda h: h.activation(row(ROW_TMP), row(ROW_VAR), AF.Ln,
                                          bias=eps_ap[0:1, :]),
            reads=["rows_var", "cpool"], writes=["rows_tmp"], tag=f"xln{tag}")
        mh_ap = cval(-0.5)
        vop("act", lambda h: h.activation(row(ROW_R1), row(ROW_TMP), AF.Exp,
                                          scale=mh_ap[0:1, :]),
            reads=["rows_tmp", "cpool"], writes=["rows_r1"], tag=f"xrstd{tag}")
        vop("dve", lambda h: h.tensor_mul(r2neg.ap()[0:1, :], row(ROW_MEAN),
                                          row(ROW_R1)),
            reads=["rows_r1", "rows_mean"], writes=["r2neg0"], tag=f"xr2{tag}")
        for c in range(CH):
            off = 0 if c >= CH // 2 else 1
            vop("pe", (lambda c=c: lambda h: h.matmul(
                PS["O0"].ap(), gb_sb.ap()[0:1, c * 128:(c + 1) * 128],
                row(ROW_R1), start=True, stop=True))(),
                reads=["gb_sb", "rows_r1"], writes=["psO0"], tag=f"xbc1_{c}{tag}")
            vop("pe", (lambda c=c: lambda h: h.matmul(
                PS["O1"].ap(), gb_sb.ap()[0:2, c * 128:(c + 1) * 128],
                r2neg.ap(), start=True, stop=True))(),
                reads=["gb_sb", "r2neg0", "r2neg"], writes=["psO1"],
                tag=f"xbc2_{c}{tag}")
            src_ap = u.ap()[:, c * TC + t0 + off: c * TC + t0 + off + 512]
            vop("dve", (lambda src_ap=src_ap: lambda h: h.tensor_mul(
                P0.ap()[:, 0:512], src_ap, PS["O0"].ap()))(),
                reads=ukey(c) + ["psO0", "uzero", "T_P0"], writes=["lnP0", "T_P0"],
                tag=f"xc1_{c}{tag}")
            dst = xn_sb.ap()[:, c * 1024 + base_in_pass: c * 1024 + base_in_pass + 512]
            vop("dve", (lambda dst=dst: lambda h: h.tensor_sub(
                dst, P0.ap()[:, 0:512], PS["O1"].ap()))(),
                reads=["lnP0", "psO1", "T_P0"], writes=[f"xn{c}"], tag=f"xc2_{c}{tag}")

    def emit_ffn(l, b):
        phase("F")
        W1l, W2l = Dw1[l], Dw2[l]

        def w1_load(jc, ring):
            jck = JCS[jc]
            for kind, dst, col0 in (("a", w1a_sb, 0), ("g", w1g_sb, DI)):
                src = bass.AP(W1l.tensor, W1l.offset + col0 + jc * 128,
                              [[2 * DI, 128], [128 * 2 * DI, CH], [1, jck]])
                dstap = dst.ap()[:, ring * CH * 128:(ring + 1) * CH * 128].rearrange(
                    "p (c j) -> p c j", c=CH)[:, :, 0:jck]
                vop("gpd", (lambda dstap=dstap, src=src: lambda h: h.dma_start(
                    out=dstap, in_=src))(),
                    writes=[f"w1{kind}r{ring}"], tag=f"w1{kind}_{jc}")

        def w2_load(oj, ring):
            src = bass.AP(W2l.tensor, W2l.offset + oj * 128,
                          [[DM, 128], [128 * DM, 21], [1, 128]])
            dstap = w2s_sb.ap()[:, ring * NJC * 128:(ring + 1) * NJC * 128].rearrange(
                "p (c j) -> p c j", c=NJC)[:, 0:21, :]
            vop("gpd", (lambda dstap=dstap, src=src: lambda h: h.dma_start(
                out=dstap, in_=src))(), writes=[f"w2r{ring}"], tag=f"w2m_{oj}")
            src2 = bass.AP(W2l.tensor, W2l.offset + 21 * 128 * DM + oj * 128,
                           [[DM, 42], [1, 128]])
            dst2 = w2s_sb.ap()[0:42, ring * NJC * 128 + 21 * 128:
                               ring * NJC * 128 + 22 * 128]
            vop("gpd", (lambda dst2=dst2, src2=src2: lambda h: h.dma_start(
                out=dst2, in_=src2))(), writes=[f"w2r{ring}t"], tag=f"w2t_{oj}")

        def w1_phase(p):
            for jc in range(NJC):
                jck = JCS[jc]
                ring = jc % 2
                w1_load(jc, ring)
                for ts in range(2):
                    gbank = "Q0" if ts == 0 else "Q2"
                    abank = "Q1" if ts == 0 else "O1"
                    def gmm(h, jc=jc, jck=jck, ring=ring, ts=ts, gbank=gbank):
                        last = None
                        for c in range(CH):
                            last = h.matmul(
                                PS[gbank].ap()[0:jck, :],
                                w1g_sb.ap()[:, ring * CH * 128 + c * 128:
                                            ring * CH * 128 + c * 128 + jck],
                                xn_sb.ap()[:, c * 1024 + ts * 512:
                                           c * 1024 + ts * 512 + 512],
                                start=(c == 0), stop=(c == CH - 1))
                        return last
                    vop("pe", gmm,
                        reads=[f"w1gr{ring}"] + [f"xn{c}" for c in range(CH)],
                        writes=[f"ps{gbank}"], tag=f"gmm{jc}_{ts}")
                    vop("act", (lambda jc=jc, jck=jck, ts=ts, gbank=gbank:
                                lambda h: h.activation(
                        P0.ap()[0:jck, ts * 512:ts * 512 + 512],
                        PS[gbank].ap()[0:jck, :], AF.Gelu,
                        bias=b1g_sb.ap()[0:jck, jc:jc + 1]))(),
                        reads=[f"ps{gbank}", "b1g_sb", "T_P0"], writes=[f"gel{ts}", "T_P0"],
                        tag=f"gelu{jc}_{ts}")
                    def amm(h, jc=jc, jck=jck, ring=ring, ts=ts, abank=abank):
                        last = None
                        for c in range(CH):
                            last = h.matmul(
                                PS[abank].ap()[0:jck, :],
                                w1a_sb.ap()[:, ring * CH * 128 + c * 128:
                                            ring * CH * 128 + c * 128 + jck],
                                xn_sb.ap()[:, c * 1024 + ts * 512:
                                           c * 1024 + ts * 512 + 512],
                                start=(c == 0), stop=(c == CH - 1))
                        return last
                    vop("pe", amm,
                        reads=[f"w1ar{ring}"] + [f"xn{c}" for c in range(CH)],
                        writes=[f"ps{abank}"], tag=f"amm{jc}_{ts}")
                    vop("dve", (lambda jc=jc, jck=jck, ts=ts, abank=abank:
                                lambda h: h.scalar_tensor_tensor(
                        hh_sb.ap()[0:jck, jc * 1024 + ts * 512:
                                   jc * 1024 + ts * 512 + 512],
                        PS[abank].ap()[0:jck, :],
                        b1a_sb.ap()[0:jck, jc:jc + 1],
                        P0.ap()[0:jck, ts * 512:ts * 512 + 512],
                        ALU.add, ALU.mult))(),
                        reads=[f"ps{abank}", f"gel{ts}", "b1a_sb", "T_P0"],
                        writes=[f"hh{jc}"], tag=f"hmul{jc}_{ts}")

        def w2_phase(p):
            for oj in range(CH):
                ring = oj % 2
                w2_load(oj, ring)
                for ts in range(2):
                    bank = "S0" if (oj * 2 + ts) % 2 == 0 else "S1"
                    def omm(h, oj=oj, ring=ring, ts=ts, bank=bank):
                        last = None
                        for hc in range(NJC):
                            hck = JCS[hc]
                            last = h.matmul(
                                PS[bank].ap(),
                                w2s_sb.ap()[0:hck, ring * NJC * 128 + hc * 128:
                                            ring * NJC * 128 + (hc + 1) * 128],
                                hh_sb.ap()[0:hck, hc * 1024 + ts * 512:
                                           hc * 1024 + ts * 512 + 512],
                                start=(hc == 0), stop=(hc == NJC - 1))
                        return last
                    vop("pe", omm,
                        reads=[f"w2r{ring}", f"w2r{ring}t"] +
                              [f"hh{hc}" for hc in range(NJC)],
                        writes=[f"ps{bank}"], tag=f"omm{oj}_{ts}")
                    tcol = p * 1024 + ts * 512
                    ukeys = [f"u{oj}_{tcol // 128 + q}" for q in range(4)]
                    vop("dve", (lambda oj=oj, ts=ts, bank=bank, tcol=tcol:
                                lambda h: h.scalar_tensor_tensor(
                        u.ap()[:, oj * TC + 1 + tcol: oj * TC + 1 + tcol + 512],
                        PS[bank].ap(), b2_sb.ap()[:, oj:oj + 1],
                        u.ap()[:, oj * TC + 1 + tcol: oj * TC + 1 + tcol + 512],
                        ALU.add, ALU.add))(),
                        reads=[f"ps{bank}", "b2_sb"] + ukeys,
                        writes=ukeys, tag=f"resid{oj}_{ts}")

        emit_ln_xn(l, 0, tag=f"_f{b}l{l}s0")
        emit_ln_xn(l, 1, tag=f"_f{b}l{l}s1")
        w1_phase(0)
        emit_ln_xn(l, 2, tag=f"_f{b}l{l}s2")
        dbg(rows.ap(), ["rows_r1", "rows_mean", "rows_var"], "rows_s2")
        emit_ln_xn(l, 3, tag=f"_f{b}l{l}s3")
        for c in range(CH):
            dbg(xn_sb.ap()[:, c * 1024:(c + 1) * 1024], [f"xn{c}"], f"xn_p1_c{c}")
        w2_phase(0)
        w1_phase(1)
        for hc in (0, 1):
            dbg(hh_sb.ap()[:, hc * 1024:(hc + 1) * 1024], [f"hh{hc}"],
                f"hh_p1_{hc}")
        w2_phase(1)

    def emit_attn(l, b):
        phase("A")
        tg = f"_a{b}l{l}"
        for c in range(CH):
            src_ap = Dwqkv[l][c * 128:(c + 1) * 128, :]
            vop("gpd", (lambda src_ap=src_ap, c=c: lambda h: h.dma_start(
                out=wqkv_sb.ap()[:, c * 3072:(c + 1) * 3072], in_=src_ap))(),
                writes=[f"wqkv{c}"], tag=f"wqkvL{c}{tg}")
            src2 = Dwo[l][c * 128:(c + 1) * 128, :]
            vop("gpd", (lambda src2=src2, c=c: lambda h: h.dma_start(
                out=wo_sb.ap()[:, c * 1024:(c + 1) * 1024], in_=src2))(),
                writes=[f"wo_sb{c}"], tag=f"woL{c}{tg}")

        TBKEYS = ["tb_aq", "tb_bq", "tb_ak", "tb_bk"]

        def emit_tables(t):
            tb = t % 2
            for k, key in enumerate(TBKEYS):
                src_ap = Dtab[key][:, t * DH:(t + 1) * DH]
                off = tb * 256 + k * 64
                vop("spd", (lambda src_ap=src_ap, off=off: lambda h: h.dma_start(
                    out=tabs_raw.ap()[:, off:off + DH], in_=src_ap))(),
                    writes=[f"tabs_raw{tb}_{k}"], tag=f"tabld{t}_{k}{tg}")
                ssb = qs_sb if k < 2 else ks_sb
                skey = "qs_sb" if k < 2 else "ks_sb"
                if k % 2 == 0:
                    sap = ssb.ap()
                else:
                    sap = bass.AP(ssb, 32, [[DH, 128], [-32, 2], [1, 32]])
                vop("dve", (lambda off=off, sap=sap: lambda h: h.tensor_mul(
                    tabs.ap()[:, off:off + DH].rearrange(
                        "p (s d) -> p s d", s=2),
                    tabs_raw.ap()[:, off:off + DH].rearrange(
                        "p (s d) -> p s d", s=2),
                    sap if sap.ndim == 3 else
                    sap.rearrange("p (s d) -> p s d", s=2)))(),
                    reads=[f"tabs_raw{tb}_{k}", skey], writes=[f"tabs{tb}_{k}"],
                    tag=f"tabmul{t}_{k}{tg}")

        def emit_cast(t):
            buf = t % 2
            t0 = t * 128
            src_ap = bass.AP(u, t0, [[CH * TC, 128], [TC, CH], [1, 129]])
            dst = bass.AP(ubf, buf * CH * 129, [[2 * CH * 129, 128], [129, CH],
                                                [1, 129]])
            rkeys = ["uzero"] + [f"u{c}_{tt}" for c in range(CH)
                                 for tt in ([t - 1, t] if t > 0 else [t])]
            vop("dve", (lambda dst=dst, src_ap=src_ap: lambda h: h.tensor_copy(
                dst, src_ap))(), reads=rkeys, writes=[f"ubf{buf}"],
                tag=f"ucast{t}{tg}")

        def ubf_slice(t, c):
            buf = t % 2
            off = 0 if c >= CH // 2 else 1
            base = buf * CH * 129 + c * 129 + off
            return ubf.ap()[:, base:base + 128]

        def emit_qkv_round(t, r):
            buf = t % 2
            for c in range(CH):
                def mm(h, t=t, r=r, c=c):
                    last = None
                    for j in range(2):
                        jt = 2 * r + j
                        last = h.matmul(
                            PS[f"Q{j}"].ap(), ubf_slice(t, c),
                            wqkv_sb.ap()[:, c * 3072 + jt * 512:
                                         c * 3072 + (jt + 1) * 512],
                            start=(c == 0), stop=(c == CH - 1))
                    return last
                vop("pe", mm, reads=[f"ubf{buf}", f"wqkv{c}"],
                    writes=["psQ0", "psQ1"], tag=f"qkv{t}r{r}c{c}{tg}")

        def process_qk(t, which):
            tb = t % 2
            ka, kb = ("tabs%d_0" % tb, "tabs%d_1" % tb) if which == "q" else                      ("tabs%d_2" % tb, "tabs%d_3" % tb)
            aoff = tb * 256 + (0 if which == "q" else 2) * 64
            boff = aoff + 64
            for j in range(2):
                vop("act", (lambda j=j: lambda h: h.copy(
                    P0.ap()[:, j * 512:(j + 1) * 512], PS[f"Q{j}"].ap()))(),
                    reads=[f"psQ{j}"], writes=["T_P0", f"evq{j}"],
                    tag=f"ev{which}{t}_{j}{tg}")
            vop("act", lambda h: h.activation(P1.ap(), P0.ap(), AF.Square),
                reads=["T_P0", "evq0", "evq1"], writes=["T_P1"],
                tag=f"sq{which}{t}{tg}")
            vop("dve", lambda h: h.reduce_sum(
                n2.ap()[:, 0:16],
                P1.ap().rearrange("p (hh d) -> p hh d", d=DH), axis=AX.X),
                reads=["T_P1"], writes=["n2"], tag=f"red{which}{t}{tg}")
            vop("act", lambda h: h.activation(rn.ap()[:, 16:32],
                                              n2.ap()[:, 0:16], AF.Ln),
                reads=["n2"], writes=["rnt"], tag=f"ln{which}{t}{tg}")
            mh = cval(-0.5)
            vop("act", lambda h: h.activation(rn.ap()[:, 0:16],
                                              rn.ap()[:, 16:32], AF.Exp,
                                              scale=mh),
                reads=["rnt", "cpool"], writes=["rn"], tag=f"ex{which}{t}{tg}")
            vop("dve", lambda h: h.tensor_scalar_min(rn.ap()[:, 0:16],
                                                     rn.ap()[:, 0:16], 1e12),
                reads=["rn"], writes=["rn"], tag=f"cl{which}{t}{tg}")
            vop("dve", lambda h: h.tensor_tensor(
                P1.ap().rearrange("p (hh d) -> p hh d", d=DH),
                P0.ap().rearrange("p (hh d) -> p hh d", d=DH),
                rn.ap()[:, 0:16].unsqueeze(2).broadcast_to([128, 16, DH]),
                ALU.mult),
                reads=["T_P0", "rn"], writes=["T_P1"], tag=f"qn{which}{t}{tg}")
            vop("dve", (lambda aoff=aoff: lambda h: h.tensor_tensor(
                P0.ap().rearrange("p (hh d) -> p hh d", d=DH),
                P1.ap().rearrange("p (hh d) -> p hh d", d=DH),
                tabs.ap()[:, aoff:aoff + DH].unsqueeze(1)
                    .broadcast_to([128, 16, DH]),
                ALU.mult))(),
                reads=["T_P1", ka], writes=["T_P0"], tag=f"t1{which}{t}{tg}")
            swap_ap = bass.AP(P1, 32, [[1024, 128], [64, 16], [-32, 2], [1, 32]])
            vop("dve", (lambda boff=boff, swap_ap=swap_ap: lambda h:
                        h.tensor_tensor(
                P2.ap().rearrange("p (hh s d) -> p hh s d", s=2, d=32),
                swap_ap,
                tabs.ap()[:, boff:boff + DH]
                    .rearrange("p (s d) -> p s d", s=2).unsqueeze(1)
                    .broadcast_to([128, 16, 2, 32]),
                ALU.mult))(),
                reads=["T_P1", kb], writes=["T_P2"], tag=f"t2{which}{t}{tg}")
            vop("dve", lambda h: h.tensor_add(rot_bf.ap(), P0.ap(), P2.ap()),
                reads=["T_P0", "T_P2"], writes=["rot_bf"],
                tag=f"rot{which}{t}{tg}")
            def tp(h):
                last = None
                for c in range(CH):
                    last = h.transpose(PS["TB"].ap()[:, c * 128:(c + 1) * 128],
                                       rot_bf.ap()[:, c * 128:(c + 1) * 128],
                                       ident.ap())
                return last
            vop("pe", tp, reads=["rot_bf", "c_ident"], writes=["psTB"],
                tag=f"tp{which}{t}{tg}")
            if which == "q":
                dst, dkey = qT.ap(), "qTb"
            else:
                slot = t % 3
                dst, dkey = kT_ring.ap()[:, slot * 1024:(slot + 1) * 1024],                     f"kT{slot}"
            vop("dve", (lambda dst=dst: lambda h: h.tensor_copy(
                dst, PS["TB"].ap()))(),
                reads=["psTB"], writes=[dkey], tag=f"tpe{which}{t}{tg}")

        def evac_v(t):
            slot = t % 3
            for j in range(2):
                vop("dve", (lambda j=j, slot=slot: lambda h: h.tensor_copy(
                    v_ring.ap()[:, slot * 1024 + j * 512:
                                slot * 1024 + (j + 1) * 512],
                    PS[f"Q{j}"].ap()))(),
                    reads=[f"psQ{j}"], writes=[f"v{slot}"],
                    tag=f"evv{t}_{j}{tg}")

        def emit_head(t, hd):
            kts = [kt for kt in (t - 2, t - 1, t) if kt >= 0]
            w = len(kts) * 128
            sbank = "S0" if hd % 2 == 0 else "S1"
            ab = hd % 4
            hp = (hd % 2) * 64
            cc = hd // 2
            rk = [f"kT{kt % 3}" for kt in kts] + ["qTb"]

            def sim(h, t=t, kts=kts, sbank=sbank, hp=hp, cc=cc):
                last = None
                for idx, kt in enumerate(kts):
                    masked = (kt == t - 2) or (kt == t)
                    last = h.matmul(
                        PS[sbank].ap()[:, idx * 128:(idx + 1) * 128],
                        qT.ap()[hp:hp + 64, cc * 128:(cc + 1) * 128],
                        kT_ring.ap()[hp:hp + 64,
                                     (kt % 3) * 1024 + cc * 128:
                                     (kt % 3) * 1024 + (cc + 1) * 128],
                        start=True, stop=not masked)
                    if masked:
                        m = m_up if kt == t - 2 else m_lo
                        last = h.matmul(
                            PS[sbank].ap()[:, idx * 128:(idx + 1) * 128],
                            ident.ap(), m.ap(), start=False, stop=True)
                return last
            vop("pe", sim, reads=rk + ["c_ident", "c_m_up", "c_m_lo"],
                writes=[f"ps{sbank}"], tag=f"sim{t}h{hd}{tg}")
            vop("dve", (lambda w=w, sbank=sbank, hd=hd: lambda h: h.reduce_max(
                negmax.ap()[:, hd:hd + 1], PS[sbank].ap()[:, 0:w], axis=AX.X,
                negate=True))(),
                reads=[f"ps{sbank}"], writes=[f"nm{hd}"], tag=f"mx{t}h{hd}{tg}")
            vop("act", (lambda w=w, sbank=sbank, hd=hd, ab=ab: lambda h:
                        h.activation(
                attn_bf.ap()[:, ab * 384:ab * 384 + w], PS[sbank].ap()[:, 0:w],
                AF.Exp, bias=negmax.ap()[:, hd:hd + 1],
                accum_out=den.ap()[:, hd:hd + 1]))(),
                reads=[f"ps{sbank}", f"nm{hd}"],
                writes=[f"attn{ab}", f"den{hd}"], tag=f"exp{t}h{hd}{tg}")
            vop("dve", (lambda hd=hd: lambda h: h.reciprocal(
                rden.ap()[:, hd:hd + 1], den.ap()[:, hd:hd + 1]))(),
                reads=[f"den{hd}"], writes=[f"rden{hd}"], tag=f"rc{t}h{hd}{tg}")
            vop("dve", (lambda w=w, hd=hd, ab=ab: lambda h: h.tensor_scalar_mul(
                attn_bf.ap()[:, ab * 384:ab * 384 + w],
                attn_bf.ap()[:, ab * 384:ab * 384 + w],
                rden.ap()[:, hd:hd + 1]))(),
                reads=[f"attn{ab}", f"rden{hd}"], writes=[f"attn{ab}"],
                tag=f"dv{t}h{hd}{tg}")
            tb = "TB" if hd % 2 == 0 else "TB2"
            tbk = "psTB" if hd % 2 == 0 else "psQ2"

            def tpa(h, kts=kts, ab=ab, tb=tb):
                last = None
                for idx in range(len(kts)):
                    last = h.transpose(
                        PS[tb].ap()[:, idx * 128:(idx + 1) * 128],
                        attn_bf.ap()[:, ab * 384 + idx * 128:
                                     ab * 384 + (idx + 1) * 128],
                        ident.ap())
                return last
            vop("pe", tpa, reads=[f"attn{ab}", "c_ident"], writes=[tbk],
                tag=f"tpa{t}h{hd}{tg}")
            vop("dve", (lambda w=w, ab=ab, tb=tb: lambda h: h.tensor_copy(
                attnT_bf.ap()[:, ab * 384:ab * 384 + w],
                PS[tb].ap()[:, 0:w]))(),
                reads=[tbk], writes=[f"attnT{ab}"], tag=f"tpe{t}h{hd}{tg}")
            obank = "O0" if cc < 4 else "O1"
            oreg = (cc % 4) * 128

            def av(h, kts=kts, hd=hd, hp=hp, ab=ab, obank=obank, oreg=oreg):
                last = None
                for idx, kt in enumerate(kts):
                    last = h.matmul(
                        PS[obank].ap()[hp:hp + 64, oreg:oreg + 128],
                        v_ring.ap()[:, (kt % 3) * 1024 + hd * 64:
                                    (kt % 3) * 1024 + hd * 64 + 64],
                        attnT_bf.ap()[:, ab * 384 + idx * 128:
                                      ab * 384 + (idx + 1) * 128],
                        start=(idx == 0), stop=(idx == len(kts) - 1),
                        tile_position=(0, 64) if hp else None)
                return last
            vop("pe", av,
                reads=[f"attnT{ab}"] + [f"v{kt % 3}" for kt in kts],
                writes=[f"ps{obank}"], tag=f"av{t}h{hd}{tg}")
            if hd % 2 == 1:
                qq = t % 2
                vop("dve", (lambda qq=qq, cc=cc, obank=obank, oreg=oreg:
                            lambda h: h.tensor_copy(
                    out2_bf.ap()[:, qq * 1024 + cc * 128:
                                 qq * 1024 + (cc + 1) * 128],
                    PS[obank].ap()[:, oreg:oreg + 128]))(),
                    reads=[f"ps{obank}"],
                    writes=[f"o2b{qq}_{cc}"], tag=f"oev{t}h{hd}{tg}")

        def emit_wo(t):
            for oj in range(CH):
                bank = "O0" if oj % 2 == 0 else "O1"
                rhs3 = out2_bf.ap().rearrange("p (qq cc q) -> p qq cc q",
                                              qq=2, cc=CH)

                def mm(h, oj=oj, bank=bank, rhs3=rhs3):
                    last = None
                    for cc in range(CH):
                        last = h.matmul(
                            PS[bank].ap()[:, 0:256],
                            wo_sb.ap()[:, cc * 1024 + oj * 128:
                                       cc * 1024 + (oj + 1) * 128],
                            rhs3[:, :, cc, :],
                            start=(cc == 0), stop=(cc == CH - 1))
                    return last
                vop("pe", mm,
                    reads=[f"o2b{q}_{c}" for q in range(2) for c in range(CH)] +
                          [f"wo_sb{c}" for c in range(CH)],
                    writes=[f"ps{bank}"], tag=f"wo{t}_{oj}{tg}")
                tcol = (t - 1) * 128
                ukeys = [f"u{oj}_{t - 1}", f"u{oj}_{t}"]
                vop("dve", (lambda oj=oj, bank=bank, tcol=tcol: lambda h:
                            h.tensor_add(
                    u.ap()[:, oj * TC + 1 + tcol: oj * TC + 1 + tcol + 256],
                    u.ap()[:, oj * TC + 1 + tcol: oj * TC + 1 + tcol + 256],
                    PS[bank].ap()[:, 0:256]))(),
                    reads=[f"ps{bank}"] + ukeys, writes=ukeys,
                    tag=f"wores{t}_{oj}{tg}")

        for t in range(NT):
            emit_tables(t)
            if t == 0:
                emit_cast(0)
            emit_qkv_round(t, 0)
            process_qk(t, "q")
            emit_qkv_round(t, 1)
            process_qk(t, "k")
            emit_qkv_round(t, 2)
            evac_v(t)
            for hd in range(H):
                emit_head(t, hd)
            if t + 1 < NT:
                emit_cast(t + 1)
            if t % 2 == 1:
                emit_wo(t)

    def emit_outer(b):
        tg = f"_o{b}"
        NSL_ = 4

        def wT_load(sl, jg):
            slot = jg % 2
            src_ap = bass.AP(DwaT.tensor, DwaT.offset + jg * 512,
                             [[3072, 128], [128 * 3072, CH], [1, 512]])
            vop("gpd", (lambda src_ap=src_ap, slot=slot: lambda h: h.dma_start(
                out=owT_sb.ap()[:, slot * CH * 512:(slot + 1) * CH * 512]
                .rearrange("p (c j) -> p c j", c=CH), in_=src_ap))(),
                writes=[f"owT{slot}"], tag=f"owT{sl}_{jg}{tg}")

        def qk_chunk(sl, jg, jj):
            slot = jg % 2
            bank = "Q0" if jj % 2 == 0 else "Q1"
            jc_glob = jg * 4 + jj

            def mm(h, slot=slot, bank=bank, jj=jj):
                last = None
                for c in range(CH):
                    last = h.matmul(
                        PS[bank].ap(),
                        owT_sb.ap()[:, slot * CH * 512 + c * 512 + jj * 128:
                                    slot * CH * 512 + c * 512 + (jj + 1) * 128],
                        olnx_sb.ap()[:, c * 512:(c + 1) * 512],
                        start=(c == 0), stop=(c == CH - 1))
                return last
            vop("pe", mm, reads=[f"owT{slot}", "olnx"], writes=[f"ps{bank}"],
                tag=f"oqk{sl}_{jc_glob}{tg}")
            vop("act", (lambda bank=bank: lambda h: h.activation(
                P0.ap()[:, 0:512], PS[bank].ap(), AF.Exp))(),
                reads=[f"ps{bank}"], writes=["T_P0"], tag=f"oexp{sl}_{jc_glob}{tg}")
            vop("dve", (lambda bank=bank: lambda h: h.tensor_scalar_add(
                P1.ap()[:, 0:512], PS[bank].ap(), 1.0))(),
                reads=[f"ps{bank}"], writes=["T_P1"], tag=f"ox1{sl}_{jc_glob}{tg}")
            if jc_glob < 8:
                dst, dkey = oqT_sb, f"oqT{jc_glob}"
                doff = jc_glob * 512
            else:
                dst, dkey = okT_sb, f"okT{jc_glob - 8}"
                doff = (jc_glob - 8) * 512
            vop("dve", (lambda dst=dst, doff=doff: lambda h: h.tensor_tensor(
                dst.ap()[:, doff:doff + 512], P0.ap()[:, 0:512],
                P1.ap()[:, 0:512], ALU.min))(),
                reads=["T_P0", "T_P1"], writes=[dkey], tag=f"omin{sl}_{jc_glob}{tg}")

        def v_tile(sl, tt):
            for g in range(2):
                jg = 4 + g
                slot = jg % 2
                bank = "Q0" if g == 0 else "Q1"

                def mm(h, slot=slot, bank=bank, tt=tt):
                    last = None
                    for c in range(CH):
                        last = h.matmul(
                            PS[bank].ap(),
                            olnx_sb.ap()[:, c * 512 + tt * 128:
                                         c * 512 + (tt + 1) * 128],
                            owT_sb.ap()[:, slot * CH * 512 + c * 512:
                                        slot * CH * 512 + (c + 1) * 512],
                            start=(c == 0), stop=(c == CH - 1))
                    return last
                vop("pe", mm, reads=[f"owT{slot}", "olnx"],
                    writes=[f"ps{bank}"], tag=f"ov{sl}_{tt}_{g}{tg}")
                dst = bass.AP(ovaug_sb, tt * H * 68 + g * 8 * 68,
                              [[4 * H * 68, 128], [68, 8], [1, 64]])
                vop("dve", (lambda dst=dst, bank=bank: lambda h: h.tensor_copy(
                    dst, PS[bank].ap().rearrange("p (hh d) -> p hh d", d=64)))(),
                    reads=[f"ps{bank}"], writes=[f"ovau{tt}_{g}"],
                    tag=f"ovev{sl}_{tt}_{g}{tg}")

        def ktok_tile(sl, tt):
            def tp(h, tt=tt):
                last = None
                for c in range(CH):
                    last = h.transpose(
                        PS["TB"].ap()[:, c * 128:(c + 1) * 128],
                        okT_sb.ap()[:, c * 512 + tt * 128:c * 512 + (tt + 1) * 128],
                        ident.ap())
                return last
            vop("pe", tp, reads=[f"okT{c}" for c in range(CH)] + ["c_ident"],
                writes=["psTB"], tag=f"oktp{sl}_{tt}{tg}")
            vop("dve", (lambda tt=tt: lambda h: h.tensor_copy(
                oktok_sb.ap()[:, tt * 1024:(tt + 1) * 1024], PS["TB"].ap()))(),
                reads=["psTB"], writes=[f"oktok{tt}"], tag=f"okte{sl}_{tt}{tg}")

        SCB = ["Q0", "Q1", "Q2", "S0"]
        YB = ["S1", "O0", "O1"]

        def scan_ctile(ct):
            sl, tt = ct // 4, ct % 4
            WAVES = int(os.environ.get('KOWAVE', '3'))
            for hd in range(H):
                bank = SCB[hd % 4]
                reg = (hd // 4) * 128
                hp = (hd % 2) * 64
                cc = hd // 2

                def mm(h, bank=bank, reg=reg, hp=hp, cc=cc, tt=tt):
                    return h.matmul(
                        PS[bank].ap()[:, reg:reg + 128],
                        okT_sb.ap()[hp:hp + 64, cc * 512 + tt * 128:
                                    cc * 512 + (tt + 1) * 128],
                        oqT_sb.ap()[hp:hp + 64, cc * 512 + tt * 128:
                                    cc * 512 + (tt + 1) * 128],
                        start=True, stop=True)
                vop("pe", mm,
                    reads=[f"okT{cc}", f"oqT{cc}"], writes=[f"ps{bank}"],
                    tag=f"osc{ct}h{hd}{tg}")
            if WAVES < 1:
                return
            for bi, bank in enumerate(SCB):
                def msk(h, bank=bank, bi=bi):
                    last = None
                    for r in range(4):
                        last = h.tensor_tensor(
                            oscm_sb.ap()[:, bi * 512 + r * 128:
                                         bi * 512 + (r + 1) * 128],
                            PS[bank].ap()[:, r * 128:(r + 1) * 128],
                            maskT.ap(), ALU.mult)
                    return last
                vop("dve", msk, reads=[f"ps{bank}", "c_maskT"],
                    writes=[f"oscm{bi}"], tag=f"omask{ct}_{bi}{tg}")
            if WAVES < 2:
                return
            for hd in range(H):
                bi = hd % 4
                yb = YB[min(hd // 7, 2)]
                yreg = (hd - 7 * min(hd // 7, 2)) * 66
                hp = (hd % 2) * 64
                cc = hd // 2

                def mm(h, hd=hd, bi=bi, yb=yb, yreg=yreg, hp=hp, cc=cc,
                       tt=tt, ct=ct):
                    last = h.matmul(
                        PS[yb].ap()[:, yreg:yreg + 65],
                        oscm_sb.ap()[:, bi * 512 + (hd // 4) * 128:
                                     bi * 512 + (hd // 4) * 128 + 128],
                        ovaug_sb.ap()[:, tt * H * 68 + hd * 68:
                                      tt * H * 68 + hd * 68 + 65],
                        start=True, stop=(ct == 0))
                    if ct > 0:
                        g = hd // 8
                        slot = (hd // 2) % 4
                        last = h.matmul(
                            PS[yb].ap()[:, yreg:yreg + 65],
                            oqT_sb.ap()[hp:hp + 64, cc * 512 + tt * 128:
                                        cc * 512 + (tt + 1) * 128],
                            oSbf.ap()[hp:hp + 64, g * 272 + slot * 68:
                                      g * 272 + slot * 68 + 65],
                            start=False, stop=True)
                    return last
                vop("pe", mm,
                    reads=[f"oscm{bi}", f"ovau{tt}_{hd // 8}", f"oqT{cc}",
                           "oSbf"],
                    writes=[f"ps{yb}"], tag=f"oy{ct}h{hd}{tg}")
            for yi, yb in enumerate(YB):
                nh = (7, 7, 2)[yi]
                h0 = (0, 7, 14)[yi]
                dap = bass.AP(PS[yb], 64, [[512, 128], [66, nh]])
                vop("dve", (lambda dap=dap, h0=h0, nh=nh: lambda h:
                            h.tensor_scalar_add(
                    den.ap()[:, h0:h0 + nh], dap, 1e-6))(),
                    reads=[f"ps{yb}"],
                    writes=[f"oden{yi}"], tag=f"oden{ct}_{yi}{tg}")
                vop("dve", (lambda h0=h0, nh=nh: lambda h: h.reciprocal(
                    rden.ap()[:, h0:h0 + nh], den.ap()[:, h0:h0 + nh]))(),
                    reads=[f"oden{yi}"], writes=[f"orden{yi}"],
                    tag=f"orc{ct}_{yi}{tg}")
                nap = bass.AP(PS[yb], 0, [[512, 128], [66, nh], [1, 64]])
                vop("dve", (lambda nap=nap, h0=h0, nh=nh, tt=tt, yi=yi:
                            lambda h: h.tensor_tensor(
                    oy_sb.ap()[:, tt * 1024 + h0 * 64:
                               tt * 1024 + (h0 + nh) * 64]
                    .rearrange("p (hh d) -> p hh d", d=64),
                    nap,
                    rden.ap()[:, h0:h0 + nh].unsqueeze(2)
                    .broadcast_to([128, nh, 64]), ALU.mult))(),
                    reads=[f"orden{yi}", f"ps{yb}"],
                    writes=[f"oyb{tt}_{yi}"], tag=f"oyev{ct}_{yi}{tg}")
            if WAVES < 3:
                return
            for hd in range(H):
                g = hd // 8
                bank = "Q0" if g == 0 else "Q1"
                slot = (hd % 8) // 2
                hp = (hd % 2) * 64

                def mm(h, hd=hd, bank=bank, slot=slot, hp=hp, tt=tt):
                    return h.matmul(
                        PS[bank].ap()[hp:hp + 64, slot * 66:slot * 66 + 65],
                        oktok_sb.ap()[:, tt * 1024 + hd * 64:
                                      tt * 1024 + hd * 64 + 64],
                        ovaug_sb.ap()[:, tt * H * 68 + hd * 68:
                                      tt * H * 68 + hd * 68 + 65],
                        start=True, stop=True,
                        tile_position=(0, 64) if hp else None)
                vop("pe", mm,
                    reads=[f"oktok{tt}", f"ovau{tt}_{hd // 8}"],
                    writes=[f"ps{bank}"], tag=f"oSd{ct}h{hd}{tg}")
            for g in range(2):
                bank = "Q0" if g == 0 else "Q1"
                rd = [f"ps{bank}"]
                if ct == 0:
                    vop("dve", (lambda bank=bank, g=g: lambda h: h.tensor_copy(
                        oS.ap()[:, g * 264:(g + 1) * 264],
                        PS[bank].ap()[:, 0:264]))(),
                        reads=rd, writes=[f"oS{g}"], tag=f"oSi{ct}_{g}{tg}")
                else:
                    vop("dve", (lambda bank=bank, g=g: lambda h: h.tensor_add(
                        oS.ap()[:, g * 264:(g + 1) * 264],
                        oS.ap()[:, g * 264:(g + 1) * 264],
                        PS[bank].ap()[:, 0:264]))(),
                        reads=rd + [f"oS{g}"], writes=[f"oS{g}"],
                        tag=f"oSa{ct}_{g}{tg}")
                vop("dve", (lambda g=g: lambda h: h.tensor_copy(
                    oSbf.ap()[:, g * 272:(g + 1) * 272]
                    .rearrange("p (s d) -> p s d", s=4)[:, :, 0:66],
                    oS.ap()[:, g * 264:(g + 1) * 264]
                    .rearrange("p (s d) -> p s d", s=4)))(),
                    reads=[f"oS{g}"], writes=["oSbf"], tag=f"oSc{ct}_{g}{tg}")

        def wpT_load(oj):
            slot = oj % 2
            src_ap = bass.AP(DwpT.tensor, DwpT.offset + oj * 128,
                             [[1024, 128], [128 * 1024, CH], [1, 128]])
            vop("gpd", (lambda src_ap=src_ap, slot=slot: lambda h: h.dma_start(
                out=owpT_sb.ap()[:, slot * 1024:(slot + 1) * 1024]
                .rearrange("p (c j) -> p c j", c=CH), in_=src_ap))(),
                writes=[f"owpT{slot}"], tag=f"owpT{oj}{tg}")

        def proj_ctile(ct):
            sl, tt = ct // 4, ct % 4
            def tp(h, tt=tt):
                last = None
                for c in range(CH):
                    last = h.transpose(
                        PS["TB"].ap()[:, c * 128:(c + 1) * 128],
                        oy_sb.ap()[:, tt * 1024 + c * 128:
                                   tt * 1024 + (c + 1) * 128],
                        ident.ap())
                return last
            vop("pe", tp, reads=[f"oyb{tt}_{yi}" for yi in range(3)] +
                ["c_ident"], writes=["psTB"], tag=f"oytp{ct}{tg}")
            vop("dve", lambda h: h.tensor_copy(oyT_sb.ap(), PS["TB"].ap()),
                reads=["psTB"], writes=["oyT"], tag=f"oyte{ct}{tg}")
            for oj in range(CH):
                slot = oj % 2
                wpT_load(oj)
                bank = "S1" if oj % 2 == 0 else "O0"

                def mm(h, oj=oj, slot=slot, bank=bank):
                    last = None
                    for c in range(CH):
                        last = h.matmul(
                            PS[bank].ap()[:, 0:128],
                            owpT_sb.ap()[:, slot * 1024 + c * 128:
                                         slot * 1024 + (c + 1) * 128],
                            oyT_sb.ap()[:, c * 128:(c + 1) * 128],
                            start=(c == 0), stop=(c == CH - 1))
                    return last
                vop("pe", mm, reads=[f"owpT{slot}", "oyT"],
                    writes=[f"ps{bank}"], tag=f"oprj{ct}_{oj}{tg}")
                tcol = ct * 128
                ukeys = [f"u{oj}_{ct}"]
                vop("dve", (lambda oj=oj, bank=bank, tcol=tcol: lambda h:
                            h.tensor_add(
                    u.ap()[:, oj * TC + 1 + tcol: oj * TC + 1 + tcol + 128],
                    u.ap()[:, oj * TC + 1 + tcol: oj * TC + 1 + tcol + 128],
                    PS[bank].ap()[:, 0:128]))(),
                    reads=[f"ps{bank}"] + ukeys, writes=ukeys,
                    tag=f"ores{ct}_{oj}{tg}")

        phase("O")
        oneap = bass.AP(ovaug_sb, 64, [[4 * H * 68, 128], [68, 4 * H]])
        vop("dve", lambda h: h.memset(oneap, 1.0), writes=["ovau_ones"],
            tag=f"ovones{tg}")
        OSUB = int(os.environ.get("KOSUB", "4"))
        for sl in range(NSL_):
            emit_ln("lnx", sl, shifted=False, tag=f"_ol{b}_{sl}")
            for jg in range(4):
                wT_load(sl, jg)
                if OSUB < 1:
                    continue
                for jj in range(4):
                    qk_chunk(sl, jg, jj)
            if OSUB < 2:
                continue
            for g in range(2):
                wT_load(sl, 4 + g)
            for tt in range(4):
                v_tile(sl, tt)
                ktok_tile(sl, tt)
            if OSUB < 3:
                continue
            for tt in range(4):
                ct = sl * 4 + tt
                scan_ctile(ct)
                if OSUB >= 4:
                    proj_ctile(ct)

    def emit_final(b, add_spill=True):
        phase("FIN")
        for sl in range(NSL):
            t0 = sl * 512
            src = Dspill[b].rearrange("(c p) t -> p c t", p=128)[:, :, t0:t0 + 512]
            if add_spill:
                vop("spd", (lambda src=src: lambda h: h.dma_start(out=scr.ap(), in_=src))(),
                    reads=["spill"], writes=["scr"], tag=f"fin_ld{sl}")
            ukeys = [f"u{c}_{sl*4+q}" for c in range(CH) for q in range(4)]
            def add(h, t0=t0):
                last = None
                for c in range(CH):
                    last = h.tensor_add(
                        u.ap()[:, c * TC + 1 + t0: c * TC + 1 + t0 + 512],
                        u.ap()[:, c * TC + 1 + t0: c * TC + 1 + t0 + 512],
                        scr.ap()[:, c * 512:(c + 1) * 512])
                return last
            if add_spill:
                vop("dve", add, reads=["scr"] + ukeys, writes=ukeys, tag=f"fin_add{sl}")
            dst = Dout[b].rearrange("(c p) t -> p c t", p=128)[:, :, t0:t0 + 512]
            srcu = bass.AP(u, 1 + t0, [[CH * TC, 128], [TC, CH], [1, 512]])
            vop("spd", (lambda dst=dst, srcu=srcu: lambda h: h.dma_start(
                out=dst, in_=srcu))(),
                reads=ukeys, writes=["outdma"], tag=f"fin_st{sl}")

    def emit_spill(b):
        ukeys = [f"u{c}_{t}" for c in range(CH) for t in range(NT)]
        dst = Dspill[b].rearrange("(c p) t -> p c t", p=128)
        srcu = bass.AP(u, 1, [[CH * TC, 128], [TC, CH], [1, T]])
        vop("spd", lambda h: h.dma_start(out=dst, in_=srcu),
            reads=ukeys, writes=["spill"], tag="spill")

    load_consts()
    for b in range(npass):
        load_x(b)
        if stage == 2:
            for sl in range(NSL):
                emit_ln("u", sl, shifted=False, tag=f"_o{b}_{sl}")
        if stage == 30:
            emit_layer_consts(0)
            emit_ffn(0, b)
        if stage == 40:
            emit_layer_consts(0)
            emit_attn(0, b)
        if stage == 50:
            for l in range(nlay):
                emit_layer_consts(l)
                emit_attn(l, b)
                emit_ffn(l, b)
        if stage == 60:
            emit_outer(b)
        if stage >= 99:
            emit_outer(b)
            emit_spill(b)
            for sl in range(NSL):
                emit_ln("u", sl, shifted=False, tag=f"_pi{b}_{sl}")
            for l in range(nlay):
                emit_layer_consts(l)
                emit_attn(l, b)
                emit_ffn(l, b)
        emit_final(b, add_spill=(stage >= 99))

    globals()['LAST_S'] = S
    globals()['LAST_PS'] = PS
    vp.emit()
    return nc


_BUILD_CACHE = {}


def kernel(**inputs):
    key = (STAGE, NPASS, NLAY)
    if key not in _BUILD_CACHE:
        _BUILD_CACHE[key] = build(NPASS, NLAY, STAGE)
    nc = _BUILD_CACHE[key]
    hl = host_layout(inputs)
    hc = host_consts()
    in_maps = []
    for c in range(NCORE):
        m = dict(
            x_t=hl["x_t"][c * BPC:(c + 1) * BPC],
            w_attn_T=hl["w_attn_T"], w_proj_T=hl["w_proj_T"],
            wqkv=hl["wqkv"], wo=hl["wo"], w1=hl["w1"], w2=hl["w2"],
            qs=hl["qs"], ks=hl["ks"], gb=hl["gb"],
            b1a=hl["b1a"], b1g=hl["b1g"], b2f=hl["b2f"],
            tb_aq=hc["tb_aq"], tb_bq=hc["tb_bq"],
            tb_ak=hc["tb_ak"], tb_bk=hc["tb_bk"],
            m_up=hc["m_up"], m_lo=hc["m_lo"], maskT=hc["maskT"],
            ident=hc["ident"],
        )
        in_maps.append(m)
    res = run_bass_kernel_spmd(nc, in_maps, core_ids=list(range(NCORE)))
    outs = [np.asarray(r["out_t"], np.float32) for r in res.results]
    full = np.concatenate(outs, axis=0)
    return np.ascontiguousarray(full.transpose(0, 2, 1))
